# version 19
# baseline (speedup 1.0000x reference)
import math
import contextlib
import numpy as np
import ml_dtypes
import concourse.bass as bass
import concourse.mybir as mybir
from concourse.alu_op_type import AluOpType
from concourse.bass_utils import run_bass_kernel_spmd

F32 = mybir.dt.float32
BF16 = mybir.dt.bfloat16
FP8 = mybir.dt.float8e4
I32 = mybir.dt.int32
AF = mybir.ActivationFunctionType
AX = mybir.AxisListType
SUB = AluOpType.subtract
ADD = AluOpType.add
MUL = AluOpType.mult

D = 1024
HEADS = 8
DC = 32
DR = 32
DK = 128
DH = 4096
NHC = DH // 128
SOFT_SCALE = 1.0 / math.sqrt(DK + DR)
CSHIFT = 30.0
EPS = 1e-5
QEPS = 1e-5
RMAGIC = 12582912.0


STAGE = 99
DEBUG_LEDGER = False
DEBUG_TAP = False


def build_nc(S=4096, WIN=1024, TF=128):
    NW = S // WIN
    NKT = S // 128
    WTT = WIN // 128
    NTF = WIN // TF
    TFT = TF // 128
    NCH = min(WIN, 512)
    NNC = WIN // NCH
    assert NW >= 2 and TF % 128 == 0 and TF + 2 <= 512 and WIN % TF == 0 \
        and (WIN % 512 == 0 or WIN == 256)

    nc = bass.Bass("TRN2", target_bir_lowering=False)

    x0_h = nc.dram_tensor("x0", [S, D], F32, kind="ExternalInput").ap()
    x0T_h = nc.dram_tensor("x0T", [D, S], F32, kind="ExternalInput").ap()
    Wproj_h = nc.dram_tensor("Wproj", [D, 96], F32, kind="ExternalInput").ap()
    bkvr_h = nc.dram_tensor("bkvr", [64, 1], F32, kind="ExternalInput").ap()
    bq_h = nc.dram_tensor("bq", [32, 1], F32, kind="ExternalInput").ap()
    WUQT_h = nc.dram_tensor("WUQT", [HEADS * DK, DC], F32, kind="ExternalInput").ap()
    WUKT_h = nc.dram_tensor("WUKT", [HEADS * DK, DC], F32, kind="ExternalInput").ap()
    WUVT_h = nc.dram_tensor("WUVT", [HEADS * DK, DC], F32, kind="ExternalInput").ap()
    bUQ_h = nc.dram_tensor("bUQ", [HEADS * DK, 1], F32, kind="ExternalInput").ap()
    bUV_h = nc.dram_tensor("bUV", [HEADS * DK, 1], F32, kind="ExternalInput").ap()
    WQR_h = nc.dram_tensor("WQR", [DC, HEADS * DR], F32, kind="ExternalInput").ap()
    bQR_h = nc.dram_tensor("bQR", [1, HEADS * DR], F32, kind="ExternalInput").ap()
    WO_h = nc.dram_tensor("WO", [HEADS * DK, D], F32, kind="ExternalInput").ap()
    bO_h = nc.dram_tensor("bO", [1, D], F32, kind="ExternalInput").ap()
    w1T_h = nc.dram_tensor("w1T", [D, DH], F32, kind="ExternalInput").ap()
    w2T_h = nc.dram_tensor("w2T", [DH, D], F32, kind="ExternalInput").ap()
    cw_h = nc.dram_tensor("convw_pre", [128, 3 * NHC], F32, kind="ExternalInput").ap()
    cb_h = nc.dram_tensor("convb_pre", [128, NHC], F32, kind="ExternalInput").ap()
    out_h = nc.dram_tensor("out", [S, D], F32, kind="ExternalOutput").ap()
    w1q_d = nc.dram_tensor("w1q_scr", [128, NHC * 1024], FP8, kind="Internal").ap()
    w2q_d = nc.dram_tensor("w2q_scr", [128, NHC * 1024], FP8, kind="Internal").ap()
    dbg_d = (nc.dram_tensor("dbg_x1", [S, 1024], F32, kind="Internal").ap()
             if DEBUG_TAP else None)

    C = {k: 0 for k in ["si", "sm", "st", "sv", "ss", "so",
                        "wa", "wb", "w1a", "w1b", "w1c", "w1d",
                        "w2a", "w2b", "w2c", "w2d", "xr",
                        "sb", "qa", "qb", "sa", "wq"]}
    es = contextlib.ExitStack()
    cms = {}
    astack = []

    def mkalloc(name, shape, dt):
        cm = nc.sbuf_tensor(name, shape, dt)
        t = cm.__enter__()
        cms[name] = cm
        astack.append(name)
        return t

    def free(*names):
        want = set(names)
        for n in reversed(list(names)):
            pass
        while want:
            top = astack.pop()
            assert top in want, f"non-LIFO free: {top} vs {want}"
            want.remove(top)
            cms[top].__exit__(None, None, None)

    def free_rest():
        while astack:
            cms[astack.pop()].__exit__(None, None, None)

    with es:
        es.callback(free_rest)
        SEM = {k: es.enter_context(nc.semaphore(k)) for k in C}

        x1 = [mkalloc(f"x1_{i}", [128, WTT * D], F32) for i in range(2)]
        K1 = mkalloc("K1", [128, S], BF16)
        K2 = mkalloc("K2", [64, S], BF16)
        CST1 = mkalloc("CST1", [128, NKT * 65], BF16)
        CST2 = mkalloc("CST2", [128, NKT * 33], BF16)
        CQD = mkalloc("CQD", [97, S], BF16)
        CQL = mkalloc("CQL", [33, S], BF16)
        QBH = mkalloc("QBH", [97, HEADS * 64], BF16)
        QBL = mkalloc("QBL", [33, HEADS * 64], BF16)
        GH = mkalloc("GH", [128, 2 * D], BF16)
        GL = mkalloc("GL", [128, 2 * D], BF16)
        BE2 = mkalloc("BE2", [2, D], BF16)
        ONE2 = mkalloc("ONE2", [2, 128], BF16)
        Pb = [mkalloc(f"P_{i}", [128, WIN], F32) for i in range(2)]
        Pl = [mkalloc(f"Pl_{i}", [128, WIN], BF16) for i in range(2)]
        Q2 = [mkalloc(f"Q2_{i}", [128, WIN], BF16) for i in range(2)]
        QST = mkalloc("QST", [64, WIN], BF16)
        UallH = mkalloc("UallH", [128, 2 * WIN], BF16)
        UallL = mkalloc("UallL", [128, 2 * WIN], BF16)
        UEV = mkalloc("UEV", [65, WIN], F32)
        UMV = mkalloc("UMV", [33, WIN], F32)
        UST = mkalloc("UST", [32, WIN], F32)
        USH = mkalloc("USH", [32, WIN], BF16)
        USL = mkalloc("USL", [32, WIN], BF16)
        ROWF = mkalloc("ROWF", [1, WIN + 128], F32)
        ROWH = mkalloc("ROWH", [1, WIN + 128], BF16)
        ROWL = mkalloc("ROWL", [1, WIN + 128], BF16)
        x0r = [mkalloc(f"x0r_{i}", [128, D], F32) for i in range(1)]
        YT = mkalloc("YT", [128, D], F32)
        INVSB = [mkalloc(f"INVSB_{i}", [128, WIN + 2], F32) for i in range(2)]
        IVX = mkalloc("IVX", [128, 1], F32)
        XQC = WIN + 2
        xqW = [mkalloc(f"xqW_{i}", [128, 8 * XQC], BF16) for i in range(2)]
        hsil = mkalloc("hsil", [128, NHC * TF], BF16)
        cvA = mkalloc("cvA", [128, TF], F32)
        amx = mkalloc("amx", [128, 128], F32)
        s2b = mkalloc("s2b", [128, TF], F32)
        iv2c = mkalloc("iv2c", [128, TFT], F32)
        w1r = [mkalloc(f"w1r_{i}", [128, 1024], FP8) for i in range(4)]
        w2r = [mkalloc(f"w2r_{i}", [128, 1024], FP8) for i in range(4)]
        Jb = [mkalloc(f"J_{i}", [128, TF + 2], F32) for i in range(2)]
        idB = mkalloc("idB", [128, 128], BF16)
        idF = mkalloc("idF", [128, 128], F32)
        cneg = mkalloc("cneg", [128, 1], F32)
        epsc = mkalloc("epsc", [128, 1], F32)
        convb = mkalloc("convb", [128, NHC], F32)
        convw = mkalloc("convw", [128, 3 * NHC], F32)
        stat = mkalloc("stat", [128, 16], F32)
        bnb = mkalloc("bnb", [128, 12], F32)
        mvb = mkalloc("mvb", [128, 2], F32)
        xqi = mkalloc("xqi", [128, D], I32)
        xqb = mkalloc("xqb", [128, D], BF16)
        wiv = mkalloc("wiv", [128, 4], F32)
        rs32 = mkalloc("rs32", [33, 128], F32)
        fen = mkalloc("fen", [1, 8], F32)

        PS = [es.enter_context(nc.psum_tensor(f"ps{i}", [128, 1024], F32))
              for i in range(4)]

        blk = es.enter_context(nc.Block())

        _VOPS = {"tensor_copy", "tensor_tensor", "tensor_scalar",
                 "scalar_tensor_tensor", "tensor_reduce", "reciprocal", "memset",
                 "bn_stats", "bn_aggr", "activation"}

        QC = {"v": 0, "s": 0}
        QS = {"v": es.enter_context(nc.semaphore("qv")),
              "s": es.enter_context(nc.semaphore("qs"))}
        PTOT = {"v": {}, "s": {}}
        PEND = {"v": {}, "s": {}}

        def _drain_pend(eng, k):
            for sm_, vv in PEND[k].items():
                eng.wait_ge(sm_, vv)
            PEND[k].clear()

        class _Proxy:
            def __init__(self, eng, key):
                self._e = eng
                self._k = key

            def then_inc(self, sem, amt):
                k = self._k
                _drain_pend(self._e, k)
                self._e.wait_ge(QS[k], QC[k])
                r = self._e.nop().then_inc(sem, amt)
                PTOT[k][sem] = PTOT[k].get(sem, 0) + amt
                PEND[k][sem] = PTOT[k][sem]
                return r

        class _Serial:
            def __init__(self, eng, key):
                self._e = eng
                self._k = key

            def __getattr__(self, name):
                a = getattr(self._e, name)
                if name in _VOPS:
                    def call(*args, _a=a, **kw):
                        _drain_pend(self._e, self._k)
                        if QC[self._k] > 0:
                            self._e.wait_ge(QS[self._k], QC[self._k])
                        r = _a(*args, **kw)
                        r.then_inc(QS[self._k], 1)
                        QC[self._k] += 1
                        return _Proxy(self._e, self._k)
                    return call
                return a

        def vq(f):
            blk.vector(lambda v: f(_Serial(v, "v")))

        def sq(f):
            blk.scalar(lambda s2_: f(_Serial(s2_, "s")))

        yq, tq, gq = blk.sync, blk.tensor, blk.gpsimd


        def g0(g):
            g.memset(idF[:], 0.0).then_inc(SEM["so"], 1)
            g.memset(idB[:], 0.0).then_inc(SEM["so"], 1)
            g.wait_ge(SEM["so"], 2)
            g.affine_select(out=idF[:], in_=idF[:], compare_op=AluOpType.not_equal,
                            fill=1.0, base=0, pattern=[[-1, 128]],
                            channel_multiplier=1).then_inc(SEM["so"], 1)
            g.affine_select(out=idB[:], in_=idB[:], compare_op=AluOpType.not_equal,
                            fill=1.0, base=0, pattern=[[-1, 128]],
                            channel_multiplier=1).then_inc(SEM["so"], 1)
            g.memset(cneg[:], -CSHIFT).then_inc(SEM["so"], 1)
            g.memset(epsc[:], EPS).then_inc(SEM["so"], 1)
            g.memset(ONE2[:], 1.0).then_inc(SEM["so"], 1)
            g.memset(CST1[:], 0.0).then_inc(SEM["so"], 1)
            g.memset(CST2[:], 0.0).then_inc(SEM["so"], 1)
            g.memset(xqW[0][:], 0.0).then_inc(SEM["so"], 1)
            g.memset(xqW[1][:], 0.0).then_inc(SEM["so"], 1)
            g.memset(INVSB[0][:], 0.0).then_inc(SEM["so"], 1)
            g.memset(INVSB[1][:], 0.0).then_inc(SEM["so"], 1)
            g.memset(CQL[:, :], 0.0).then_inc(SEM["so"], 1)
            g.memset(stat[:], 0.0).then_inc(SEM["so"], 1)
            g.wait_ge(SEM["so"], 15)
            g.nop().then_inc(SEM["sm"], 1)
        gq(g0)
        C["so"] += 15
        C["sm"] += 1
        SM_G0 = C["sm"]

        def v0(v):
            v.memset(CQD[32:64, :], 0.0).then_inc(SEM["sv"], 1)
            v.memset(QBH[32:64, :], 0.0)
            v.wait_ge(SEM["sv"], C["sv"] + 1)
            v.memset(CQD[32:33, :], 1.0) \
                .then_inc(SEM["sv"], 1)
        vq(v0)
        C["sv"] += 2

        WP = mkalloc("WP", [128, 8 * 96], F32)
        WPH = mkalloc("WPH", [128, 8 * 96], BF16)
        WPL = mkalloc("WPL", [128, 8 * 96], BF16)
        BKV = mkalloc("BKV", [64, 1], F32)
        BQ = mkalloc("BQ", [32, 1], F32)
        MQW = mkalloc("MQW", [128, HEADS * 33], F32)
        MKW = mkalloc("MKW", [128, HEADS * 32], F32)
        MVW = mkalloc("MVW", [128, HEADS * 33], F32)
        WOC = YT
        WOHs = mkalloc("WOHs", [128, D], BF16)
        WOLs = mkalloc("WOLs", [128, D], BF16)
        QRW = mkalloc("QRW", [33, HEADS * DR], F32)

        def y1(y):
            n = 0
            for fc in range(8):
                y.dma_start(out=WP[:, fc * 96:fc * 96 + 96],
                            in_=Wproj_h[fc * 128:(fc + 1) * 128, :]).then_inc(SEM["si"], 16)
                n += 1
            y.dma_start(out=BKV[:, :], in_=bkvr_h[:, :]).then_inc(SEM["si"], 16)
            y.dma_start(out=BQ[:, :], in_=bq_h[:, :]).then_inc(SEM["si"], 16)
            n += 2
            for h in range(HEADS):
                y.dma_start(out=MQW[:, h * 33:h * 33 + 32],
                            in_=WUQT_h[h * DK:(h + 1) * DK, :]).then_inc(SEM["si"], 16)
                y.dma_start(out=MQW[:, h * 33 + 32:h * 33 + 33],
                            in_=bUQ_h[h * DK:(h + 1) * DK, :]).then_inc(SEM["si"], 16)
                y.dma_start(out=MKW[:, h * 32:(h + 1) * 32],
                            in_=WUKT_h[h * DK:(h + 1) * DK, :]).then_inc(SEM["si"], 16)
                y.dma_start(out=MVW[:, h * 33:h * 33 + 32],
                            in_=WUVT_h[h * DK:(h + 1) * DK, :]).then_inc(SEM["si"], 16)
                y.dma_start(out=MVW[:, h * 33 + 32:h * 33 + 33],
                            in_=bUV_h[h * DK:(h + 1) * DK, :]).then_inc(SEM["si"], 16)
                n += 5
            y.dma_start(out=QRW[0:32, :], in_=WQR_h[:, :]).then_inc(SEM["si"], 16)
            y.dma_start(out=QRW[32:33, :], in_=bQR_h[:, :]).then_inc(SEM["si"], 16)
            n += 2
            y.dma_start(out=convb[:, :], in_=cb_h[:, :]).then_inc(SEM["si"], 16)
            y.dma_start(out=convw[:, :], in_=cw_h[:, :]).then_inc(SEM["si"], 16)
            n += 2
            C["si"] += 16 * n
        yq(y1)
        SI_SMALL = C["si"]

        def w_unit_src(widx, hc):
            if widx == 0:
                return w1T_h[:, hc * 128:(hc + 1) * 128] \
                    .rearrange("(fc p) c -> p fc c", p=128)
            return w2T_h[hc * 128:(hc + 1) * 128, :]

        WQB = {}

        def emit_wq_alloc():
            WQB["WFa"] = mkalloc("WFa", [128, 1024], F32)
            WQB["WFb"] = mkalloc("WFb", [128, 1024], F32)
            WQB["WQ8"] = mkalloc("WQ8", [128, 1024], FP8)
            WQB["WAB"] = mkalloc("WAB", [128, 2 * NHC], F32)
            WQB["sv0"] = C["sv"]
            WQB["sm0"] = C["sm"]
            WQB["si0"] = C["si"]

        def emit_wq_scan_part(part, nparts):
            WFa, WFb, WAB = WQB["WFa"], WQB["WFb"], WQB["WAB"]
            per = -(-(2 * NHC) // nparts)
            for idx in range(part * per, min((part + 1) * per, 2 * NHC)):
                widx, hc = divmod(idx, NHC)
                buf = WFa if hc % 2 == 0 else WFb
                gate = max(0, C["sv"] - 1)

                wsem = "wa" if hc % 2 == 0 else "wb"

                def yA(y, buf=buf, widx=widx, hc=hc, gate=gate, idx=idx,
                       wsem=wsem, g0=dict(WQB)):
                    if idx >= 2:
                        y.wait_ge(SEM["sv"], gate)
                    else:
                        y.wait_ge(SEM["sv"], g0["sv0"])
                        y.wait_ge(SEM["sm"], g0["sm0"])
                        y.wait_ge(SEM["si"], g0["si0"])
                    y.dma_start(out=buf[:, :], in_=w_unit_src(widx, hc)) \
                        .then_inc(SEM[wsem], 16)
                yq(yA)
                C[wsem] += 16
                si_here = C[wsem]

                def vAb(v, buf=buf, col=WAB[:, idx:idx + 1], si_here=si_here,
                        wsem=wsem):
                    v.wait_ge(SEM[wsem], si_here)
                    v.tensor_reduce(col, buf[:, :], axis=AX.X, op=ADD,
                                    apply_absolute_value=True) \
                        .then_inc(SEM["sv"], 1)
                vq(vAb)
                C["sv"] += 1
            WQB["SV_ABS"] = C["sv"]

        def emit_wq_tail():
            WFa, WFb = WQB["WFa"], WQB["WFb"]
            WQ8, WAB = WQB["WQ8"], WQB["WAB"]
            WI = xqi
            SV_ABS = WQB["SV_ABS"]

            def vA(v):
                v.wait_ge(SEM["sv"], SV_ABS)
                for widx in range(2):
                    v.tensor_reduce(stat[:, widx:widx + 1],
                                    WAB[:, widx * NHC:(widx + 1) * NHC],
                                    axis=AX.X, op=ADD)
                v.tensor_copy(stat[:, 15:16], stat[:, 15:16]).then_inc(SEM["sv"], 1)
            vq(vA)
            C["sv"] += 1
            SV_A = C["sv"]

            def tA(t, SV_A=SV_A):
                t.wait_ge(SEM["sv"], SV_A)
                t.matmul(PS[3][0:1, 0:128], stat[:, 0:1], idF[:], start=True,
                         stop=True, is_transpose=True)
                t.matmul(PS[3][0:1, 128:256], stat[:, 1:2], idF[:], start=True,
                         stop=True, is_transpose=True).then_inc(SEM["st"], 1)
            tq(tA)
            C["st"] += 1
            ST_A = C["st"]

            def vB(v, ST_A=ST_A):
                v.wait_ge(SEM["st"], ST_A)
                for widx in range(2):
                    v.tensor_reduce(stat[0:1, 4 + widx:5 + widx],
                                    PS[3][0:1, widx * 128:widx * 128 + 128],
                                    axis=AX.X, op=ADD)
                    v.tensor_scalar(stat[0:1, 4 + widx:5 + widx],
                                    stat[0:1, 4 + widx:5 + widx],
                                    1.0 / (DH * D), QEPS, op0=MUL,
                                    op1=AluOpType.max)
                    v.reciprocal(stat[0:1, 6 + widx:7 + widx],
                                 stat[0:1, 4 + widx:5 + widx])
                v.tensor_copy(stat[0:1, 8:10], stat[0:1, 6:8])
                v.tensor_copy(stat[0:1, 10:12], stat[0:1, 4:6])
                v.tensor_copy(ROWH[0:1, 0:4], stat[0:1, 8:12])
                v.tensor_tensor(ROWL[0:1, 0:4], stat[0:1, 8:12], ROWH[0:1, 0:4],
                                op=SUB) \
                    .then_inc(SEM["sv"], 1)
            vq(vB)
            C["sv"] += 1
            SV_B = C["sv"]

            def tB(t, SV_B=SV_B):
                t.wait_ge(SEM["sv"], SV_B)
                t.matmul(PS[3][:, 0:4], ONE2[0:1, 0:128], ROWH[0:1, 0:4],
                         start=True, stop=False)
                t.matmul(PS[3][:, 0:4], ONE2[0:1, 0:128], ROWL[0:1, 0:4],
                         start=False, stop=True).then_inc(SEM["st"], 1)
            tq(tB)
            C["st"] += 1
            ST_B = C["st"]

            def vC(v, ST_B=ST_B):
                v.wait_ge(SEM["st"], ST_B)
                v.tensor_copy(wiv[:, 0:4], PS[3][:, 0:4]).then_inc(SEM["sv"], 1)
            vq(vC)
            C["sv"] += 1
            FST["sv_wq_c"] = C["sv"]

            for widx in range(2):
                for hc in range(NHC):
                    buf = WFa if hc % 2 == 0 else WFb
                    idx = widx * NHC + hc
                    gate = C["sv"] - 1 if idx >= 2 else C["sv"]

                    wsem = "wa" if hc % 2 == 0 else "wb"

                    def yBq(y, buf=buf, widx=widx, hc=hc, gate=gate, wsem=wsem):
                        y.wait_ge(SEM["sv"], gate)
                        y.dma_start(out=buf[:, :], in_=w_unit_src(widx, hc)) \
                            .then_inc(SEM[wsem], 16)
                    yq(yBq)
                    C[wsem] += 16
                    si_here = C[wsem]
                    wq_gate = C["wq"]

                    def vQ(v, buf=buf, widx=widx, si_here=si_here,
                           wq_gate=wq_gate, wsem=wsem):
                        v.wait_ge(SEM[wsem], si_here)
                        v.wait_ge(SEM["wq"], wq_gate)
                        v.tensor_scalar(WI[:, :], buf[:, :],
                                        wiv[:, widx:widx + 1], None, op0=MUL)
                        v.tensor_scalar(WI[:, :], WI[:, :], 1, -1,
                                        op0=AluOpType.min, op1=AluOpType.max)
                        v.tensor_copy(WQ8[:, :], WI[:, :]).then_inc(SEM["sv"], 1)
                    vq(vQ)
                    C["sv"] += 1
                    sv_here = C["sv"]

                    def yS(y, widx=widx, hc=hc, sv_here=sv_here):
                        y.wait_ge(SEM["sv"], sv_here)
                        dst = (w1q_d if widx == 0
                               else w2q_d)[:, hc * 1024:(hc + 1) * 1024]
                        y.dma_start(out=dst, in_=WQ8[:, :]).then_inc(SEM["wq"], 16)
                    yq(yS)
                    C["wq"] += 16
            FST["so_wq"] = C["wq"]
            free("WFa", "WFb", "WQ8", "WAB")

        X0P = [mkalloc(f"X0P_{i}", [128, NCH], F32) for i in range(2)]
        X0PH = [mkalloc(f"X0PH_{i}", [128, NCH], BF16) for i in range(2)]
        X0PL = [mkalloc(f"X0PL_{i}", [128, NCH], BF16) for i in range(2)]
        NPC = S // NCH
        assert NPC % 4 == 0 or NPC <= 4

        def vWp(v):
            v.wait_ge(SEM["si"], SI_SMALL)
            v.tensor_copy(WPH[:, :], WP[:, :])
            v.tensor_tensor(WPL[:, :], WP[:, :], WPH[:, :], op=SUB) \
                .then_inc(SEM["sv"], 1)
        vq(vWp)
        C["sv"] += 1

        NRND = max(1, NPC // 4)
        RTC = min(4, NPC)
        sv_split = {}
        st_mm = {}
        for rnd in range(NRND):
            for tci in range(RTC):
                tc = rnd * RTC + tci
                for fc in range(8):
                    k = tc * 8 + fc
                    slot = k % 2
                    if tci == 0 and fc == 0:
                        waits = [("sv", sv_split.get(rnd - 1, C["sv"]))]
                    elif (k - 2) in st_mm:
                        waits = [("st", st_mm[k - 2])]
                    else:
                        waits = []

                    wsem = "wa" if slot == 0 else "wb"

                    def yPp(y, tc=tc, fc=fc, slot=slot, waits=tuple(waits), wsem=wsem):
                        for sem, val in waits:
                            y.wait_ge(SEM[sem], val)
                        y.dma_start(out=X0P[slot][:, :],
                                    in_=x0T_h[fc * 128:(fc + 1) * 128,
                                              tc * NCH:(tc + 1) * NCH]) \
                            .then_inc(SEM[wsem], 16)
                    yq(yPp)
                    C[wsem] += 16
                    si_p = C[wsem]

                    def vPp(v, slot=slot, si_p=si_p, waits=tuple(waits), wsem=wsem):
                        v.wait_ge(SEM[wsem], si_p)
                        for sem, val in waits:
                            v.wait_ge(SEM[sem], val)
                        v.tensor_copy(X0PH[slot][:, :], X0P[slot][:, :])
                        v.tensor_tensor(X0PL[slot][:, :], X0P[slot][:, :],
                                        X0PH[slot][:, :], op=SUB) \
                            .then_inc(SEM["sv"], 1)
                    vq(vPp)
                    C["sv"] += 1
                    sv_p = C["sv"]

                    def tPp(t, tci=tci, fc=fc, slot=slot, sv_p=sv_p):
                        t.wait_ge(SEM["sv"], sv_p)
                        if fc == 0 and tci == 0:
                            t.wait_ge(SEM["sm"], SM_G0)
                        lo = fc * 96
                        a1 = PS[tci // 2][0:64, (tci % 2) * 512:(tci % 2) * 512 + NCH]
                        a2 = PS[2 + tci // 2][0:32, (tci % 2) * 512:(tci % 2) * 512 + NCH]
                        st0 = (fc == 0)
                        sp = (fc == 7)
                        t.matmul(a1, WPH[:, lo:lo + 64], X0PH[slot][:, :],
                                 start=st0, stop=False)
                        t.matmul(a1, WPL[:, lo:lo + 64], X0PH[slot][:, :],
                                 start=False, stop=False)
                        mm1 = t.matmul(a1, WPH[:, lo:lo + 64], X0PL[slot][:, :],
                                       start=False, stop=sp)
                        t.matmul(a2, WPH[:, lo + 64:lo + 96], X0PH[slot][:, :],
                                 start=st0, stop=False)
                        t.matmul(a2, WPL[:, lo + 64:lo + 96], X0PH[slot][:, :],
                                 start=False, stop=False)
                        mm2 = t.matmul(a2, WPH[:, lo + 64:lo + 96], X0PL[slot][:, :],
                                       start=False, stop=sp)
                        mm2.then_inc(SEM["st"], 1)
                    tq(tPp)
                    C["st"] += 1
                    st_mm[k] = C["st"]
            st_rnd = C["st"]

            def vEvR(v, rnd=rnd, st_rnd=st_rnd):
                v.wait_ge(SEM["st"], st_rnd)
                for tci in range(RTC):
                    tc = rnd * RTC + tci
                    cols = slice(tc * NCH, (tc + 1) * NCH)
                    a1 = PS[tci // 2][0:64, (tci % 2) * 512:(tci % 2) * 512 + NCH]
                    a2 = PS[2 + tci // 2][0:32, (tci % 2) * 512:(tci % 2) * 512 + NCH]
                    v.tensor_scalar(K1[0:64, cols], a1, BKV[:, 0:1], None, op0=ADD)
                    v.scalar_tensor_tensor(K2[0:64, cols], a1, BKV[:, 0:1],
                                           K1[0:64, cols], op0=ADD, op1=SUB)
                    v.tensor_scalar(CQD[0:32, cols], a2, BQ[:, 0:1], None, op0=ADD)
                    v.scalar_tensor_tensor(CQL[0:32, cols], a2, BQ[:, 0:1],
                                           CQD[0:32, cols], op0=ADD, op1=SUB)
                v.tensor_copy(stat[:, 15:16], stat[:, 15:16]).then_inc(SEM["sv"], 1)
            vq(vEvR)
            C["sv"] += 1
            sv_split[rnd] = C["sv"]
        SV_STK = C["sv"]

        def yDup(y):
            y.wait_ge(SEM["sv"], SV_STK)
            y.dma_start(out=K1[64:128, :], in_=K1[0:64, :]).then_inc(SEM["sm"], 16)
            y.dma_start(out=CQD[64:97, :], in_=CQD[0:33, :]).then_inc(SEM["sm"], 16)
        yq(yDup)
        C["sm"] += 32
        SM_DUP = C["sm"]

        st_cst0 = C["st"]

        def tCst(t):
            pb = PS[3][:, 0:64].bitcast(BF16)
            for kt in range(NKT):
                t.wait_ge(SEM["sv"], SV_STK + kt)
                t.matmul(pb[:, 0:32], K1[0:32, kt * 128:(kt + 1) * 128],
                         idB[0:32, 0:32], start=True, stop=True, is_transpose=True)
                t.matmul(pb[:, 32:64], K2[0:32, kt * 128:(kt + 1) * 128],
                         idB[0:32, 0:32], start=True, stop=True, is_transpose=True) \
                    .then_inc(SEM["st"], 1)
        tq(tCst)
        C["st"] += NKT

        def vCst(v):
            pb = PS[3][:, 0:64].bitcast(BF16)
            for kt in range(NKT):
                v.wait_ge(SEM["st"], st_cst0 + kt + 1)
                v.tensor_copy(CST1[:, kt * 65 + 0:kt * 65 + 32], pb[:, 0:32])
                v.tensor_copy(CST1[:, kt * 65 + 32:kt * 65 + 64], pb[:, 32:64])
                v.tensor_copy(CST2[:, kt * 33 + 0:kt * 33 + 32], pb[:, 0:32]) \
                    .then_inc(SEM["sv"], 1)
            v.memset(CST1[:].rearrange("p (kt c) -> p kt c", c=65)[:, :, 64:65], 1.0)
            v.memset(CST2[:].rearrange("p (kt c) -> p kt c", c=33)[:, :, 32:33], 1.0) \
                .then_inc(SEM["sv"], 1)
        vq(vCst)
        C["sv"] += NKT + 1
        free("X0PL_1", "X0PL_0", "X0PH_1", "X0PH_0", "X0P_1", "X0P_0")

        MQH = mkalloc("MQH", [128, HEADS * 33], BF16)
        MQL = mkalloc("MQL", [128, HEADS * 33], BF16)
        MKH = mkalloc("MKH", [128, HEADS * 32], BF16)
        MKL = mkalloc("MKL", [128, HEADS * 32], BF16)
        MVH = mkalloc("MVH", [128, HEADS * 33], BF16)
        MVL = mkalloc("MVL", [128, HEADS * 33], BF16)
        QBF = mkalloc("QBF", [33, HEADS * 64], F32)
        GSF = mkalloc("GSF", [33, D], F32)
        GSH = mkalloc("GSH", [33, D], BF16)
        GSL = mkalloc("GSL", [33, D], BF16)
        BEF = mkalloc("BEF", [33, D], F32)

        def vMq(v):
            v.wait_ge(SEM["si"], SI_SMALL)
            v.tensor_copy(MQH[:, :], MQW[:, :])
            v.tensor_tensor(MQL[:, :], MQW[:, :], MQH[:, :], op=SUB)
            v.tensor_copy(MKH[:, :], MKW[:, :])
            v.tensor_tensor(MKL[:, :], MKW[:, :], MKH[:, :], op=SUB)
            v.tensor_copy(MVH[:, :], MVW[:, :])
            v.tensor_tensor(MVL[:, :], MVW[:, :], MVH[:, :], op=SUB) \
                .then_inc(SEM["sv"], 1)
        vq(vMq)
        C["sv"] += 1
        SV_MQ = C["sv"]

        st_mq0 = C["st"]

        def tMq(t):
            for h in range(HEADS):
                t.wait_ge(SEM["sv"], SV_MQ + h)
                a = PS[2][0:33, 0:32]
                t.matmul(a, MQH[:, h * 33:(h + 1) * 33], MKH[:, h * 32:(h + 1) * 32],
                         start=True, stop=False)
                t.matmul(a, MQL[:, h * 33:(h + 1) * 33], MKH[:, h * 32:(h + 1) * 32],
                         start=False, stop=False)
                t.matmul(a, MQH[:, h * 33:(h + 1) * 33], MKL[:, h * 32:(h + 1) * 32],
                         start=False, stop=True).then_inc(SEM["st"], 1)
        tq(tMq)
        C["st"] += HEADS

        def vMq2(v):
            for h in range(HEADS):
                v.wait_ge(SEM["st"], st_mq0 + h + 1)
                v.tensor_copy(QBF[:, h * 64:h * 64 + 32], PS[2][0:33, 0:32]) \
                    .then_inc(SEM["sv"], 1)
            for h in range(HEADS):
                v.tensor_copy(QBF[:, h * 64 + 32:h * 64 + 64],
                              QRW[:, h * DR:(h + 1) * DR])
            v.tensor_copy(QBH[0:33, :], QBF[:, :])
            v.tensor_tensor(QBL[0:33, :], QBF[:, :], QBH[0:33, :], op=SUB) \
                .then_inc(SEM["sv"], 1)
        vq(vMq2)
        C["sv"] += HEADS + 1
        SV_QB = C["sv"]

        def yQb(y):
            y.wait_ge(SEM["sv"], SV_QB)
            y.dma_start(out=QBH[64:97, :], in_=QBL[0:33, :]).then_inc(SEM["sm"], 16)
        yq(yQb)
        C["sm"] += 16

        for h in range(HEADS):
            gate = C["sm"]

            def yG(y, h=h, gate=gate):
                y.wait_ge(SEM["sm"], gate)
                y.dma_start(out=WOC[:, :], in_=WO_h[h * DK:(h + 1) * DK, :]) \
                    .then_inc(SEM["si"], 16)
            yq(yG)
            C["si"] += 16
            si_g = C["si"]

            def vGs(v, si_g=si_g, h=h):
                v.wait_ge(SEM["si"], si_g)
                v.tensor_copy(WOHs[:, :], WOC[:, :])
                v.tensor_tensor(WOLs[:, :], WOC[:, :], WOHs[:, :], op=SUB) \
                    .then_inc(SEM["sv"], 1)
            vq(vGs)
            C["sv"] += 1
            sv_g = C["sv"]
            st_g0 = C["st"]

            def tG(t, h=h, sv_g=sv_g):
                t.wait_ge(SEM["sv"], sv_g)
                for nck in range(2):
                    a = PS[2][0:33, nck * 512:(nck + 1) * 512]
                    t.matmul(a, MVH[:, h * 33:(h + 1) * 33],
                             WOHs[:, nck * 512:(nck + 1) * 512], start=True, stop=False)
                    t.matmul(a, MVL[:, h * 33:(h + 1) * 33],
                             WOHs[:, nck * 512:(nck + 1) * 512], start=False, stop=False)
                    t.matmul(a, MVH[:, h * 33:(h + 1) * 33],
                             WOLs[:, nck * 512:(nck + 1) * 512], start=False,
                             stop=True).then_inc(SEM["st"], 1)
            tq(tG)
            C["st"] += 2
            st_g1 = C["st"]

            def vGe(v, h=h, st_g1=st_g1, sm_gm=C["sm"]):
                v.wait_ge(SEM["st"], st_g1)
                v.wait_ge(SEM["sm"], sm_gm)
                v.tensor_copy(GSF[:, 0:512], PS[2][0:33, 0:512])
                v.tensor_copy(GSF[:, 512:1024], PS[2][0:33, 512:1024])
                if h == 0:
                    v.tensor_copy(BEF[32:33, :], GSF[32:33, :])
                else:
                    v.tensor_tensor(BEF[32:33, :], BEF[32:33, :], GSF[32:33, :],
                                    op=ADD)
                v.tensor_copy(GSH[0:32, :], GSF[0:32, :])
                v.tensor_tensor(GSL[0:32, :].bitcast(F32)[0:32, 0:512]
                                if False else GSL[0:32, :], GSF[0:32, :],
                                GSH[0:32, :], op=SUB) \
                    .then_inc(SEM["sv"], 1)
            vq(vGe)
            C["sv"] += 1
            sv_ge = C["sv"]

            def yGm(y, h=h, sv_ge=sv_ge):
                y.wait_ge(SEM["sv"], sv_ge)
                kc, j = h // 4, h % 4
                y.dma_start(out=GH[32 * j:32 * j + 32, kc * D:(kc + 1) * D],
                            in_=GSH[0:32, :]).then_inc(SEM["sm"], 16)
                y.dma_start(out=GL[32 * j:32 * j + 32, kc * D:(kc + 1) * D],
                            in_=GSL[0:32, :]).then_inc(SEM["sm"], 16)
            yq(yGm)
            C["sm"] += 32

        sv_all_ge = C["sv"]

        def yBo(y, sv_all_ge=sv_all_ge):
            y.wait_ge(SEM["sv"], sv_all_ge)
            y.dma_start(out=GSF[32:33, :], in_=bO_h[:, :]).then_inc(SEM["si"], 16)
        yq(yBo)
        C["si"] += 16
        SI_BO = C["si"]

        def vBe(v, sm_all=C["sm"]):
            v.wait_ge(SEM["si"], SI_BO)
            v.wait_ge(SEM["sm"], sm_all)
            v.tensor_tensor(BEF[32:33, :], BEF[32:33, :], GSF[32:33, :], op=ADD) \
                .then_inc(SEM["sv"], 1)
        vq(vBe)
        C["sv"] += 1
        SV_BEA = C["sv"]

        def yBm(y, SV_BEA=SV_BEA):
            y.wait_ge(SEM["sv"], SV_BEA)
            y.dma_start(out=BEF[0:1, :], in_=BEF[32:33, :]).then_inc(SEM["sm"], 16)
        yq(yBm)
        C["sm"] += 16
        SM_BEM = C["sm"]

        def vBe2(v, SM_BEM=SM_BEM):
            v.wait_ge(SEM["sm"], SM_BEM)
            v.tensor_copy(BE2[0:1, :], BEF[0:1, :])
            v.tensor_tensor(GSF[0:1, :], BEF[0:1, :], BE2[0:1, :], op=SUB)
            v.tensor_copy(GSH[0:1, :], GSF[0:1, :]) \
                .then_inc(SEM["sv"], 1)
        vq(vBe2)
        C["sv"] += 1
        SV_BE = C["sv"]

        def yBe(y):
            y.wait_ge(SEM["sv"], SV_BE)
            y.dma_start(out=BE2[1:2, :], in_=GSH[0:1, :]).then_inc(SEM["sm"], 16)
        yq(yBe)
        C["sm"] += 16
        free("MQW", "MKW", "MVW", "MQH", "MQL", "MKH", "MKL", "MVH", "MVL",
             "QBF", "GSF", "GSH", "GSL", "BEF", "WOHs", "WOLs", "QRW",
             "WP", "WPH", "WPL", "BQ", "BKV")

        emit_wq_alloc()

        def bcast_row(n, dst_evac, extra_scale_col=None, sa_gate=None):
            gate = C["sv"]

            def v_(v, n=n, gate=gate, sa_gate=sa_gate):
                v.wait_ge(SEM["sv"], gate)
                if sa_gate is not None:
                    v.wait_ge(SEM["sa"], sa_gate)
                if extra_scale_col is not None:
                    v.tensor_scalar(ROWF[0:1, 0:n], ROWF[0:1, 0:n],
                                    extra_scale_col, None, op0=MUL)
                v.tensor_copy(ROWH[0:1, 0:n], ROWF[0:1, 0:n])
                v.tensor_tensor(ROWL[0:1, 0:n], ROWF[0:1, 0:n], ROWH[0:1, 0:n], op=SUB) \
                    .then_inc(SEM["sv"], 1)
            vq(v_)
            C["sv"] += 1
            svh = C["sv"]
            nch = (n + 511) // 512
            st0 = C["st"]
            sv0 = C["sv"]

            def t_(t, n=n, svh=svh, nch=nch, sv0=sv0):
                t.wait_ge(SEM["sv"], svh)
                for ci in range(nch):
                    if ci >= 2:
                        t.wait_ge(SEM["sv"], sv0 + ci - 1)
                    ln = min(512, n - ci * 512)
                    half = PS[2][:, (ci % 2) * 512:(ci % 2) * 512 + 512]
                    t.matmul(half[:, 0:ln], ONE2[0:1, 0:128],
                             ROWH[0:1, ci * 512:ci * 512 + ln],
                             start=True, stop=False)
                    t.matmul(half[:, 0:ln], ONE2[0:1, 0:128],
                             ROWL[0:1, ci * 512:ci * 512 + ln],
                             start=False, stop=True).then_inc(SEM["st"], 1)
            tq(t_)
            C["st"] += nch

            def v2_(v, n=n, st0=st0, nch=nch):
                for ci in range(nch):
                    v.wait_ge(SEM["st"], st0 + ci + 1)
                    ln = min(512, n - ci * 512)
                    half = PS[2][:, (ci % 2) * 512:(ci % 2) * 512 + 512]
                    dst_evac(v, ci, half[:, 0:ln])
                    v.tensor_copy(stat[:, 15:16], stat[:, 15:16]) \
                        .then_inc(SEM["sv"], 1)
            vq(v2_)
            C["sv"] += nch

        def qbuild(h, slot, w, gate_sv=None):
            q0 = w * WIN
            if gate_sv is None:
                gate_sv = C["sv"]
            gate_sm = C["sm"]

            def t_(t, h=h, slot=slot, q0=q0, gate_sv=gate_sv, gate_sm=gate_sm):
                t.wait_ge(SEM["sv"], gate_sv)
                t.wait_ge(SEM["sm"], gate_sm)
                for ncx in range(NNC):
                    a = PS[3][0:64, ncx * NCH:(ncx + 1) * NCH]
                    cols = slice(q0 + ncx * NCH, q0 + (ncx + 1) * NCH)
                    t.matmul(a, QBH[:, h * 64:(h + 1) * 64], CQD[:, cols],
                             start=True, stop=False)
                    t.matmul(a, QBL[:, h * 64:(h + 1) * 64], CQL[:, cols],
                             start=False, stop=True).then_inc(SEM["st"], 1)
            tq(t_)
            C["st"] += NNC
            st_q = C["st"]

            def v_(v, slot=slot, st_q=st_q):
                v.wait_ge(SEM["st"], st_q)
                v.tensor_copy(Q2[slot][0:64, :], PS[3][0:64, 0:WIN])
                v.tensor_tensor(QST[0:64, :], PS[3][0:64, 0:WIN], Q2[slot][0:64, :],
                                op=SUB) \
                    .then_inc(SEM["sv"], 1)
            vq(v_)
            C["sv"] += 1
            sv_q = C["sv"]

            qk = "qa" if slot == 0 else "qb"

            def y_(y, slot=slot, sv_q=sv_q, qk=qk):
                y.wait_ge(SEM["sv"], sv_q)
                y.dma_start(out=Q2[slot][64:128, :], in_=QST[0:64, :]) \
                    .then_inc(SEM[qk], 16)
            yq(y_)
            C[qk] += 16
            return (qk, C[qk])

        def attn_window(w):
            q0 = w * WIN
            sm_q = qbuild(0, 0, w,
                          gate_sv=(FST["sv_vy_last"] if w >= 1 else None))
            sv_y_hist = []

            for h in range(HEADS):
                slot = h % 2
                plan = []
                for kt in range(NKT):
                    plan.append(("sc", kt))
                    if kt >= 1:
                        plan.append(("av", kt - 1))
                plan.append(("av", NKT - 1))
                st_base = C["st"]
                st_sc, st_av = {}, {}
                n = 0
                for op, kt in plan:
                    n += 1
                    (st_sc if op == "sc" else st_av)[kt] = st_base + n
                ss_base = C["ss"]
                ss_exp = {kt: ss_base + kt + 1 for kt in range(NKT)}
                sv_base = C["sv"]
                sv_sub = {kt: sv_base + kt + 1 for kt in range(NKT)}

                def t_loop(t, slot=slot, plan=tuple(plan), ss_exp=dict(ss_exp),
                           sv_sub=dict(sv_sub), sm_q=sm_q):
                    t.wait_ge(SEM[sm_q[0]], sm_q[1])
                    for op, kt in plan:
                        if op == "sc":
                            if kt >= 2:
                                t.wait_ge(SEM["ss"], ss_exp[kt - 2])
                            pss = PS[2 + (kt % 2)]
                            for ncx in range(NNC):
                                pc = slice(512 + ncx * NCH, 512 + (ncx + 1) * NCH)
                                t.matmul(pss[:, pc], K1[:, kt * 128:(kt + 1) * 128],
                                         Q2[slot][:, ncx * NCH:(ncx + 1) * NCH],
                                         start=True, stop=False)
                                mm = t.matmul(pss[:, pc],
                                              K2[:, kt * 128:(kt + 1) * 128],
                                              Q2[slot][0:64,
                                                       ncx * NCH:(ncx + 1) * NCH],
                                              start=False, stop=True)
                                if ncx == NNC - 1:
                                    mm.then_inc(SEM["st"], 1)
                        else:
                            t.wait_ge(SEM["sv"], sv_sub[kt])
                            phv = Pb[kt % 2][:].bitcast(BF16) \
                                .rearrange("p (n two) -> p n two", two=2)
                            for ncx in range(NNC):
                                pc = slice(ncx * NCH, (ncx + 1) * NCH)
                                t.matmul(PS[2][0:65, pc],
                                         CST1[:, kt * 65:(kt + 1) * 65],
                                         phv[:, pc, 1],
                                         start=(kt == 0), stop=(kt == NKT - 1))
                                mm = t.matmul(PS[3][0:33, pc],
                                              CST2[:, kt * 33:(kt + 1) * 33],
                                              Pl[kt % 2][:, pc],
                                              start=(kt == 0), stop=(kt == NKT - 1))
                                if ncx == NNC - 1:
                                    mm.then_inc(SEM["st"], 1)
                tq(t_loop)
                C["st"] += len(plan)

                def s_loop(s, st_sc=dict(st_sc), st_av=dict(st_av)):
                    for kt in range(NKT):
                        s.wait_ge(SEM["st"], st_sc[kt])
                        if kt >= 2:
                            s.wait_ge(SEM["st"], st_av[kt - 2])
                        s.activation(out=Pb[kt % 2][:, :],
                                     in_=PS[2 + (kt % 2)][:, 512:512 + WIN],
                                     func=AF.Exp, bias=cneg[:, 0:1],
                                     scale=SOFT_SCALE).then_inc(SEM["ss"], 1)
                sq(s_loop)
                C["ss"] += NKT

                def v_loop(v, ss_exp=dict(ss_exp)):
                    for kt in range(NKT):
                        v.wait_ge(SEM["ss"], ss_exp[kt])
                        phv = Pb[kt % 2][:].bitcast(BF16) \
                            .rearrange("p (n two) -> p n two", two=2)
                        v.tensor_tensor(Pl[kt % 2][:, :], Pb[kt % 2][:, :],
                                        phv[:, :, 1], op=SUB).then_inc(SEM["sv"], 1)
                vq(v_loop)
                C["sv"] += NKT

                st_av_fin = st_av[NKT - 1]

                def v_u1(v, st_av_fin=st_av_fin):
                    v.wait_ge(SEM["st"], st_av_fin)
                    v.tensor_copy(UEV[0:65, :], PS[2][0:65, 0:WIN]) \
                        .then_inc(SEM["sv"], 1)
                vq(v_u1)
                C["sv"] += 1
                sv_u1 = C["sv"]

                def y_u1(y, sv_u1=sv_u1):
                    y.wait_ge(SEM["sv"], sv_u1)
                    y.dma_start(out=UMV[0:33, :], in_=UEV[32:65, :]) \
                        .then_inc(SEM["sa"], 16)
                yq(y_u1)
                C["sa"] += 16
                sm_u1 = C["sa"]

                def v_u2(v, sm_u1=sm_u1):
                    v.wait_ge(SEM["sa"], sm_u1)
                    v.tensor_tensor(UST[0:32, :], UEV[0:32, :], UMV[0:32, :], op=ADD)
                    v.tensor_tensor(UST[0:32, :], UST[0:32, :], PS[3][0:32, 0:WIN],
                                    op=ADD)
                    v.tensor_tensor(UMV[32:33, :], UMV[32:33, :], PS[3][32:33, 0:WIN],
                                    op=ADD)
                    v.reciprocal(UMV[32:33, :], UMV[32:33, :]) \
                        .then_inc(SEM["sv"], 1)
                vq(v_u2)
                C["sv"] += 1
                sv_u2 = C["sv"]

                def y_u2(y, sv_u2=sv_u2):
                    y.wait_ge(SEM["sv"], sv_u2)
                    y.dma_start(out=ROWF[0:1, 0:WIN], in_=UMV[32:33, :]) \
                        .then_inc(SEM["sa"], 16)
                yq(y_u2)
                C["sa"] += 16

                def evac_r(v, ci, ps):
                    ln = ps.ap[-1][1] if False else min(512, WIN - ci * 512)
                    v.tensor_tensor(UST[0:32, ci * 512:ci * 512 + ln],
                                    UST[0:32, ci * 512:ci * 512 + ln],
                                    ps[0:32, :], op=MUL)
                bcast_row(WIN, evac_r, sa_gate=C["sa"])
                sv_u3 = C["sv"]

                def v_u4(v, sv_u3=sv_u3):
                    v.wait_ge(SEM["sv"], sv_u3)
                    v.tensor_copy(USH[0:32, :], UST[0:32, :])
                    v.tensor_tensor(UEV[0:32, :], UST[0:32, :], USH[0:32, :], op=SUB)
                    v.tensor_copy(USL[0:32, :], UEV[0:32, :]) \
                        .then_inc(SEM["sv"], 1)
                vq(v_u4)
                C["sv"] += 1
                sv_u4 = C["sv"]

                def y_u3(y, h=h, sv_u4=sv_u4):
                    y.wait_ge(SEM["sv"], sv_u4)
                    kc, j = h // 4, h % 4
                    y.dma_start(out=UallH[32 * j:32 * j + 32, kc * WIN:(kc + 1) * WIN],
                                in_=USH[0:32, :]).then_inc(SEM["sa"], 16)
                    y.dma_start(out=UallL[32 * j:32 * j + 32, kc * WIN:(kc + 1) * WIN],
                                in_=USL[0:32, :]).then_inc(SEM["sa"], 16)
                yq(y_u3)
                C["sa"] += 32

                if h + 1 < HEADS:
                    sm_q = qbuild(h + 1, (h + 1) % 2, w, gate_sv=sv_u2)
                if STAGE >= 5 and w >= 1 and NTF >= 2:
                    ngrp = NHC // 4
                    tot = (NTF - 1) * ngrp
                    per = -(-tot // HEADS)
                    for k_ in range(h * per, min((h + 1) * per, tot)):
                        ffn_th_chunk(w - 1, k_ // ngrp, groups=[k_ % ngrp])
                if w == 0:
                    if h < 6:
                        emit_wq_scan_part(h, 6)
                    elif h == 6:
                        emit_wq_tail()

            sm_uall = C["sa"]

            fslots = [[] for _ in range(WTT)]
            if STAGE >= 5 and w >= 1:
                items = []
                for th_ in range(NTF):
                    if th_ == NTF - 1:
                        items.append(("chunk", th_, 0))
                    for ttc_ in range(TFT):
                        items.append(("A", th_, ttc_))
                    for ttc_ in range(TFT):
                        items.append(("B", th_, ttc_))
                psl = -(-len(items) // WTT)
                for i_, it_ in enumerate(items):
                    fslots[min(i_ // psl, WTT - 1)].append(it_)

            def emit_fslot(si, wp=w - 1):
                for kind_, th_, ttc_ in fslots[si]:
                    if kind_ == "chunk":
                        ffn_th_chunk(wp, th_)
                    elif kind_ == "A":
                        ffn_ttc_A(wp, th_, ttc_)
                    else:
                        ffn_ttc_B(wp, th_, ttc_)

            sv_y = {}
            for tc in range(WTT):
                gate_x0 = sv_y.get(tc - 1, 0)

                def y_x0(y, tc=tc, q0=q0, gate_x0=gate_x0):
                    if gate_x0:
                        y.wait_ge(SEM["sv"], gate_x0)
                    y.dma_start(out=x0r[0][:, :],
                                in_=x0_h[q0 + tc * 128:q0 + (tc + 1) * 128, :]) \
                        .then_inc(SEM["xr"], 16)
                yq(y_x0)
                C["xr"] += 16
                si_x0 = C["xr"]
                gate_yt = sv_y.get(tc - 1, 0)

                wq_c = FST["sv_wq_c"] if w == 0 else 0

                def t_y(t, tc=tc, sm_uall=sm_uall, gate_yt=gate_yt, wq_c=wq_c):
                    t.wait_ge(SEM["sa"], sm_uall)
                    if wq_c:
                        t.wait_ge(SEM["sv"], wq_c)
                    if gate_yt:
                        t.wait_ge(SEM["sv"], gate_yt)
                    for nc2 in range(2):
                        a = PS[3][:, nc2 * 512:(nc2 + 1) * 512]
                        for kc in range(2):
                            u_sl = slice(kc * WIN + tc * 128, kc * WIN + (tc + 1) * 128)
                            g_sl = slice(kc * D + nc2 * 512, kc * D + (nc2 + 1) * 512)
                            t.matmul(a, UallH[:, u_sl], GH[:, g_sl],
                                     start=(kc == 0), stop=False)
                            t.matmul(a, UallL[:, u_sl], GH[:, g_sl],
                                     start=False, stop=False)
                            t.matmul(a, UallH[:, u_sl], GL[:, g_sl],
                                     start=False, stop=False)
                        t.matmul(a, ONE2[:, 0:128], BE2[:, nc2 * 512:(nc2 + 1) * 512],
                                 start=False, stop=True).then_inc(SEM["st"], 1)
                tq(t_y)
                C["st"] += 2
                st_y = C["st"]

                so_gate_y = C["so"]

                def v_y(v, tc=tc, w=w, st_y=st_y, si_x0=si_x0,
                        so_gate_y=so_gate_y):
                    v.wait_ge(SEM["st"], st_y)
                    v.wait_ge(SEM["xr"], si_x0)
                    v.wait_ge(SEM["so"], so_gate_y)
                    v.tensor_tensor(YT[:, 0:512], x0r[0][:, 0:512],
                                    PS[3][:, 0:512], op=ADD)
                    v.tensor_tensor(YT[:, 512:1024], x0r[0][:, 512:1024],
                                    PS[3][:, 512:1024], op=ADD)
                    v.bn_stats(bnb[:, 0:6], YT[:, 0:512])
                    v.bn_stats(bnb[:, 6:12], YT[:, 512:1024])
                    v.bn_aggr(mvb[:, 0:2], bnb[:, 0:12]) \
                        .then_inc(SEM["sv"], 1)
                vq(v_y)
                C["sv"] += 1
                sv_vy = C["sv"]
                FST["sv_vy_last"] = sv_vy

                def v_x1(v, tc=tc, w=w, sv_vy=sv_vy):
                    v.wait_ge(SEM["sv"], sv_vy)
                    v.tensor_scalar(stat[:, 2:3], mvb[:, 1:2], epsc[:, 0:1], None,
                                    op0=ADD)
                    s2i = stat[:, 2:3].bitcast(I32)
                    s1i = stat[:, 1:2].bitcast(I32)
                    v.tensor_scalar(s1i, s2i, 1, None,
                                    op0=AluOpType.arith_shift_right)
                    v.tensor_scalar(s1i, s1i, -1, 0x5F3759DF, op0=MUL, op1=ADD)
                    for _ in range(3):
                        v.tensor_tensor(stat[:, 3:4], stat[:, 1:2], stat[:, 1:2],
                                        op=MUL)
                        v.tensor_tensor(stat[:, 3:4], stat[:, 3:4], stat[:, 2:3],
                                        op=MUL)
                        v.tensor_scalar(stat[:, 3:4], stat[:, 3:4], -0.5, 1.5,
                                        op0=MUL, op1=ADD)
                        v.tensor_tensor(stat[:, 1:2], stat[:, 1:2], stat[:, 3:4],
                                        op=MUL)
                    v.tensor_scalar(x1[w % 2][:, tc * D:(tc + 1) * D], YT[:, :],
                                    mvb[:, 0:1], stat[:, 1:2], op0=SUB, op1=MUL) \
                        .then_inc(SEM["sv"], 1)
                vq(v_x1)
                C["sv"] += 1
                sv_y[tc] = C["sv"]
                if DEBUG_TAP:
                    def y_dbg(y, tc=tc, w=w, gate=sv_y[tc]):
                        y.wait_ge(SEM["sv"], gate)
                        y.dma_start(out=dbg_d[w * WIN + tc * 128:
                                              w * WIN + (tc + 1) * 128, :],
                                    in_=x1[w % 2][:, tc * D:(tc + 1) * D]) \
                            .then_inc(SEM["so"], 16)
                    yq(y_dbg)
                    C["so"] += 16
                if STAGE >= 5 and tc >= 1:
                    if tc == 1 and w >= 1:
                        emit_lhalo(w)
                    quant_tile(w, tc - 1, sv_y[tc - 1])
                    emit_fslot(tc - 1)
            if STAGE >= 5:
                quant_tile(w, WTT - 1, sv_y[WTT - 1])
                if w == NW - 1:
                    emit_zhalo(w)
                emit_fslot(WTT - 1)
            return C["sv"]

        FST = {"st_i1": {}, "sv_j": {}, "st_w2": {}, "ni1": 0, "nw2": 0,
               "sv_q": {}, "st_i1_w": {}, "sv_zhalo": 0, "sv_ps1": 0,
               "ss_sil": {}, "sv_xq2": {}, "st_w2_last": 0,
               "sv_wq_c": 0, "so_wq": 0, "sv_vy_last": 0}
        W1S = ["w1a", "w1b", "w1c", "w1d"]
        W2S = ["w2a", "w2b", "w2c", "w2d"]

        def quant_tile(w, tc, sv_gate):
            src = x1[w % 2][:, tc * D:(tc + 1) * D]
            st_gate = FST["st_i1_w"].get(w - 2, 0)

            def v_q(v, src=src, sv_gate=sv_gate):
                v.wait_ge(SEM["sv"], sv_gate)
                v.tensor_reduce(stat[:, 4:5], src, axis=AX.X, op=AluOpType.max,
                                apply_absolute_value=True)
                v.tensor_scalar(stat[:, 4:5], stat[:, 4:5], QEPS, None,
                                op0=AluOpType.max)
                v.reciprocal(stat[:, 5:6], stat[:, 4:5])
                v.tensor_scalar(stat[:, 5:6], stat[:, 5:6], 127.0, None, op0=MUL)
                v.tensor_scalar(IVX[:, 0:1], stat[:, 4:5], 1.0 / 127.0, None, op0=MUL)
                v.tensor_scalar(xqi[:, :], src, stat[:, 5:6], None, op0=MUL)
                v.tensor_copy(xqb[:, :], xqi[:, :]) \
                    .then_inc(SEM["sv"], 1)
            vq(v_q)
            C["sv"] += 1
            sv_qq = C["sv"]

            def t_tp(t, sv_qq=sv_qq):
                t.wait_ge(SEM["sv"], sv_qq)
                pb = PS[2][:, 0:512].bitcast(BF16)
                for fc in range(8):
                    t.matmul(pb[:, fc * 128:(fc + 1) * 128],
                             xqb[:, fc * 128:(fc + 1) * 128], idB[:],
                             start=True, stop=True, is_transpose=True) \
                        .then_inc(SEM["st"], 1)
                t.matmul(PS[2][0:1, 512:640], IVX[:, 0:1], idF[:], start=True,
                         stop=True, is_transpose=True).then_inc(SEM["st"], 1)
            tq(t_tp)
            C["st"] += 9
            st_tpe = C["st"]

            def v_tpe(v, w=w, tc=tc, st_tpe=st_tpe, st_gate=st_gate):
                v.wait_ge(SEM["st"], st_tpe)
                if st_gate:
                    v.wait_ge(SEM["st"], st_gate)
                pb = PS[2][:, 0:512].bitcast(BF16)
                buf = xqW[w % 2]
                r = None
                for fc in range(8):
                    base = fc * XQC
                    r = v.tensor_copy(
                        buf[:, base + 1 + tc * 128:base + 1 + (tc + 1) * 128],
                        pb[:, fc * 128:(fc + 1) * 128])
                    if tc == 0 and w >= 1:
                        r = v.tensor_copy(
                            xqW[(w - 1) % 2][:, base + WIN + 1:base + WIN + 2],
                            pb[:, fc * 128:fc * 128 + 1])
                r.then_inc(SEM["sv"], 1)
            vq(v_tpe)
            C["sv"] += 1

            def v_tpe2(v):
                v.tensor_copy(ROWF[0:1, 0:128], PS[2][0:1, 512:640]) \
                    .then_inc(SEM["sv"], 1)
            vq(v_tpe2)
            C["sv"] += 1

            def evac_iv(v, ci, ps, w=w, tc=tc):
                v.tensor_copy(INVSB[w % 2][:, 1 + tc * 128:1 + (tc + 1) * 128],
                              ps[:, 0:128])
                if tc == 0 and w >= 1:
                    v.tensor_copy(INVSB[(w - 1) % 2][:, WIN + 1:WIN + 2],
                                  ps[:, 0:1])
            bcast_row(128, evac_iv, extra_scale_col=wiv[0:1, 2:3])
            FST["sv_q"][(w, tc)] = C["sv"]
            return C["sv"]

        def emit_lhalo(w):
            st_gate = FST["st_i1_w"].get(w - 2, 0)

            def v_lh(v, w=w, st_gate=st_gate):
                if st_gate:
                    v.wait_ge(SEM["st"], st_gate)
                r = None
                for fc in range(8):
                    base = fc * XQC
                    r = v.tensor_copy(xqW[w % 2][:, base:base + 1],
                                      xqW[(w - 1) % 2][:, base + WIN:base + WIN + 1])
                r = v.tensor_copy(INVSB[w % 2][:, 0:1],
                                  INVSB[(w - 1) % 2][:, WIN:WIN + 1])
                r.then_inc(SEM["sv"], 1)
            vq(v_lh)
            C["sv"] += 1

        def emit_zhalo(w):
            st_gate = FST["st_i1_w"].get(w - 2, 0)

            def v_zh(v, w=w, st_gate=st_gate):
                if st_gate:
                    v.wait_ge(SEM["st"], st_gate)
                r = None
                for fc in range(8):
                    base = fc * XQC
                    r = v.memset(xqW[w % 2][:, base + WIN + 1:base + WIN + 2], 0.0)
                r = v.memset(INVSB[w % 2][:, WIN + 1:WIN + 2], 0.0)
                r.then_inc(SEM["sv"], 1)
            vq(v_zh)
            C["sv"] += 1
            FST["sv_zhalo"] = C["sv"]

        def ffn_th_chunk(w, th, groups=None):
            st_i1 = FST["st_i1"]
            sv_j = FST["sv_j"]
            if th == NTF - 1:
                gate_q = FST["sv_q"].get((w + 1, 0), FST["sv_zhalo"])
            else:
                gate_q = FST["sv_q"][(w, min(WTT - 1, (th * TF + TF) // 128))]
            st_w2_prev = FST["st_w2_last"]
            grange = (range(0, NHC, 4) if groups is None
                      else [g * 4 for g in groups])

            for g0_ in grange:
                base_ki = FST["ni1"]
                grp = []
                for gi_, hc in enumerate(range(g0_, g0_ + 4)):
                    ki1 = FST["ni1"]
                    FST["ni1"] += 1
                    gate_ld = st_i1.get(ki1 - 4, 0)
                    wsem = W1S[hc % 4]

                    def y_w1(y, hc=hc, gate_ld=gate_ld, wsem=wsem,
                             wqg=FST["so_wq"]):
                        y.wait_ge(SEM["wq"], wqg)
                        if gate_ld:
                            y.wait_ge(SEM["st"], gate_ld)
                        y.dma_start(out=w1r[hc % 4][:, :],
                                    in_=w1q_d[:, hc * 1024:(hc + 1) * 1024]) \
                            .then_inc(SEM[wsem], 16)
                    gq(y_w1)
                    C[wsem] += 16
                    grp.append((hc, ki1, C[wsem], wsem))

                svbase = C["sv"]

                def t_i1g(t, grp=tuple(grp), th=th, w=w, gate_q=gate_q,
                          base_ki=base_ki, svbase=svbase):
                    for gi_, (hc, ki1, si_w1, wsem) in enumerate(grp):
                        t.wait_ge(SEM[wsem], si_w1)
                        if gi_ == 0:
                            t.wait_ge(SEM["sv"], gate_q)
                        kim2 = ki1 - 2
                        if kim2 >= base_ki:
                            gate_j = svbase + (kim2 - base_ki) + 1
                        else:
                            gate_j = sv_j.get(kim2, 0)
                        if gate_j:
                            t.wait_ge(SEM["sv"], gate_j)
                        a = PS[0][:, (hc % 2) * 512:(hc % 2) * 512 + TF + 2]
                        for fc in range(8):
                            mm = t.matmul(
                                a, w1r[hc % 4][:, fc * 128:(fc + 1) * 128],
                                xqW[w % 2][:, fc * XQC + th * TF:
                                            fc * XQC + th * TF + TF + 2],
                                start=(fc == 0), stop=(fc == 7))
                        mm.then_inc(SEM["st"], 1)
                tq(t_i1g)
                for hc, ki1, _, _ in grp:
                    C["st"] += 1
                    st_i1[ki1] = C["st"]

                for hc, ki1, _, _ in grp:
                    st_here = st_i1[ki1]
                    ss_cva = C["ss"]

                    def v_j(v, hc=hc, th=th, w=w, st_here=st_here, ss_cva=ss_cva):
                        v.wait_ge(SEM["st"], st_here)
                        v.wait_ge(SEM["ss"], ss_cva)
                        a = PS[0][:, (hc % 2) * 512:(hc % 2) * 512 + TF + 2]
                        J = Jb[hc % 2]
                        v.tensor_tensor(J[:, :], a,
                                        INVSB[w % 2][:, th * TF:th * TF + TF + 2],
                                        op=MUL)
                        v.tensor_scalar(cvA[:, :], J[:, 1:TF + 1],
                                        convw[:, NHC + hc:NHC + hc + 1],
                                        convb[:, hc:hc + 1], op0=MUL, op1=ADD)
                        v.scalar_tensor_tensor(cvA[:, :], J[:, 0:TF],
                                               convw[:, hc:hc + 1], cvA[:, :],
                                               op0=MUL, op1=ADD)
                        v.scalar_tensor_tensor(cvA[:, :], J[:, 2:TF + 2],
                                               convw[:, 2 * NHC + hc:2 * NHC + hc + 1],
                                               cvA[:, :], op0=MUL, op1=ADD) \
                            .then_inc(SEM["sv"], 1)
                    vq(v_j)
                    C["sv"] += 1
                    sv_j[ki1] = C["sv"]
                    sv_cv = C["sv"]

                    def s_si(s, hc=hc, sv_cv=sv_cv, st_w2_prev=st_w2_prev):
                        s.wait_ge(SEM["sv"], sv_cv)
                        if st_w2_prev:
                            s.wait_ge(SEM["st"], st_w2_prev)
                        s.activation(out=hsil[:, hc * TF:(hc + 1) * TF], in_=cvA[:, :],
                                     func=AF.Silu) \
                            .then_inc(SEM["ss"], 1)
                    sq(s_si)
                    C["ss"] += 1
                FST["ss_sil"][(w, th)] = C["ss"]
                if th == NTF - 1:
                    FST["st_i1_w"][w] = st_i1[FST["ni1"] - 1]

        def ffn_ttc_A(w, th, ttc):
            ss_sil = FST["ss_sil"][(w, th)]
            if True:
                for ttc in [ttc]:
                    def v_am(v, ttc=ttc, ss_sil=ss_sil):
                        v.wait_ge(SEM["ss"], ss_sil)
                        h3 = hsil[:].rearrange("p (hc t) -> p t hc", hc=NHC)
                        v.tensor_reduce(amx[:, 0:128],
                                        h3[:, ttc * 128:(ttc + 1) * 128, :],
                                        axis=AX.X, op=AluOpType.max,
                                        apply_absolute_value=True) \
                            .then_inc(SEM["sv"], 1)
                    vq(v_am)
                    C["sv"] += 1
                    sv_am = C["sv"]

                    def t_am(t, sv_am=sv_am):
                        t.wait_ge(SEM["sv"], sv_am)
                        t.matmul(PS[2][:, 0:128], amx[:, 0:128], idF[:], start=True,
                                 stop=True, is_transpose=True).then_inc(SEM["st"], 1)
                    tq(t_am)
                    C["st"] += 1
                    st_am = C["st"]

                    def v_s2(v, ttc=ttc, st_am=st_am):
                        v.wait_ge(SEM["st"], st_am)
                        v.tensor_reduce(stat[:, 6:7], PS[2][:, 0:128], axis=AX.X,
                                        op=AluOpType.max)
                        v.tensor_scalar(stat[:, 6:7], stat[:, 6:7], QEPS, None,
                                        op0=AluOpType.max)
                        v.reciprocal(stat[:, 7:8], stat[:, 6:7])
                        v.tensor_scalar(stat[:, 7:8], stat[:, 7:8], 127.0, None,
                                        op0=MUL)
                        v.tensor_scalar(iv2c[:, ttc:ttc + 1], stat[:, 6:7],
                                        1.0 / 127.0, None, op0=MUL)
                        v.tensor_scalar(iv2c[:, ttc:ttc + 1], iv2c[:, ttc:ttc + 1],
                                        wiv[:, 3:4], None, op0=MUL) \
                            .then_inc(SEM["sv"], 1)
                    vq(v_s2)
                    C["sv"] += 1
                    sv_s2 = C["sv"]

                    def t_s2(t, sv_s2=sv_s2):
                        t.wait_ge(SEM["sv"], sv_s2)
                        t.matmul(PS[2][0:1, 0:128], stat[:, 7:8], idF[:], start=True,
                                 stop=True, is_transpose=True).then_inc(SEM["st"], 1)
                    tq(t_s2)
                    C["st"] += 1
                    st_s2 = C["st"]

                    def v_s2b(v, st_s2=st_s2):
                        v.wait_ge(SEM["st"], st_s2)
                        v.tensor_copy(ROWF[0:1, 0:128], PS[2][0:1, 0:128]) \
                            .then_inc(SEM["sv"], 1)
                    vq(v_s2b)
                    C["sv"] += 1

                    def evac_s2(v, ci, ps, ttc=ttc):
                        v.tensor_copy(s2b[:, ttc * 128:(ttc + 1) * 128], ps[:, 0:128])
                    bcast_row(128, evac_s2)
                    sv_s2bb = C["sv"]

                    def v_xq2(v, ttc=ttc, sv_s2bb=sv_s2bb):
                        v.wait_ge(SEM["sv"], sv_s2bb)
                        for hc in range(NHC):
                            cs = hc * TF + ttc * 128
                            v.tensor_tensor(xqi[:, 0:128], hsil[:, cs:cs + 128],
                                            s2b[:, ttc * 128:(ttc + 1) * 128], op=MUL)
                            v.tensor_copy(hsil[:, cs:cs + 128], xqi[:, 0:128])
                        v.tensor_copy(stat[:, 15:16], stat[:, 15:16]) \
                            .then_inc(SEM["sv"], 1)
                    vq(v_xq2)
                    C["sv"] += 1
                    FST["sv_xq2"][(w, th, ttc)] = C["sv"]

        def ffn_ttc_B(w, th, ttc):
            q0 = w * WIN
            XB = x1[w % 2]
            st_w2g = FST["st_w2"]
            if True:
                for ttc in [ttc]:
                    sv_xq2 = max(FST["sv_xq2"][(w, th, ttc)], FST["sv_ps1"])
                    for g0_ in range(0, NHC, 4):
                        grp = []
                        for hc in range(g0_, g0_ + 4):
                            kw2 = FST["nw2"]
                            FST["nw2"] += 1
                            gate_ld = st_w2g.get(kw2 - 4, 0)
                            wsem = W2S[hc % 4]

                            def y_w2(y, hc=hc, gate_ld=gate_ld, wsem=wsem,
                                     wqg=FST["so_wq"]):
                                y.wait_ge(SEM["wq"], wqg)
                                if gate_ld:
                                    y.wait_ge(SEM["st"], gate_ld)
                                y.dma_start(out=w2r[hc % 4][:, :],
                                            in_=w2q_d[:,
                                                      hc * 1024:(hc + 1) * 1024]) \
                                    .then_inc(SEM[wsem], 16)
                            yq(y_w2)
                            C[wsem] += 16
                            grp.append((hc, kw2, C[wsem], wsem))

                        def t_w2g(t, grp=tuple(grp), ttc=ttc, sv_xq2=sv_xq2):
                            for hc, kw2, si_w2, wsem in grp:
                                t.wait_ge(SEM[wsem], si_w2)
                                if hc == 0:
                                    t.wait_ge(SEM["sv"], sv_xq2)
                                xsl = hsil[:, hc * TF + ttc * 128:
                                           hc * TF + (ttc + 1) * 128]
                                for nc2 in range(2):
                                    mm = t.matmul(
                                        PS[1][:, nc2 * 512:(nc2 + 1) * 512], xsl,
                                        w2r[hc % 4][:, nc2 * 512:(nc2 + 1) * 512],
                                        start=(hc == 0), stop=(hc == NHC - 1))
                                mm.then_inc(SEM["st"], 1)
                        tq(t_w2g)
                        for hc, kw2, _, _ in grp:
                            C["st"] += 1
                            st_w2g[kw2] = C["st"]
                    st_w2fin = C["st"]
                    FST["st_w2_last"] = C["st"]

                    glob_t = th * TFT + ttc
                    so_gate = C["so"]

                    def v_y2(v, glob_t=glob_t, ttc=ttc, w=w, st_w2fin=st_w2fin,
                             so_gate=so_gate):
                        v.wait_ge(SEM["st"], st_w2fin)
                        v.wait_ge(SEM["so"], so_gate)
                        v.scalar_tensor_tensor(YT[:, 0:512], PS[1][:, 0:512],
                                               iv2c[:, ttc:ttc + 1],
                                               XB[:, glob_t * D:glob_t * D + 512],
                                               op0=MUL, op1=ADD)
                        v.scalar_tensor_tensor(YT[:, 512:1024], PS[1][:, 512:1024],
                                               iv2c[:, ttc:ttc + 1],
                                               XB[:, glob_t * D + 512:(glob_t + 1) * D],
                                               op0=MUL, op1=ADD)
                        v.bn_stats(bnb[:, 0:6], YT[:, 0:512])
                        v.bn_stats(bnb[:, 6:12], YT[:, 512:1024])
                        v.bn_aggr(mvb[:, 0:2], bnb[:, 0:12]) \
                            .then_inc(SEM["sv"], 1)
                    vq(v_y2)
                    C["sv"] += 1
                    sv_y2 = C["sv"]
                    FST["sv_ps1"] = C["sv"]

                    def v_x2(v, sv_y2=sv_y2):
                        v.wait_ge(SEM["sv"], sv_y2)
                        v.tensor_scalar(stat[:, 2:3], mvb[:, 1:2], epsc[:, 0:1],
                                        None, op0=ADD)
                        s2i = stat[:, 2:3].bitcast(I32)
                        s1i = stat[:, 1:2].bitcast(I32)
                        v.tensor_scalar(s1i, s2i, 1, None,
                                        op0=AluOpType.arith_shift_right)
                        v.tensor_scalar(s1i, s1i, -1, 0x5F3759DF, op0=MUL, op1=ADD)
                        for _ in range(3):
                            v.tensor_tensor(stat[:, 3:4], stat[:, 1:2], stat[:, 1:2],
                                            op=MUL)
                            v.tensor_tensor(stat[:, 3:4], stat[:, 3:4], stat[:, 2:3],
                                            op=MUL)
                            v.tensor_scalar(stat[:, 3:4], stat[:, 3:4], -0.5, 1.5,
                                            op0=MUL, op1=ADD)
                            v.tensor_tensor(stat[:, 1:2], stat[:, 1:2], stat[:, 3:4],
                                            op=MUL)
                        v.tensor_scalar(YT[:, :], YT[:, :], mvb[:, 0:1], stat[:, 1:2],
                                        op0=SUB, op1=MUL).then_inc(SEM["sv"], 1)
                    vq(v_x2)
                    C["sv"] += 1
                    sv_x2 = C["sv"]

                    def s_out(s, glob_t=glob_t, q0=q0, sv_x2=sv_x2):
                        s.wait_ge(SEM["sv"], sv_x2)
                        s.dma_start(out=out_h[q0 + glob_t * 128:
                                              q0 + (glob_t + 1) * 128, :],
                                    in_=YT[:, :]).then_inc(SEM["so"], 16)
                    sq(s_out)
                    C["so"] += 16

        def ffn_window_seq(w):
            for th in range(NTF):
                ffn_th_chunk(w, th)
                for ttc in range(TFT):
                    ffn_ttc_A(w, th, ttc)
                for ttc in range(TFT):
                    ffn_ttc_B(w, th, ttc)

        def _dbg(tag):
            if DEBUG_LEDGER:
                print(f"LEDGER {tag}: C_sv={C['sv']} "
                      f"actual={PTOT['v'].get(SEM['sv'], 0)} "
                      f"C_ss={C['ss']} actual_s={PTOT['s'].get(SEM['ss'], 0)}")

        sv_x1 = {}
        _dbg("startup")
        if STAGE >= 3:
            sv_x1[0] = attn_window(0)
            _dbg("attn0")
            if STAGE >= 4:
                for w in range(1, NW):
                    sv_x1[w] = attn_window(w)
                    _dbg(f"attn{w}")
                if STAGE >= 5:
                    ffn_window_seq(NW - 1)
                    _dbg(f"ffn{NW - 1}")

        SO_FIN = C["so"]

        def s_fin(s):
            s.wait_ge(SEM["so"], SO_FIN)
        sq(s_fin)

        for _k, _sem in SEM.items():
            _tv = PTOT["v"].get(_sem, 0)
            _ts = PTOT["s"].get(_sem, 0)
            if _tv or _ts:
                assert C[_k] == _tv + _ts, \
                    f"ledger mismatch {_k}: C={C[_k]} proxied={_tv + _ts}"

    return nc


_NC_CACHE = {}


def kernel(**inputs):
    S = inputs["L0_attn_h_t"].shape[1]
    B = inputs["L0_attn_h_t"].shape[0]
    WIN = 512 if S % 512 == 0 else 256
    key = (S, WIN)
    if key not in _NC_CACHE:
        _NC_CACHE[key] = build_nc(S=S, WIN=WIN, TF=256)
    nc = _NC_CACHE[key]

    f = np.float32
    asc = np.ascontiguousarray
    base = {
        "Wproj": asc(np.concatenate([inputs["L0_attn_W_DKV"], inputs["L0_attn_W_KR"],
                                     inputs["L0_attn_W_DQ"]], axis=1), dtype=f),
        "bkvr": asc(np.concatenate([np.asarray(inputs["L0_attn_b_DKV"]),
                                    np.zeros(32, f)])[:, None], dtype=f),
        "bq": asc(np.asarray(inputs["L0_attn_b_DQ"])[:, None], dtype=f),
        "WUQT": asc(np.asarray(inputs["L0_attn_W_UQ"]).T, dtype=f),
        "WUKT": asc(np.asarray(inputs["L0_attn_W_UK"]).T, dtype=f),
        "WUVT": asc(np.asarray(inputs["L0_attn_W_UV"]).T, dtype=f),
        "bUQ": asc(np.asarray(inputs["L0_attn_b_UQ"])[:, None], dtype=f),
        "bUV": asc(np.asarray(inputs["L0_attn_b_UV"])[:, None], dtype=f),
        "WQR": asc(np.asarray(inputs["L0_attn_W_QR"]), dtype=f),
        "bQR": asc(np.asarray(inputs["L0_attn_b_QR"])[None, :], dtype=f),
        "WO": asc(np.asarray(inputs["L0_attn_W_O"]), dtype=f),
        "bO": asc(np.asarray(inputs["L0_attn_b_O"])[None, :], dtype=f),
        "w1T": asc(np.asarray(inputs["L0_ffn_w1"]).T, dtype=f),
        "w2T": asc(np.asarray(inputs["L0_ffn_w2"]).T, dtype=f),
        "convw_pre": asc(np.asarray(inputs["L0_ffn_conv_w"]).reshape(NHC, 128, 3)
                         .transpose(1, 2, 0).reshape(128, 3 * NHC), dtype=f),
        "convb_pre": asc(np.asarray(inputs["L0_ffn_conv_b"]).reshape(NHC, 128).T,
                         dtype=f),
    }
    ht = np.asarray(inputs["L0_attn_h_t"], dtype=f)
    in_maps = []
    for b in range(B):
        m = dict(base)
        m["x0"] = asc(ht[b])
        m["x0T"] = asc(ht[b].T)
        in_maps.append(m)
    res = run_bass_kernel_spmd(nc, in_maps, core_ids=list(range(B)))
    return np.stack([np.asarray(res.results[b]["out"], dtype=np.float32)
                     for b in range(B)], axis=0)


# version 20
# speedup vs baseline: 1.0248x; 1.0248x over previous
import math
import contextlib
import numpy as np
import ml_dtypes
import concourse.bass as bass
import concourse.mybir as mybir
from concourse.alu_op_type import AluOpType
from concourse.bass_utils import run_bass_kernel_spmd

F32 = mybir.dt.float32
BF16 = mybir.dt.bfloat16
FP8 = mybir.dt.float8e4
I32 = mybir.dt.int32
AF = mybir.ActivationFunctionType
AX = mybir.AxisListType
SUB = AluOpType.subtract
ADD = AluOpType.add
MUL = AluOpType.mult

D = 1024
HEADS = 8
DC = 32
DR = 32
DK = 128
DH = 4096
NHC = DH // 128
SOFT_SCALE = 1.0 / math.sqrt(DK + DR)
CSHIFT = 30.0
EPS = 1e-5
QEPS = 1e-5
RMAGIC = 12582912.0


STAGE = 99
DEBUG_LEDGER = False
DEBUG_TAP = False


def build_nc(S=4096, WIN=1024, TF=128):
    NW = S // WIN
    NKT = S // 128
    WTT = WIN // 128
    NTF = WIN // TF
    TFT = TF // 128
    NCH = min(WIN, 512)
    NNC = WIN // NCH
    assert NW >= 2 and TF % 128 == 0 and TF + 2 <= 512 and WIN % TF == 0 \
        and (WIN % 512 == 0 or WIN == 256)

    nc = bass.Bass("TRN2", target_bir_lowering=False)

    x0_h = nc.dram_tensor("x0", [S, D], F32, kind="ExternalInput").ap()
    x0T_h = nc.dram_tensor("x0T", [D, S], F32, kind="ExternalInput").ap()
    Wproj_h = nc.dram_tensor("Wproj", [D, 96], F32, kind="ExternalInput").ap()
    bkvr_h = nc.dram_tensor("bkvr", [64, 1], F32, kind="ExternalInput").ap()
    bq_h = nc.dram_tensor("bq", [32, 1], F32, kind="ExternalInput").ap()
    WUQT_h = nc.dram_tensor("WUQT", [HEADS * DK, DC], F32, kind="ExternalInput").ap()
    WUKT_h = nc.dram_tensor("WUKT", [HEADS * DK, DC], F32, kind="ExternalInput").ap()
    WUVT_h = nc.dram_tensor("WUVT", [HEADS * DK, DC], F32, kind="ExternalInput").ap()
    bUQ_h = nc.dram_tensor("bUQ", [HEADS * DK, 1], F32, kind="ExternalInput").ap()
    bUV_h = nc.dram_tensor("bUV", [HEADS * DK, 1], F32, kind="ExternalInput").ap()
    WQR_h = nc.dram_tensor("WQR", [DC, HEADS * DR], F32, kind="ExternalInput").ap()
    bQR_h = nc.dram_tensor("bQR", [1, HEADS * DR], F32, kind="ExternalInput").ap()
    WO_h = nc.dram_tensor("WO", [HEADS * DK, D], F32, kind="ExternalInput").ap()
    bO_h = nc.dram_tensor("bO", [1, D], F32, kind="ExternalInput").ap()
    w1T_h = nc.dram_tensor("w1T", [D, DH], F32, kind="ExternalInput").ap()
    w2T_h = nc.dram_tensor("w2T", [DH, D], F32, kind="ExternalInput").ap()
    cw_h = nc.dram_tensor("convw_pre", [128, 3 * NHC], F32, kind="ExternalInput").ap()
    cb_h = nc.dram_tensor("convb_pre", [128, NHC], F32, kind="ExternalInput").ap()
    out_h = nc.dram_tensor("out", [S, D], F32, kind="ExternalOutput").ap()
    w1q_d = nc.dram_tensor("w1q_scr", [128, NHC * 1024], FP8, kind="Internal").ap()
    w2q_d = nc.dram_tensor("w2q_scr", [128, NHC * 1024], FP8, kind="Internal").ap()
    dbg_d = (nc.dram_tensor("dbg_x1", [S, 1024], F32, kind="Internal").ap()
             if DEBUG_TAP else None)

    C = {k: 0 for k in ["si", "sm", "st", "sv", "ss", "so",
                        "wa", "wb", "w1a", "w1b", "w1c", "w1d",
                        "w2a", "w2b", "w2c", "w2d", "xr",
                        "sb", "qa", "qb", "sa", "wq"]}
    es = contextlib.ExitStack()
    cms = {}
    astack = []

    def mkalloc(name, shape, dt):
        cm = nc.sbuf_tensor(name, shape, dt)
        t = cm.__enter__()
        cms[name] = cm
        astack.append(name)
        return t

    def free(*names):
        want = set(names)
        for n in reversed(list(names)):
            pass
        while want:
            top = astack.pop()
            assert top in want, f"non-LIFO free: {top} vs {want}"
            want.remove(top)
            cms[top].__exit__(None, None, None)

    def free_rest():
        while astack:
            cms[astack.pop()].__exit__(None, None, None)

    with es:
        es.callback(free_rest)
        SEM = {k: es.enter_context(nc.semaphore(k)) for k in C}

        x1 = [mkalloc(f"x1_{i}", [128, WTT * D], F32) for i in range(2)]
        K1 = mkalloc("K1", [128, S], BF16)
        K2 = mkalloc("K2", [64, S], BF16)
        CST1 = mkalloc("CST1", [128, NKT * 65], BF16)
        CST2 = mkalloc("CST2", [128, NKT * 33], BF16)
        CQD = mkalloc("CQD", [97, S], BF16)
        CQL = mkalloc("CQL", [33, S], BF16)
        QBH = mkalloc("QBH", [97, HEADS * 64], BF16)
        QBL = mkalloc("QBL", [33, HEADS * 64], BF16)
        GH = mkalloc("GH", [128, 2 * D], BF16)
        GL = mkalloc("GL", [128, 2 * D], BF16)
        BE2 = mkalloc("BE2", [2, D], BF16)
        ONE2 = mkalloc("ONE2", [2, 128], BF16)
        Pb = [mkalloc(f"P_{i}", [128, WIN], F32) for i in range(2)]
        Pl = [mkalloc(f"Pl_{i}", [128, WIN], BF16) for i in range(2)]
        Q2 = [mkalloc(f"Q2_{i}", [128, WIN], BF16) for i in range(2)]
        QST = mkalloc("QST", [64, WIN], BF16)
        UallH = mkalloc("UallH", [128, 2 * WIN], BF16)
        UallL = mkalloc("UallL", [128, 2 * WIN], BF16)
        UEV = mkalloc("UEV", [65, WIN], F32)
        UMV = mkalloc("UMV", [33, WIN], F32)
        UST = mkalloc("UST", [32, WIN], F32)
        USH = mkalloc("USH", [32, WIN], BF16)
        USL = mkalloc("USL", [32, WIN], BF16)
        ROWF = mkalloc("ROWF", [1, WIN + 128], F32)
        ROWH = mkalloc("ROWH", [1, WIN + 128], BF16)
        ROWL = mkalloc("ROWL", [1, WIN + 128], BF16)
        x0r = [mkalloc(f"x0r_{i}", [128, D], F32) for i in range(1)]
        YT = mkalloc("YT", [128, D], F32)
        INVSB = [mkalloc(f"INVSB_{i}", [128, WIN + 2], F32) for i in range(2)]
        IVX = mkalloc("IVX", [128, 1], F32)
        XQC = WIN + 2
        xqW = [mkalloc(f"xqW_{i}", [128, 8 * XQC], BF16) for i in range(2)]
        hsil = mkalloc("hsil", [128, NHC * TF], BF16)
        cvA = mkalloc("cvA", [128, TF], F32)
        amx = mkalloc("amx", [128, 128], F32)
        s2b = mkalloc("s2b", [128, TF], F32)
        iv2c = mkalloc("iv2c", [128, TFT], F32)
        w1r = [mkalloc(f"w1r_{i}", [128, 1024], FP8) for i in range(4)]
        w2r = [mkalloc(f"w2r_{i}", [128, 1024], FP8) for i in range(4)]
        Jb = [mkalloc(f"J_{i}", [128, TF + 2], F32) for i in range(2)]
        idB = mkalloc("idB", [128, 128], BF16)
        idF = mkalloc("idF", [128, 128], F32)
        cneg = mkalloc("cneg", [128, 1], F32)
        epsc = mkalloc("epsc", [128, 1], F32)
        convb = mkalloc("convb", [128, NHC], F32)
        convw = mkalloc("convw", [128, 3 * NHC], F32)
        stat = mkalloc("stat", [128, 16], F32)
        bnb = mkalloc("bnb", [128, 12], F32)
        mvb = mkalloc("mvb", [128, 2], F32)
        xqi = mkalloc("xqi", [128, D], I32)
        xqb = mkalloc("xqb", [128, D], BF16)
        wiv = mkalloc("wiv", [128, 4], F32)
        rs32 = mkalloc("rs32", [33, 128], F32)
        fen = mkalloc("fen", [1, 8], F32)

        PS = [es.enter_context(nc.psum_tensor(f"ps{i}", [128, 1024], F32))
              for i in range(4)]

        blk = es.enter_context(nc.Block())

        _VOPS = {"tensor_copy", "tensor_tensor", "tensor_scalar",
                 "scalar_tensor_tensor", "tensor_reduce", "reciprocal", "memset",
                 "bn_stats", "bn_aggr", "activation"}

        QC = {"v": 0, "s": 0}
        QS = {"v": es.enter_context(nc.semaphore("qv")),
              "s": es.enter_context(nc.semaphore("qs"))}
        PTOT = {"v": {}, "s": {}}
        PEND = {"v": {}, "s": {}}

        def _drain_pend(eng, k):
            for sm_, vv in PEND[k].items():
                eng.wait_ge(sm_, vv)
            PEND[k].clear()

        class _Proxy:
            def __init__(self, eng, key):
                self._e = eng
                self._k = key

            def then_inc(self, sem, amt):
                k = self._k
                _drain_pend(self._e, k)
                self._e.wait_ge(QS[k], QC[k])
                r = self._e.nop().then_inc(sem, amt)
                PTOT[k][sem] = PTOT[k].get(sem, 0) + amt
                PEND[k][sem] = PTOT[k][sem]
                return r

        class _Serial:
            def __init__(self, eng, key):
                self._e = eng
                self._k = key

            def __getattr__(self, name):
                a = getattr(self._e, name)
                if name in _VOPS:
                    def call(*args, _a=a, **kw):
                        _drain_pend(self._e, self._k)
                        if QC[self._k] > 0:
                            self._e.wait_ge(QS[self._k], QC[self._k])
                        r = _a(*args, **kw)
                        r.then_inc(QS[self._k], 1)
                        QC[self._k] += 1
                        return _Proxy(self._e, self._k)
                    return call
                return a

        def vq(f):
            blk.vector(lambda v: f(_Serial(v, "v")))

        def sq(f):
            blk.scalar(lambda s2_: f(_Serial(s2_, "s")))

        yq, tq, gq = blk.sync, blk.tensor, blk.gpsimd


        def g0(g):
            g.memset(idF[:], 0.0).then_inc(SEM["so"], 1)
            g.memset(idB[:], 0.0).then_inc(SEM["so"], 1)
            g.wait_ge(SEM["so"], 2)
            g.affine_select(out=idF[:], in_=idF[:], compare_op=AluOpType.not_equal,
                            fill=1.0, base=0, pattern=[[-1, 128]],
                            channel_multiplier=1).then_inc(SEM["so"], 1)
            g.affine_select(out=idB[:], in_=idB[:], compare_op=AluOpType.not_equal,
                            fill=1.0, base=0, pattern=[[-1, 128]],
                            channel_multiplier=1).then_inc(SEM["so"], 1)
            g.memset(cneg[:], -CSHIFT).then_inc(SEM["so"], 1)
            g.memset(epsc[:], EPS).then_inc(SEM["so"], 1)
            g.memset(ONE2[:], 1.0).then_inc(SEM["so"], 1)
            g.memset(CST1[:], 0.0).then_inc(SEM["so"], 1)
            g.memset(CST2[:], 0.0).then_inc(SEM["so"], 1)
            g.memset(xqW[0][:], 0.0).then_inc(SEM["so"], 1)
            g.memset(xqW[1][:], 0.0).then_inc(SEM["so"], 1)
            g.memset(INVSB[0][:], 0.0).then_inc(SEM["so"], 1)
            g.memset(INVSB[1][:], 0.0).then_inc(SEM["so"], 1)
            g.memset(CQL[:, :], 0.0).then_inc(SEM["so"], 1)
            g.memset(stat[:], 0.0).then_inc(SEM["so"], 1)
            g.wait_ge(SEM["so"], 15)
            g.nop().then_inc(SEM["sm"], 1)
        gq(g0)
        C["so"] += 15
        C["sm"] += 1
        SM_G0 = C["sm"]

        def v0(v):
            v.memset(CQD[32:64, :], 0.0).then_inc(SEM["sv"], 1)
            v.memset(QBH[32:64, :], 0.0)
            v.wait_ge(SEM["sv"], C["sv"] + 1)
            v.memset(CQD[32:33, :], 1.0) \
                .then_inc(SEM["sv"], 1)
        vq(v0)
        C["sv"] += 2

        WP = mkalloc("WP", [128, 8 * 96], F32)
        WPH = mkalloc("WPH", [128, 8 * 96], BF16)
        WPL = mkalloc("WPL", [128, 8 * 96], BF16)
        BKV = mkalloc("BKV", [64, 1], F32)
        BQ = mkalloc("BQ", [32, 1], F32)
        MQW = mkalloc("MQW", [128, HEADS * 33], F32)
        MKW = mkalloc("MKW", [128, HEADS * 32], F32)
        MVW = mkalloc("MVW", [128, HEADS * 33], F32)
        WOC = YT
        WOHs = mkalloc("WOHs", [128, D], BF16)
        WOLs = mkalloc("WOLs", [128, D], BF16)
        QRW = mkalloc("QRW", [33, HEADS * DR], F32)

        def y1(y):
            n = 0
            for fc in range(8):
                y.dma_start(out=WP[:, fc * 96:fc * 96 + 96],
                            in_=Wproj_h[fc * 128:(fc + 1) * 128, :]).then_inc(SEM["si"], 16)
                n += 1
            y.dma_start(out=BKV[:, :], in_=bkvr_h[:, :]).then_inc(SEM["si"], 16)
            y.dma_start(out=BQ[:, :], in_=bq_h[:, :]).then_inc(SEM["si"], 16)
            n += 2
            for h in range(HEADS):
                y.dma_start(out=MQW[:, h * 33:h * 33 + 32],
                            in_=WUQT_h[h * DK:(h + 1) * DK, :]).then_inc(SEM["si"], 16)
                y.dma_start(out=MQW[:, h * 33 + 32:h * 33 + 33],
                            in_=bUQ_h[h * DK:(h + 1) * DK, :]).then_inc(SEM["si"], 16)
                y.dma_start(out=MKW[:, h * 32:(h + 1) * 32],
                            in_=WUKT_h[h * DK:(h + 1) * DK, :]).then_inc(SEM["si"], 16)
                y.dma_start(out=MVW[:, h * 33:h * 33 + 32],
                            in_=WUVT_h[h * DK:(h + 1) * DK, :]).then_inc(SEM["si"], 16)
                y.dma_start(out=MVW[:, h * 33 + 32:h * 33 + 33],
                            in_=bUV_h[h * DK:(h + 1) * DK, :]).then_inc(SEM["si"], 16)
                n += 5
            y.dma_start(out=QRW[0:32, :], in_=WQR_h[:, :]).then_inc(SEM["si"], 16)
            y.dma_start(out=QRW[32:33, :], in_=bQR_h[:, :]).then_inc(SEM["si"], 16)
            n += 2
            y.dma_start(out=convb[:, :], in_=cb_h[:, :]).then_inc(SEM["si"], 16)
            y.dma_start(out=convw[:, :], in_=cw_h[:, :]).then_inc(SEM["si"], 16)
            n += 2
            C["si"] += 16 * n
        yq(y1)
        SI_SMALL = C["si"]

        def w_unit_src(widx, hc):
            if widx == 0:
                return w1T_h[:, hc * 128:(hc + 1) * 128] \
                    .rearrange("(fc p) c -> p fc c", p=128)
            return w2T_h[hc * 128:(hc + 1) * 128, :]

        WQB = {}

        def emit_wq_alloc():
            WQB["WFa"] = mkalloc("WFa", [128, 1024], F32)
            WQB["WFb"] = mkalloc("WFb", [128, 1024], F32)
            WQB["WQ8"] = mkalloc("WQ8", [128, 1024], FP8)
            WQB["WAB"] = mkalloc("WAB", [128, 2 * NHC], F32)
            WQB["sv0"] = C["sv"]
            WQB["sm0"] = C["sm"]
            WQB["si0"] = C["si"]

        def emit_wq_scan_part(part, nparts):
            WFa, WFb, WAB = WQB["WFa"], WQB["WFb"], WQB["WAB"]
            per = (2 * NHC) // nparts
            for idx in range(part * per, (part + 1) * per):
                widx, hc = divmod(idx, NHC)
                buf = WFa if hc % 2 == 0 else WFb
                gate = max(0, C["sv"] - 1)

                wsem = "wa" if hc % 2 == 0 else "wb"

                def yA(y, buf=buf, widx=widx, hc=hc, gate=gate, idx=idx,
                       wsem=wsem, g0=dict(WQB)):
                    if idx >= 2:
                        y.wait_ge(SEM["sv"], gate)
                    else:
                        y.wait_ge(SEM["sv"], g0["sv0"])
                        y.wait_ge(SEM["sm"], g0["sm0"])
                        y.wait_ge(SEM["si"], g0["si0"])
                    y.dma_start(out=buf[:, :], in_=w_unit_src(widx, hc)) \
                        .then_inc(SEM[wsem], 16)
                yq(yA)
                C[wsem] += 16
                si_here = C[wsem]

                def vAb(v, buf=buf, col=WAB[:, idx:idx + 1], si_here=si_here,
                        wsem=wsem):
                    v.wait_ge(SEM[wsem], si_here)
                    v.tensor_reduce(col, buf[:, :], axis=AX.X, op=ADD,
                                    apply_absolute_value=True) \
                        .then_inc(SEM["sv"], 1)
                vq(vAb)
                C["sv"] += 1
            WQB["SV_ABS"] = C["sv"]

        def emit_wq_tail():
            WFa, WFb = WQB["WFa"], WQB["WFb"]
            WQ8, WAB = WQB["WQ8"], WQB["WAB"]
            WI = xqi
            SV_ABS = WQB["SV_ABS"]

            def vA(v):
                v.wait_ge(SEM["sv"], SV_ABS)
                for widx in range(2):
                    v.tensor_reduce(stat[:, widx:widx + 1],
                                    WAB[:, widx * NHC:(widx + 1) * NHC],
                                    axis=AX.X, op=ADD)
                v.tensor_copy(stat[:, 15:16], stat[:, 15:16]).then_inc(SEM["sv"], 1)
            vq(vA)
            C["sv"] += 1
            SV_A = C["sv"]

            def tA(t, SV_A=SV_A):
                t.wait_ge(SEM["sv"], SV_A)
                t.matmul(PS[3][0:1, 0:128], stat[:, 0:1], idF[:], start=True,
                         stop=True, is_transpose=True)
                t.matmul(PS[3][0:1, 128:256], stat[:, 1:2], idF[:], start=True,
                         stop=True, is_transpose=True).then_inc(SEM["st"], 1)
            tq(tA)
            C["st"] += 1
            ST_A = C["st"]

            def vB(v, ST_A=ST_A):
                v.wait_ge(SEM["st"], ST_A)
                for widx in range(2):
                    v.tensor_reduce(stat[0:1, 4 + widx:5 + widx],
                                    PS[3][0:1, widx * 128:widx * 128 + 128],
                                    axis=AX.X, op=ADD)
                    v.tensor_scalar(stat[0:1, 4 + widx:5 + widx],
                                    stat[0:1, 4 + widx:5 + widx],
                                    1.0 / (DH * D), QEPS, op0=MUL,
                                    op1=AluOpType.max)
                    v.reciprocal(stat[0:1, 6 + widx:7 + widx],
                                 stat[0:1, 4 + widx:5 + widx])
                v.tensor_copy(stat[0:1, 8:10], stat[0:1, 6:8])
                v.tensor_copy(stat[0:1, 10:12], stat[0:1, 4:6])
                v.tensor_copy(ROWH[0:1, 0:4], stat[0:1, 8:12])
                v.tensor_tensor(ROWL[0:1, 0:4], stat[0:1, 8:12], ROWH[0:1, 0:4],
                                op=SUB) \
                    .then_inc(SEM["sv"], 1)
            vq(vB)
            C["sv"] += 1
            SV_B = C["sv"]

            def tB(t, SV_B=SV_B):
                t.wait_ge(SEM["sv"], SV_B)
                t.matmul(PS[3][:, 0:4], ONE2[0:1, 0:128], ROWH[0:1, 0:4],
                         start=True, stop=False)
                t.matmul(PS[3][:, 0:4], ONE2[0:1, 0:128], ROWL[0:1, 0:4],
                         start=False, stop=True).then_inc(SEM["st"], 1)
            tq(tB)
            C["st"] += 1
            ST_B = C["st"]

            def vC(v, ST_B=ST_B):
                v.wait_ge(SEM["st"], ST_B)
                v.tensor_copy(wiv[:, 0:4], PS[3][:, 0:4]).then_inc(SEM["sv"], 1)
            vq(vC)
            C["sv"] += 1
            FST["sv_wq_c"] = C["sv"]

            for widx in range(2):
                for hc in range(NHC):
                    buf = WFa if hc % 2 == 0 else WFb
                    idx = widx * NHC + hc
                    gate = C["sv"] - 1 if idx >= 2 else C["sv"]

                    wsem = "wa" if hc % 2 == 0 else "wb"

                    def yBq(y, buf=buf, widx=widx, hc=hc, gate=gate, wsem=wsem):
                        y.wait_ge(SEM["sv"], gate)
                        y.dma_start(out=buf[:, :], in_=w_unit_src(widx, hc)) \
                            .then_inc(SEM[wsem], 16)
                    yq(yBq)
                    C[wsem] += 16
                    si_here = C[wsem]
                    wq_gate = C["wq"]

                    def vQ(v, buf=buf, widx=widx, si_here=si_here,
                           wq_gate=wq_gate, wsem=wsem):
                        v.wait_ge(SEM[wsem], si_here)
                        v.wait_ge(SEM["wq"], wq_gate)
                        v.tensor_scalar(WI[:, :], buf[:, :],
                                        wiv[:, widx:widx + 1], None, op0=MUL)
                        v.tensor_scalar(WI[:, :], WI[:, :], 1, -1,
                                        op0=AluOpType.min, op1=AluOpType.max)
                        v.tensor_copy(WQ8[:, :], WI[:, :]).then_inc(SEM["sv"], 1)
                    vq(vQ)
                    C["sv"] += 1
                    sv_here = C["sv"]

                    def yS(y, widx=widx, hc=hc, sv_here=sv_here):
                        y.wait_ge(SEM["sv"], sv_here)
                        dst = (w1q_d if widx == 0
                               else w2q_d)[:, hc * 1024:(hc + 1) * 1024]
                        y.dma_start(out=dst, in_=WQ8[:, :]).then_inc(SEM["wq"], 16)
                    yq(yS)
                    C["wq"] += 16
            FST["so_wq"] = C["wq"]
            free("WFa", "WFb", "WQ8", "WAB")

        X0P = [mkalloc(f"X0P_{i}", [128, NCH], F32) for i in range(2)]
        X0PH = [mkalloc(f"X0PH_{i}", [128, NCH], BF16) for i in range(2)]
        X0PL = [mkalloc(f"X0PL_{i}", [128, NCH], BF16) for i in range(2)]
        NPC = S // NCH
        assert NPC % 4 == 0 or NPC <= 4

        def vWp(v):
            v.wait_ge(SEM["si"], SI_SMALL)
            v.tensor_copy(WPH[:, :], WP[:, :])
            v.tensor_tensor(WPL[:, :], WP[:, :], WPH[:, :], op=SUB) \
                .then_inc(SEM["sv"], 1)
        vq(vWp)
        C["sv"] += 1

        NRND = max(1, NPC // 4)
        RTC = min(4, NPC)
        sv_split = {}
        st_mm = {}
        for rnd in range(NRND):
            for tci in range(RTC):
                tc = rnd * RTC + tci
                for fc in range(8):
                    k = tc * 8 + fc
                    slot = k % 2
                    if tci == 0 and fc == 0:
                        waits = [("sv", sv_split.get(rnd - 1, C["sv"]))]
                    elif (k - 2) in st_mm:
                        waits = [("st", st_mm[k - 2])]
                    else:
                        waits = []

                    wsem = "wa" if slot == 0 else "wb"

                    def yPp(y, tc=tc, fc=fc, slot=slot, waits=tuple(waits), wsem=wsem):
                        for sem, val in waits:
                            y.wait_ge(SEM[sem], val)
                        y.dma_start(out=X0P[slot][:, :],
                                    in_=x0T_h[fc * 128:(fc + 1) * 128,
                                              tc * NCH:(tc + 1) * NCH]) \
                            .then_inc(SEM[wsem], 16)
                    yq(yPp)
                    C[wsem] += 16
                    si_p = C[wsem]

                    def vPp(v, slot=slot, si_p=si_p, waits=tuple(waits), wsem=wsem):
                        v.wait_ge(SEM[wsem], si_p)
                        for sem, val in waits:
                            v.wait_ge(SEM[sem], val)
                        v.tensor_copy(X0PH[slot][:, :], X0P[slot][:, :])
                        v.tensor_tensor(X0PL[slot][:, :], X0P[slot][:, :],
                                        X0PH[slot][:, :], op=SUB) \
                            .then_inc(SEM["sv"], 1)
                    vq(vPp)
                    C["sv"] += 1
                    sv_p = C["sv"]

                    def tPp(t, tci=tci, fc=fc, slot=slot, sv_p=sv_p):
                        t.wait_ge(SEM["sv"], sv_p)
                        if fc == 0 and tci == 0:
                            t.wait_ge(SEM["sm"], SM_G0)
                        lo = fc * 96
                        a1 = PS[tci // 2][0:64, (tci % 2) * 512:(tci % 2) * 512 + NCH]
                        a2 = PS[2 + tci // 2][0:32, (tci % 2) * 512:(tci % 2) * 512 + NCH]
                        st0 = (fc == 0)
                        sp = (fc == 7)
                        t.matmul(a1, WPH[:, lo:lo + 64], X0PH[slot][:, :],
                                 start=st0, stop=False)
                        t.matmul(a1, WPL[:, lo:lo + 64], X0PH[slot][:, :],
                                 start=False, stop=False)
                        mm1 = t.matmul(a1, WPH[:, lo:lo + 64], X0PL[slot][:, :],
                                       start=False, stop=sp)
                        t.matmul(a2, WPH[:, lo + 64:lo + 96], X0PH[slot][:, :],
                                 start=st0, stop=False)
                        t.matmul(a2, WPL[:, lo + 64:lo + 96], X0PH[slot][:, :],
                                 start=False, stop=False)
                        mm2 = t.matmul(a2, WPH[:, lo + 64:lo + 96], X0PL[slot][:, :],
                                       start=False, stop=sp)
                        mm2.then_inc(SEM["st"], 1)
                    tq(tPp)
                    C["st"] += 1
                    st_mm[k] = C["st"]
            st_rnd = C["st"]

            def vEvR(v, rnd=rnd, st_rnd=st_rnd):
                v.wait_ge(SEM["st"], st_rnd)
                for tci in range(RTC):
                    tc = rnd * RTC + tci
                    cols = slice(tc * NCH, (tc + 1) * NCH)
                    a1 = PS[tci // 2][0:64, (tci % 2) * 512:(tci % 2) * 512 + NCH]
                    a2 = PS[2 + tci // 2][0:32, (tci % 2) * 512:(tci % 2) * 512 + NCH]
                    v.tensor_scalar(K1[0:64, cols], a1, BKV[:, 0:1], None, op0=ADD)
                    v.scalar_tensor_tensor(K2[0:64, cols], a1, BKV[:, 0:1],
                                           K1[0:64, cols], op0=ADD, op1=SUB)
                    v.tensor_scalar(CQD[0:32, cols], a2, BQ[:, 0:1], None, op0=ADD)
                    v.scalar_tensor_tensor(CQL[0:32, cols], a2, BQ[:, 0:1],
                                           CQD[0:32, cols], op0=ADD, op1=SUB)
                v.tensor_copy(stat[:, 15:16], stat[:, 15:16]).then_inc(SEM["sv"], 1)
            vq(vEvR)
            C["sv"] += 1
            sv_split[rnd] = C["sv"]
        SV_STK = C["sv"]

        def yDup(y):
            y.wait_ge(SEM["sv"], SV_STK)
            y.dma_start(out=K1[64:128, :], in_=K1[0:64, :]).then_inc(SEM["sm"], 16)
            y.dma_start(out=CQD[64:97, :], in_=CQD[0:33, :]).then_inc(SEM["sm"], 16)
        yq(yDup)
        C["sm"] += 32
        SM_DUP = C["sm"]

        st_cst0 = C["st"]

        def tCst(t):
            pb = PS[3][:, 0:64].bitcast(BF16)
            for kt in range(NKT):
                t.wait_ge(SEM["sv"], SV_STK + kt)
                t.matmul(pb[:, 0:32], K1[0:32, kt * 128:(kt + 1) * 128],
                         idB[0:32, 0:32], start=True, stop=True, is_transpose=True)
                t.matmul(pb[:, 32:64], K2[0:32, kt * 128:(kt + 1) * 128],
                         idB[0:32, 0:32], start=True, stop=True, is_transpose=True) \
                    .then_inc(SEM["st"], 1)
        tq(tCst)
        C["st"] += NKT

        def vCst(v):
            pb = PS[3][:, 0:64].bitcast(BF16)
            for kt in range(NKT):
                v.wait_ge(SEM["st"], st_cst0 + kt + 1)
                v.tensor_copy(CST1[:, kt * 65 + 0:kt * 65 + 32], pb[:, 0:32])
                v.tensor_copy(CST1[:, kt * 65 + 32:kt * 65 + 64], pb[:, 32:64])
                v.tensor_copy(CST2[:, kt * 33 + 0:kt * 33 + 32], pb[:, 0:32]) \
                    .then_inc(SEM["sv"], 1)
            v.memset(CST1[:].rearrange("p (kt c) -> p kt c", c=65)[:, :, 64:65], 1.0)
            v.memset(CST2[:].rearrange("p (kt c) -> p kt c", c=33)[:, :, 32:33], 1.0) \
                .then_inc(SEM["sv"], 1)
        vq(vCst)
        C["sv"] += NKT + 1
        free("X0PL_1", "X0PL_0", "X0PH_1", "X0PH_0", "X0P_1", "X0P_0")

        MQH = mkalloc("MQH", [128, HEADS * 33], BF16)
        MQL = mkalloc("MQL", [128, HEADS * 33], BF16)
        MKH = mkalloc("MKH", [128, HEADS * 32], BF16)
        MKL = mkalloc("MKL", [128, HEADS * 32], BF16)
        MVH = mkalloc("MVH", [128, HEADS * 33], BF16)
        MVL = mkalloc("MVL", [128, HEADS * 33], BF16)
        QBF = mkalloc("QBF", [33, HEADS * 64], F32)
        GSF = mkalloc("GSF", [33, D], F32)
        GSH = mkalloc("GSH", [33, D], BF16)
        GSL = mkalloc("GSL", [33, D], BF16)
        BEF = mkalloc("BEF", [33, D], F32)

        def vMq(v):
            v.wait_ge(SEM["si"], SI_SMALL)
            v.tensor_copy(MQH[:, :], MQW[:, :])
            v.tensor_tensor(MQL[:, :], MQW[:, :], MQH[:, :], op=SUB)
            v.tensor_copy(MKH[:, :], MKW[:, :])
            v.tensor_tensor(MKL[:, :], MKW[:, :], MKH[:, :], op=SUB)
            v.tensor_copy(MVH[:, :], MVW[:, :])
            v.tensor_tensor(MVL[:, :], MVW[:, :], MVH[:, :], op=SUB) \
                .then_inc(SEM["sv"], 1)
        vq(vMq)
        C["sv"] += 1
        SV_MQ = C["sv"]

        st_mq0 = C["st"]

        def tMq(t):
            for h in range(HEADS):
                t.wait_ge(SEM["sv"], SV_MQ + h)
                a = PS[2][0:33, 0:32]
                t.matmul(a, MQH[:, h * 33:(h + 1) * 33], MKH[:, h * 32:(h + 1) * 32],
                         start=True, stop=False)
                t.matmul(a, MQL[:, h * 33:(h + 1) * 33], MKH[:, h * 32:(h + 1) * 32],
                         start=False, stop=False)
                t.matmul(a, MQH[:, h * 33:(h + 1) * 33], MKL[:, h * 32:(h + 1) * 32],
                         start=False, stop=True).then_inc(SEM["st"], 1)
        tq(tMq)
        C["st"] += HEADS

        def vMq2(v):
            for h in range(HEADS):
                v.wait_ge(SEM["st"], st_mq0 + h + 1)
                v.tensor_copy(QBF[:, h * 64:h * 64 + 32], PS[2][0:33, 0:32]) \
                    .then_inc(SEM["sv"], 1)
            for h in range(HEADS):
                v.tensor_copy(QBF[:, h * 64 + 32:h * 64 + 64],
                              QRW[:, h * DR:(h + 1) * DR])
            v.tensor_copy(QBH[0:33, :], QBF[:, :])
            v.tensor_tensor(QBL[0:33, :], QBF[:, :], QBH[0:33, :], op=SUB) \
                .then_inc(SEM["sv"], 1)
        vq(vMq2)
        C["sv"] += HEADS + 1
        SV_QB = C["sv"]

        def yQb(y):
            y.wait_ge(SEM["sv"], SV_QB)
            y.dma_start(out=QBH[64:97, :], in_=QBL[0:33, :]).then_inc(SEM["sm"], 16)
        yq(yQb)
        C["sm"] += 16

        for h in range(HEADS):
            gate = C["sm"]

            def yG(y, h=h, gate=gate):
                y.wait_ge(SEM["sm"], gate)
                y.dma_start(out=WOC[:, :], in_=WO_h[h * DK:(h + 1) * DK, :]) \
                    .then_inc(SEM["si"], 16)
            yq(yG)
            C["si"] += 16
            si_g = C["si"]

            def vGs(v, si_g=si_g, h=h):
                v.wait_ge(SEM["si"], si_g)
                v.tensor_copy(WOHs[:, :], WOC[:, :])
                v.tensor_tensor(WOLs[:, :], WOC[:, :], WOHs[:, :], op=SUB) \
                    .then_inc(SEM["sv"], 1)
            vq(vGs)
            C["sv"] += 1
            sv_g = C["sv"]
            st_g0 = C["st"]

            def tG(t, h=h, sv_g=sv_g):
                t.wait_ge(SEM["sv"], sv_g)
                for nck in range(2):
                    a = PS[2][0:33, nck * 512:(nck + 1) * 512]
                    t.matmul(a, MVH[:, h * 33:(h + 1) * 33],
                             WOHs[:, nck * 512:(nck + 1) * 512], start=True, stop=False)
                    t.matmul(a, MVL[:, h * 33:(h + 1) * 33],
                             WOHs[:, nck * 512:(nck + 1) * 512], start=False, stop=False)
                    t.matmul(a, MVH[:, h * 33:(h + 1) * 33],
                             WOLs[:, nck * 512:(nck + 1) * 512], start=False,
                             stop=True).then_inc(SEM["st"], 1)
            tq(tG)
            C["st"] += 2
            st_g1 = C["st"]

            def vGe(v, h=h, st_g1=st_g1, sm_gm=C["sm"]):
                v.wait_ge(SEM["st"], st_g1)
                v.wait_ge(SEM["sm"], sm_gm)
                v.tensor_copy(GSF[:, 0:512], PS[2][0:33, 0:512])
                v.tensor_copy(GSF[:, 512:1024], PS[2][0:33, 512:1024])
                if h == 0:
                    v.tensor_copy(BEF[32:33, :], GSF[32:33, :])
                else:
                    v.tensor_tensor(BEF[32:33, :], BEF[32:33, :], GSF[32:33, :],
                                    op=ADD)
                v.tensor_copy(GSH[0:32, :], GSF[0:32, :])
                v.tensor_tensor(GSL[0:32, :].bitcast(F32)[0:32, 0:512]
                                if False else GSL[0:32, :], GSF[0:32, :],
                                GSH[0:32, :], op=SUB) \
                    .then_inc(SEM["sv"], 1)
            vq(vGe)
            C["sv"] += 1
            sv_ge = C["sv"]

            def yGm(y, h=h, sv_ge=sv_ge):
                y.wait_ge(SEM["sv"], sv_ge)
                kc, j = h // 4, h % 4
                y.dma_start(out=GH[32 * j:32 * j + 32, kc * D:(kc + 1) * D],
                            in_=GSH[0:32, :]).then_inc(SEM["sm"], 16)
                y.dma_start(out=GL[32 * j:32 * j + 32, kc * D:(kc + 1) * D],
                            in_=GSL[0:32, :]).then_inc(SEM["sm"], 16)
            yq(yGm)
            C["sm"] += 32

        sv_all_ge = C["sv"]

        def yBo(y, sv_all_ge=sv_all_ge):
            y.wait_ge(SEM["sv"], sv_all_ge)
            y.dma_start(out=GSF[32:33, :], in_=bO_h[:, :]).then_inc(SEM["si"], 16)
        yq(yBo)
        C["si"] += 16
        SI_BO = C["si"]

        def vBe(v, sm_all=C["sm"]):
            v.wait_ge(SEM["si"], SI_BO)
            v.wait_ge(SEM["sm"], sm_all)
            v.tensor_tensor(BEF[32:33, :], BEF[32:33, :], GSF[32:33, :], op=ADD) \
                .then_inc(SEM["sv"], 1)
        vq(vBe)
        C["sv"] += 1
        SV_BEA = C["sv"]

        def yBm(y, SV_BEA=SV_BEA):
            y.wait_ge(SEM["sv"], SV_BEA)
            y.dma_start(out=BEF[0:1, :], in_=BEF[32:33, :]).then_inc(SEM["sm"], 16)
        yq(yBm)
        C["sm"] += 16
        SM_BEM = C["sm"]

        def vBe2(v, SM_BEM=SM_BEM):
            v.wait_ge(SEM["sm"], SM_BEM)
            v.tensor_copy(BE2[0:1, :], BEF[0:1, :])
            v.tensor_tensor(GSF[0:1, :], BEF[0:1, :], BE2[0:1, :], op=SUB)
            v.tensor_copy(GSH[0:1, :], GSF[0:1, :]) \
                .then_inc(SEM["sv"], 1)
        vq(vBe2)
        C["sv"] += 1
        SV_BE = C["sv"]

        def yBe(y):
            y.wait_ge(SEM["sv"], SV_BE)
            y.dma_start(out=BE2[1:2, :], in_=GSH[0:1, :]).then_inc(SEM["sm"], 16)
        yq(yBe)
        C["sm"] += 16
        free("MQW", "MKW", "MVW", "MQH", "MQL", "MKH", "MKL", "MVH", "MVL",
             "QBF", "GSF", "GSH", "GSL", "BEF", "WOHs", "WOLs", "QRW",
             "WP", "WPH", "WPL", "BQ", "BKV")

        emit_wq_alloc()

        def bcast_row(n, dst_evac, extra_scale_col=None, sa_gate=None):
            gate = C["sv"]

            rfh = ROWF[:].bitcast(BF16).rearrange("p (n two) -> p n two", two=2)

            def v_(v, n=n, gate=gate, sa_gate=sa_gate):
                v.wait_ge(SEM["sv"], gate)
                if sa_gate is not None:
                    v.wait_ge(SEM["sa"], sa_gate)
                if extra_scale_col is not None:
                    v.tensor_scalar(ROWF[0:1, 0:n], ROWF[0:1, 0:n],
                                    extra_scale_col, None, op0=MUL)
                v.tensor_tensor(ROWL[0:1, 0:n], ROWF[0:1, 0:n], rfh[0:1, 0:n, 1],
                                op=SUB) \
                    .then_inc(SEM["sv"], 1)
            vq(v_)
            C["sv"] += 1
            svh = C["sv"]
            nch = (n + 511) // 512
            st0 = C["st"]
            sv0 = C["sv"]

            def t_(t, n=n, svh=svh, nch=nch, sv0=sv0):
                t.wait_ge(SEM["sv"], svh)
                for ci in range(nch):
                    if ci >= 2:
                        t.wait_ge(SEM["sv"], sv0 + ci - 1)
                    ln = min(512, n - ci * 512)
                    half = PS[2][:, (ci % 2) * 512:(ci % 2) * 512 + 512]
                    t.matmul(half[:, 0:ln], ONE2[0:1, 0:128],
                             rfh[0:1, ci * 512:ci * 512 + ln, 1],
                             start=True, stop=False)
                    t.matmul(half[:, 0:ln], ONE2[0:1, 0:128],
                             ROWL[0:1, ci * 512:ci * 512 + ln],
                             start=False, stop=True).then_inc(SEM["st"], 1)
            tq(t_)
            C["st"] += nch

            def v2_(v, n=n, st0=st0, nch=nch):
                for ci in range(nch):
                    v.wait_ge(SEM["st"], st0 + ci + 1)
                    ln = min(512, n - ci * 512)
                    half = PS[2][:, (ci % 2) * 512:(ci % 2) * 512 + 512]
                    dst_evac(v, ci, half[:, 0:ln])
                    v.tensor_copy(stat[:, 15:16], stat[:, 15:16]) \
                        .then_inc(SEM["sv"], 1)
            vq(v2_)
            C["sv"] += nch

        def qbuild(h, slot, w, gate_sv=None):
            q0 = w * WIN
            if gate_sv is None:
                gate_sv = C["sv"]
            gate_sm = C["sm"]

            def t_(t, h=h, slot=slot, q0=q0, gate_sv=gate_sv, gate_sm=gate_sm):
                t.wait_ge(SEM["sv"], gate_sv)
                t.wait_ge(SEM["sm"], gate_sm)
                for ncx in range(NNC):
                    a = PS[3][0:64, ncx * NCH:(ncx + 1) * NCH]
                    cols = slice(q0 + ncx * NCH, q0 + (ncx + 1) * NCH)
                    t.matmul(a, QBH[:, h * 64:(h + 1) * 64], CQD[:, cols],
                             start=True, stop=False)
                    t.matmul(a, QBL[:, h * 64:(h + 1) * 64], CQL[:, cols],
                             start=False, stop=True).then_inc(SEM["st"], 1)
            tq(t_)
            C["st"] += NNC
            st_q = C["st"]

            def v_(v, slot=slot, st_q=st_q):
                v.wait_ge(SEM["st"], st_q)
                v.tensor_copy(Q2[slot][0:64, :], PS[3][0:64, 0:WIN])
                v.tensor_tensor(QST[0:64, :], PS[3][0:64, 0:WIN], Q2[slot][0:64, :],
                                op=SUB) \
                    .then_inc(SEM["sv"], 1)
            vq(v_)
            C["sv"] += 1
            sv_q = C["sv"]

            qk = "qa" if slot == 0 else "qb"

            def y_(y, slot=slot, sv_q=sv_q, qk=qk):
                y.wait_ge(SEM["sv"], sv_q)
                y.dma_start(out=Q2[slot][64:128, :], in_=QST[0:64, :]) \
                    .then_inc(SEM[qk], 16)
            yq(y_)
            C[qk] += 16
            return (qk, C[qk])

        def attn_window(w):
            q0 = w * WIN
            sm_q = qbuild(0, 0, w,
                          gate_sv=(FST["sv_vy_last"] if w >= 1 else None))
            sv_y_hist = []

            for h in range(HEADS):
                slot = h % 2
                plan = []
                for kt in range(NKT):
                    plan.append(("sc", kt))
                    if kt >= 1:
                        plan.append(("av", kt - 1))
                plan.append(("av", NKT - 1))
                st_base = C["st"]
                st_sc, st_av = {}, {}
                n = 0
                for op, kt in plan:
                    n += 1
                    (st_sc if op == "sc" else st_av)[kt] = st_base + n
                ss_base = C["ss"]
                ss_exp = {kt: ss_base + kt + 1 for kt in range(NKT)}
                sv_base = C["sv"]
                sv_sub = {kt: sv_base + kt + 1 for kt in range(NKT)}

                def t_loop(t, slot=slot, plan=tuple(plan), ss_exp=dict(ss_exp),
                           sv_sub=dict(sv_sub), sm_q=sm_q):
                    t.wait_ge(SEM[sm_q[0]], sm_q[1])
                    for op, kt in plan:
                        if op == "sc":
                            if kt >= 2:
                                t.wait_ge(SEM["ss"], ss_exp[kt - 2])
                            pss = PS[2 + (kt % 2)]
                            for ncx in range(NNC):
                                pc = slice(512 + ncx * NCH, 512 + (ncx + 1) * NCH)
                                t.matmul(pss[:, pc], K1[:, kt * 128:(kt + 1) * 128],
                                         Q2[slot][:, ncx * NCH:(ncx + 1) * NCH],
                                         start=True, stop=False)
                                mm = t.matmul(pss[:, pc],
                                              K2[:, kt * 128:(kt + 1) * 128],
                                              Q2[slot][0:64,
                                                       ncx * NCH:(ncx + 1) * NCH],
                                              start=False, stop=True)
                                if ncx == NNC - 1:
                                    mm.then_inc(SEM["st"], 1)
                        else:
                            t.wait_ge(SEM["sv"], sv_sub[kt])
                            phv = Pb[kt % 2][:].bitcast(BF16) \
                                .rearrange("p (n two) -> p n two", two=2)
                            for ncx in range(NNC):
                                pc = slice(ncx * NCH, (ncx + 1) * NCH)
                                t.matmul(PS[2][0:65, pc],
                                         CST1[:, kt * 65:(kt + 1) * 65],
                                         phv[:, pc, 1],
                                         start=(kt == 0), stop=(kt == NKT - 1))
                                mm = t.matmul(PS[3][0:33, pc],
                                              CST2[:, kt * 33:(kt + 1) * 33],
                                              Pl[kt % 2][:, pc],
                                              start=(kt == 0), stop=(kt == NKT - 1))
                                if ncx == NNC - 1:
                                    mm.then_inc(SEM["st"], 1)
                tq(t_loop)
                C["st"] += len(plan)

                def s_loop(s, st_sc=dict(st_sc), st_av=dict(st_av)):
                    for kt in range(NKT):
                        s.wait_ge(SEM["st"], st_sc[kt])
                        if kt >= 2:
                            s.wait_ge(SEM["st"], st_av[kt - 2])
                        s.activation(out=Pb[kt % 2][:, :],
                                     in_=PS[2 + (kt % 2)][:, 512:512 + WIN],
                                     func=AF.Exp, bias=cneg[:, 0:1],
                                     scale=SOFT_SCALE).then_inc(SEM["ss"], 1)
                sq(s_loop)
                C["ss"] += NKT

                def v_loop(v, ss_exp=dict(ss_exp)):
                    for kt in range(NKT):
                        v.wait_ge(SEM["ss"], ss_exp[kt])
                        phv = Pb[kt % 2][:].bitcast(BF16) \
                            .rearrange("p (n two) -> p n two", two=2)
                        v.tensor_tensor(Pl[kt % 2][:, :], Pb[kt % 2][:, :],
                                        phv[:, :, 1], op=SUB).then_inc(SEM["sv"], 1)
                vq(v_loop)
                C["sv"] += NKT

                st_av_fin = st_av[NKT - 1]

                def v_u1(v, st_av_fin=st_av_fin):
                    v.wait_ge(SEM["st"], st_av_fin)
                    v.tensor_copy(UEV[0:65, :], PS[2][0:65, 0:WIN]) \
                        .then_inc(SEM["sv"], 1)
                vq(v_u1)
                C["sv"] += 1
                sv_u1 = C["sv"]

                def y_u1(y, sv_u1=sv_u1):
                    y.wait_ge(SEM["sv"], sv_u1)
                    y.dma_start(out=UMV[0:33, :], in_=UEV[32:65, :]) \
                        .then_inc(SEM["sa"], 16)
                yq(y_u1)
                C["sa"] += 16
                sm_u1 = C["sa"]

                def v_u2(v, sm_u1=sm_u1):
                    v.wait_ge(SEM["sa"], sm_u1)
                    v.tensor_tensor(UST[0:32, :], UEV[0:32, :], UMV[0:32, :], op=ADD)
                    v.tensor_tensor(UST[0:32, :], UST[0:32, :], PS[3][0:32, 0:WIN],
                                    op=ADD)
                    v.tensor_tensor(UMV[32:33, :], UMV[32:33, :], PS[3][32:33, 0:WIN],
                                    op=ADD)
                    v.reciprocal(UMV[32:33, :], UMV[32:33, :]) \
                        .then_inc(SEM["sv"], 1)
                vq(v_u2)
                C["sv"] += 1
                sv_u2 = C["sv"]

                def y_u2(y, sv_u2=sv_u2):
                    y.wait_ge(SEM["sv"], sv_u2)
                    y.dma_start(out=ROWF[0:1, 0:WIN], in_=UMV[32:33, :]) \
                        .then_inc(SEM["sa"], 16)
                yq(y_u2)
                C["sa"] += 16

                def evac_r(v, ci, ps):
                    ln = ps.ap[-1][1] if False else min(512, WIN - ci * 512)
                    v.tensor_tensor(UST[0:32, ci * 512:ci * 512 + ln],
                                    UST[0:32, ci * 512:ci * 512 + ln],
                                    ps[0:32, :], op=MUL)
                bcast_row(WIN, evac_r, sa_gate=C["sa"])
                sv_u3 = C["sv"]

                def v_u4(v, sv_u3=sv_u3):
                    v.wait_ge(SEM["sv"], sv_u3)
                    v.tensor_copy(USH[0:32, :], UST[0:32, :])
                    v.tensor_tensor(UEV[0:32, :], UST[0:32, :], USH[0:32, :], op=SUB)
                    v.tensor_copy(USL[0:32, :], UEV[0:32, :]) \
                        .then_inc(SEM["sv"], 1)
                vq(v_u4)
                C["sv"] += 1
                sv_u4 = C["sv"]

                def y_u3(y, h=h, sv_u4=sv_u4):
                    y.wait_ge(SEM["sv"], sv_u4)
                    kc, j = h // 4, h % 4
                    y.dma_start(out=UallH[32 * j:32 * j + 32, kc * WIN:(kc + 1) * WIN],
                                in_=USH[0:32, :]).then_inc(SEM["sa"], 16)
                    y.dma_start(out=UallL[32 * j:32 * j + 32, kc * WIN:(kc + 1) * WIN],
                                in_=USL[0:32, :]).then_inc(SEM["sa"], 16)
                yq(y_u3)
                C["sa"] += 32

                if h + 1 < HEADS:
                    sm_q = qbuild(h + 1, (h + 1) % 2, w, gate_sv=sv_u2)
                if STAGE >= 5 and w >= 1 and NTF >= 2:
                    ngrp = NHC // 4
                    tot = (NTF - 1) * ngrp
                    per = -(-tot // HEADS)
                    for k_ in range(h * per, min((h + 1) * per, tot)):
                        ffn_th_chunk(w - 1, k_ // ngrp, groups=[k_ % ngrp])
                if w == 0:
                    emit_wq_scan_part(h, HEADS)

            sm_uall = C["sa"]

            if w == 0:
                emit_wq_tail()

            fslots = [[] for _ in range(WTT)]
            if STAGE >= 5 and w >= 1:
                items = []
                for th_ in range(NTF):
                    if th_ == NTF - 1:
                        items.append(("chunk", th_, 0))
                    for ttc_ in range(TFT):
                        items.append(("A", th_, ttc_))
                    for ttc_ in range(TFT):
                        items.append(("B", th_, ttc_))
                psl = -(-len(items) // WTT)
                for i_, it_ in enumerate(items):
                    fslots[min(i_ // psl, WTT - 1)].append(it_)

            def emit_fslot(si, wp=w - 1):
                for kind_, th_, ttc_ in fslots[si]:
                    if kind_ == "chunk":
                        ffn_th_chunk(wp, th_)
                    elif kind_ == "A":
                        ffn_ttc_A(wp, th_, ttc_)
                    else:
                        ffn_ttc_B(wp, th_, ttc_)

            sv_y = {}
            for tc in range(WTT):
                gate_x0 = sv_y.get(tc - 1, 0)

                def y_x0(y, tc=tc, q0=q0, gate_x0=gate_x0):
                    if gate_x0:
                        y.wait_ge(SEM["sv"], gate_x0)
                    y.dma_start(out=x0r[0][:, :],
                                in_=x0_h[q0 + tc * 128:q0 + (tc + 1) * 128, :]) \
                        .then_inc(SEM["xr"], 16)
                yq(y_x0)
                C["xr"] += 16
                si_x0 = C["xr"]
                gate_yt = sv_y.get(tc - 1, 0)

                wq_c = FST["sv_wq_c"] if w == 0 else 0

                def t_y(t, tc=tc, sm_uall=sm_uall, gate_yt=gate_yt, wq_c=wq_c):
                    t.wait_ge(SEM["sa"], sm_uall)
                    if wq_c:
                        t.wait_ge(SEM["sv"], wq_c)
                    if gate_yt:
                        t.wait_ge(SEM["sv"], gate_yt)
                    for nc2 in range(2):
                        a = PS[3][:, nc2 * 512:(nc2 + 1) * 512]
                        for kc in range(2):
                            u_sl = slice(kc * WIN + tc * 128, kc * WIN + (tc + 1) * 128)
                            g_sl = slice(kc * D + nc2 * 512, kc * D + (nc2 + 1) * 512)
                            t.matmul(a, UallH[:, u_sl], GH[:, g_sl],
                                     start=(kc == 0), stop=False)
                            t.matmul(a, UallL[:, u_sl], GH[:, g_sl],
                                     start=False, stop=False)
                            t.matmul(a, UallH[:, u_sl], GL[:, g_sl],
                                     start=False, stop=False)
                        t.matmul(a, ONE2[:, 0:128], BE2[:, nc2 * 512:(nc2 + 1) * 512],
                                 start=False, stop=True).then_inc(SEM["st"], 1)
                tq(t_y)
                C["st"] += 2
                st_y = C["st"]

                so_gate_y = C["so"]

                def v_y(v, tc=tc, w=w, st_y=st_y, si_x0=si_x0,
                        so_gate_y=so_gate_y):
                    v.wait_ge(SEM["st"], st_y)
                    v.wait_ge(SEM["xr"], si_x0)
                    v.wait_ge(SEM["so"], so_gate_y)
                    v.tensor_tensor(YT[:, 0:512], x0r[0][:, 0:512],
                                    PS[3][:, 0:512], op=ADD)
                    v.tensor_tensor(YT[:, 512:1024], x0r[0][:, 512:1024],
                                    PS[3][:, 512:1024], op=ADD)
                    v.bn_stats(bnb[:, 0:6], YT[:, 0:512])
                    v.bn_stats(bnb[:, 6:12], YT[:, 512:1024])
                    v.bn_aggr(mvb[:, 0:2], bnb[:, 0:12]) \
                        .then_inc(SEM["sv"], 1)
                vq(v_y)
                C["sv"] += 1
                sv_vy = C["sv"]
                FST["sv_vy_last"] = sv_vy

                def v_x1(v, tc=tc, w=w, sv_vy=sv_vy):
                    v.wait_ge(SEM["sv"], sv_vy)
                    v.tensor_scalar(stat[:, 2:3], mvb[:, 1:2], epsc[:, 0:1], None,
                                    op0=ADD)
                    s2i = stat[:, 2:3].bitcast(I32)
                    s1i = stat[:, 1:2].bitcast(I32)
                    v.tensor_scalar(s1i, s2i, 1, None,
                                    op0=AluOpType.arith_shift_right)
                    v.tensor_scalar(s1i, s1i, -1, 0x5F3759DF, op0=MUL, op1=ADD)
                    for _ in range(3):
                        v.tensor_tensor(stat[:, 3:4], stat[:, 1:2], stat[:, 1:2],
                                        op=MUL)
                        v.tensor_tensor(stat[:, 3:4], stat[:, 3:4], stat[:, 2:3],
                                        op=MUL)
                        v.tensor_scalar(stat[:, 3:4], stat[:, 3:4], -0.5, 1.5,
                                        op0=MUL, op1=ADD)
                        v.tensor_tensor(stat[:, 1:2], stat[:, 1:2], stat[:, 3:4],
                                        op=MUL)
                    v.tensor_scalar(x1[w % 2][:, tc * D:(tc + 1) * D], YT[:, :],
                                    mvb[:, 0:1], stat[:, 1:2], op0=SUB, op1=MUL) \
                        .then_inc(SEM["sv"], 1)
                vq(v_x1)
                C["sv"] += 1
                sv_y[tc] = C["sv"]
                if DEBUG_TAP:
                    def y_dbg(y, tc=tc, w=w, gate=sv_y[tc]):
                        y.wait_ge(SEM["sv"], gate)
                        y.dma_start(out=dbg_d[w * WIN + tc * 128:
                                              w * WIN + (tc + 1) * 128, :],
                                    in_=x1[w % 2][:, tc * D:(tc + 1) * D]) \
                            .then_inc(SEM["so"], 16)
                    yq(y_dbg)
                    C["so"] += 16
                if STAGE >= 5 and tc >= 1:
                    if tc == 1 and w >= 1:
                        emit_lhalo(w)
                    quant_tile(w, tc - 1, sv_y[tc - 1])
                    emit_fslot(tc - 1)
            if STAGE >= 5:
                quant_tile(w, WTT - 1, sv_y[WTT - 1])
                if w == NW - 1:
                    emit_zhalo(w)
                emit_fslot(WTT - 1)
            return C["sv"]

        FST = {"st_i1": {}, "sv_j": {}, "st_w2": {}, "ni1": 0, "nw2": 0,
               "sv_q": {}, "st_i1_w": {}, "sv_zhalo": 0, "sv_ps1": 0,
               "ss_sil": {}, "sv_xq2": {}, "st_w2_last": 0,
               "sv_wq_c": 0, "so_wq": 0, "sv_vy_last": 0}
        W1S = ["w1a", "w1b", "w1c", "w1d"]
        W2S = ["w2a", "w2b", "w2c", "w2d"]

        def quant_tile(w, tc, sv_gate):
            src = x1[w % 2][:, tc * D:(tc + 1) * D]
            st_gate = FST["st_i1_w"].get(w - 2, 0)

            def v_q(v, src=src, sv_gate=sv_gate):
                v.wait_ge(SEM["sv"], sv_gate)
                v.tensor_reduce(stat[:, 4:5], src, axis=AX.X, op=AluOpType.max,
                                apply_absolute_value=True)
                v.tensor_scalar(stat[:, 4:5], stat[:, 4:5], QEPS, None,
                                op0=AluOpType.max)
                v.reciprocal(stat[:, 5:6], stat[:, 4:5])
                v.tensor_scalar(stat[:, 5:6], stat[:, 5:6], 127.0, None, op0=MUL)
                v.tensor_scalar(IVX[:, 0:1], stat[:, 4:5], 1.0 / 127.0, None, op0=MUL)
                v.tensor_scalar(xqi[:, :], src, stat[:, 5:6], None, op0=MUL)
                v.tensor_copy(xqb[:, :], xqi[:, :]) \
                    .then_inc(SEM["sv"], 1)
            vq(v_q)
            C["sv"] += 1
            sv_qq = C["sv"]

            def t_tp(t, sv_qq=sv_qq):
                t.wait_ge(SEM["sv"], sv_qq)
                pb = PS[2][:, 0:512].bitcast(BF16)
                for fc in range(8):
                    t.matmul(pb[:, fc * 128:(fc + 1) * 128],
                             xqb[:, fc * 128:(fc + 1) * 128], idB[:],
                             start=True, stop=True, is_transpose=True) \
                        .then_inc(SEM["st"], 1)
                t.matmul(PS[2][0:1, 512:640], IVX[:, 0:1], idF[:], start=True,
                         stop=True, is_transpose=True).then_inc(SEM["st"], 1)
            tq(t_tp)
            C["st"] += 9
            st_tpe = C["st"]

            def v_tpe(v, w=w, tc=tc, st_tpe=st_tpe, st_gate=st_gate):
                v.wait_ge(SEM["st"], st_tpe)
                if st_gate:
                    v.wait_ge(SEM["st"], st_gate)
                pb = PS[2][:, 0:512].bitcast(BF16)
                buf = xqW[w % 2]
                r = None
                for fc in range(8):
                    base = fc * XQC
                    r = v.tensor_copy(
                        buf[:, base + 1 + tc * 128:base + 1 + (tc + 1) * 128],
                        pb[:, fc * 128:(fc + 1) * 128])
                    if tc == 0 and w >= 1:
                        r = v.tensor_copy(
                            xqW[(w - 1) % 2][:, base + WIN + 1:base + WIN + 2],
                            pb[:, fc * 128:fc * 128 + 1])
                r.then_inc(SEM["sv"], 1)
            vq(v_tpe)
            C["sv"] += 1

            def v_tpe2(v):
                v.tensor_copy(ROWF[0:1, 0:128], PS[2][0:1, 512:640]) \
                    .then_inc(SEM["sv"], 1)
            vq(v_tpe2)
            C["sv"] += 1

            def evac_iv(v, ci, ps, w=w, tc=tc):
                v.tensor_copy(INVSB[w % 2][:, 1 + tc * 128:1 + (tc + 1) * 128],
                              ps[:, 0:128])
                if tc == 0 and w >= 1:
                    v.tensor_copy(INVSB[(w - 1) % 2][:, WIN + 1:WIN + 2],
                                  ps[:, 0:1])
            bcast_row(128, evac_iv, extra_scale_col=wiv[0:1, 2:3])
            FST["sv_q"][(w, tc)] = C["sv"]
            return C["sv"]

        def emit_lhalo(w):
            st_gate = FST["st_i1_w"].get(w - 2, 0)

            def v_lh(v, w=w, st_gate=st_gate):
                if st_gate:
                    v.wait_ge(SEM["st"], st_gate)
                r = None
                for fc in range(8):
                    base = fc * XQC
                    r = v.tensor_copy(xqW[w % 2][:, base:base + 1],
                                      xqW[(w - 1) % 2][:, base + WIN:base + WIN + 1])
                r = v.tensor_copy(INVSB[w % 2][:, 0:1],
                                  INVSB[(w - 1) % 2][:, WIN:WIN + 1])
                r.then_inc(SEM["sv"], 1)
            vq(v_lh)
            C["sv"] += 1

        def emit_zhalo(w):
            st_gate = FST["st_i1_w"].get(w - 2, 0)

            def v_zh(v, w=w, st_gate=st_gate):
                if st_gate:
                    v.wait_ge(SEM["st"], st_gate)
                r = None
                for fc in range(8):
                    base = fc * XQC
                    r = v.memset(xqW[w % 2][:, base + WIN + 1:base + WIN + 2], 0.0)
                r = v.memset(INVSB[w % 2][:, WIN + 1:WIN + 2], 0.0)
                r.then_inc(SEM["sv"], 1)
            vq(v_zh)
            C["sv"] += 1
            FST["sv_zhalo"] = C["sv"]

        def ffn_th_chunk(w, th, groups=None):
            st_i1 = FST["st_i1"]
            sv_j = FST["sv_j"]
            if th == NTF - 1:
                gate_q = FST["sv_q"].get((w + 1, 0), FST["sv_zhalo"])
            else:
                gate_q = FST["sv_q"][(w, min(WTT - 1, (th * TF + TF) // 128))]
            st_w2_prev = FST["st_w2_last"]
            grange = (range(0, NHC, 4) if groups is None
                      else [g * 4 for g in groups])

            for g0_ in grange:
                base_ki = FST["ni1"]
                grp = []
                for gi_, hc in enumerate(range(g0_, g0_ + 4)):
                    ki1 = FST["ni1"]
                    FST["ni1"] += 1
                    gate_ld = st_i1.get(ki1 - 4, 0)
                    wsem = W1S[hc % 4]

                    def y_w1(y, hc=hc, gate_ld=gate_ld, wsem=wsem,
                             wqg=FST["so_wq"]):
                        y.wait_ge(SEM["wq"], wqg)
                        if gate_ld:
                            y.wait_ge(SEM["st"], gate_ld)
                        y.dma_start(out=w1r[hc % 4][:, :],
                                    in_=w1q_d[:, hc * 1024:(hc + 1) * 1024]) \
                            .then_inc(SEM[wsem], 16)
                    gq(y_w1)
                    C[wsem] += 16
                    grp.append((hc, ki1, C[wsem], wsem))

                svbase = C["sv"]

                def t_i1g(t, grp=tuple(grp), th=th, w=w, gate_q=gate_q,
                          base_ki=base_ki, svbase=svbase):
                    for gi_, (hc, ki1, si_w1, wsem) in enumerate(grp):
                        t.wait_ge(SEM[wsem], si_w1)
                        if gi_ == 0:
                            t.wait_ge(SEM["sv"], gate_q)
                        kim2 = ki1 - 2
                        if kim2 >= base_ki:
                            gate_j = svbase + (kim2 - base_ki) + 1
                        else:
                            gate_j = sv_j.get(kim2, 0)
                        if gate_j:
                            t.wait_ge(SEM["sv"], gate_j)
                        a = PS[0][:, (hc % 2) * 512:(hc % 2) * 512 + TF + 2]
                        for fc in range(8):
                            mm = t.matmul(
                                a, w1r[hc % 4][:, fc * 128:(fc + 1) * 128],
                                xqW[w % 2][:, fc * XQC + th * TF:
                                            fc * XQC + th * TF + TF + 2],
                                start=(fc == 0), stop=(fc == 7))
                        mm.then_inc(SEM["st"], 1)
                tq(t_i1g)
                for hc, ki1, _, _ in grp:
                    C["st"] += 1
                    st_i1[ki1] = C["st"]

                for hc, ki1, _, _ in grp:
                    st_here = st_i1[ki1]
                    ss_cva = C["ss"]

                    def v_j(v, hc=hc, th=th, w=w, st_here=st_here, ss_cva=ss_cva):
                        v.wait_ge(SEM["st"], st_here)
                        v.wait_ge(SEM["ss"], ss_cva)
                        a = PS[0][:, (hc % 2) * 512:(hc % 2) * 512 + TF + 2]
                        J = Jb[hc % 2]
                        v.tensor_tensor(J[:, :], a,
                                        INVSB[w % 2][:, th * TF:th * TF + TF + 2],
                                        op=MUL)
                        v.tensor_scalar(cvA[:, :], J[:, 1:TF + 1],
                                        convw[:, NHC + hc:NHC + hc + 1],
                                        convb[:, hc:hc + 1], op0=MUL, op1=ADD)
                        v.scalar_tensor_tensor(cvA[:, :], J[:, 0:TF],
                                               convw[:, hc:hc + 1], cvA[:, :],
                                               op0=MUL, op1=ADD)
                        v.scalar_tensor_tensor(cvA[:, :], J[:, 2:TF + 2],
                                               convw[:, 2 * NHC + hc:2 * NHC + hc + 1],
                                               cvA[:, :], op0=MUL, op1=ADD) \
                            .then_inc(SEM["sv"], 1)
                    vq(v_j)
                    C["sv"] += 1
                    sv_j[ki1] = C["sv"]
                    sv_cv = C["sv"]

                    def s_si(s, hc=hc, sv_cv=sv_cv, st_w2_prev=st_w2_prev):
                        s.wait_ge(SEM["sv"], sv_cv)
                        if st_w2_prev:
                            s.wait_ge(SEM["st"], st_w2_prev)
                        s.activation(out=hsil[:, hc * TF:(hc + 1) * TF], in_=cvA[:, :],
                                     func=AF.Silu) \
                            .then_inc(SEM["ss"], 1)
                    sq(s_si)
                    C["ss"] += 1
                FST["ss_sil"][(w, th)] = C["ss"]
                if th == NTF - 1:
                    FST["st_i1_w"][w] = st_i1[FST["ni1"] - 1]

        def ffn_ttc_A(w, th, ttc):
            ss_sil = FST["ss_sil"][(w, th)]
            if True:
                for ttc in [ttc]:
                    def v_am(v, ttc=ttc, ss_sil=ss_sil):
                        v.wait_ge(SEM["ss"], ss_sil)
                        h3 = hsil[:].rearrange("p (hc t) -> p t hc", hc=NHC)
                        v.tensor_reduce(amx[:, 0:128],
                                        h3[:, ttc * 128:(ttc + 1) * 128, :],
                                        axis=AX.X, op=AluOpType.max,
                                        apply_absolute_value=True) \
                            .then_inc(SEM["sv"], 1)
                    vq(v_am)
                    C["sv"] += 1
                    sv_am = C["sv"]

                    def t_am(t, sv_am=sv_am):
                        t.wait_ge(SEM["sv"], sv_am)
                        t.matmul(PS[2][:, 0:128], amx[:, 0:128], idF[:], start=True,
                                 stop=True, is_transpose=True).then_inc(SEM["st"], 1)
                    tq(t_am)
                    C["st"] += 1
                    st_am = C["st"]

                    def v_s2(v, ttc=ttc, st_am=st_am):
                        v.wait_ge(SEM["st"], st_am)
                        v.tensor_reduce(stat[:, 6:7], PS[2][:, 0:128], axis=AX.X,
                                        op=AluOpType.max)
                        v.tensor_scalar(stat[:, 6:7], stat[:, 6:7], QEPS, None,
                                        op0=AluOpType.max)
                        v.reciprocal(stat[:, 7:8], stat[:, 6:7])
                        v.tensor_scalar(stat[:, 7:8], stat[:, 7:8], 127.0, None,
                                        op0=MUL)
                        v.tensor_scalar(iv2c[:, ttc:ttc + 1], stat[:, 6:7],
                                        1.0 / 127.0, None, op0=MUL)
                        v.tensor_scalar(iv2c[:, ttc:ttc + 1], iv2c[:, ttc:ttc + 1],
                                        wiv[:, 3:4], None, op0=MUL) \
                            .then_inc(SEM["sv"], 1)
                    vq(v_s2)
                    C["sv"] += 1
                    sv_s2 = C["sv"]

                    def t_s2(t, sv_s2=sv_s2):
                        t.wait_ge(SEM["sv"], sv_s2)
                        t.matmul(PS[2][0:1, 0:128], stat[:, 7:8], idF[:], start=True,
                                 stop=True, is_transpose=True).then_inc(SEM["st"], 1)
                    tq(t_s2)
                    C["st"] += 1
                    st_s2 = C["st"]

                    def v_s2b(v, st_s2=st_s2):
                        v.wait_ge(SEM["st"], st_s2)
                        v.tensor_copy(ROWF[0:1, 0:128], PS[2][0:1, 0:128]) \
                            .then_inc(SEM["sv"], 1)
                    vq(v_s2b)
                    C["sv"] += 1

                    def evac_s2(v, ci, ps, ttc=ttc):
                        v.tensor_copy(s2b[:, ttc * 128:(ttc + 1) * 128], ps[:, 0:128])
                    bcast_row(128, evac_s2)
                    sv_s2bb = C["sv"]

                    def v_xq2(v, ttc=ttc, sv_s2bb=sv_s2bb):
                        v.wait_ge(SEM["sv"], sv_s2bb)
                        for hc in range(NHC):
                            cs = hc * TF + ttc * 128
                            v.tensor_tensor(xqi[:, 0:128], hsil[:, cs:cs + 128],
                                            s2b[:, ttc * 128:(ttc + 1) * 128], op=MUL)
                            v.tensor_copy(hsil[:, cs:cs + 128], xqi[:, 0:128])
                        v.tensor_copy(stat[:, 15:16], stat[:, 15:16]) \
                            .then_inc(SEM["sv"], 1)
                    vq(v_xq2)
                    C["sv"] += 1
                    FST["sv_xq2"][(w, th, ttc)] = C["sv"]

        def ffn_ttc_B(w, th, ttc):
            q0 = w * WIN
            XB = x1[w % 2]
            st_w2g = FST["st_w2"]
            if True:
                for ttc in [ttc]:
                    sv_xq2 = max(FST["sv_xq2"][(w, th, ttc)], FST["sv_ps1"])
                    for g0_ in range(0, NHC, 4):
                        grp = []
                        for hc in range(g0_, g0_ + 4):
                            kw2 = FST["nw2"]
                            FST["nw2"] += 1
                            gate_ld = st_w2g.get(kw2 - 4, 0)
                            wsem = W2S[hc % 4]

                            def y_w2(y, hc=hc, gate_ld=gate_ld, wsem=wsem,
                                     wqg=FST["so_wq"]):
                                y.wait_ge(SEM["wq"], wqg)
                                if gate_ld:
                                    y.wait_ge(SEM["st"], gate_ld)
                                y.dma_start(out=w2r[hc % 4][:, :],
                                            in_=w2q_d[:,
                                                      hc * 1024:(hc + 1) * 1024]) \
                                    .then_inc(SEM[wsem], 16)
                            yq(y_w2)
                            C[wsem] += 16
                            grp.append((hc, kw2, C[wsem], wsem))

                        def t_w2g(t, grp=tuple(grp), ttc=ttc, sv_xq2=sv_xq2):
                            for hc, kw2, si_w2, wsem in grp:
                                t.wait_ge(SEM[wsem], si_w2)
                                if hc == 0:
                                    t.wait_ge(SEM["sv"], sv_xq2)
                                xsl = hsil[:, hc * TF + ttc * 128:
                                           hc * TF + (ttc + 1) * 128]
                                for nc2 in range(2):
                                    mm = t.matmul(
                                        PS[1][:, nc2 * 512:(nc2 + 1) * 512], xsl,
                                        w2r[hc % 4][:, nc2 * 512:(nc2 + 1) * 512],
                                        start=(hc == 0), stop=(hc == NHC - 1))
                                mm.then_inc(SEM["st"], 1)
                        tq(t_w2g)
                        for hc, kw2, _, _ in grp:
                            C["st"] += 1
                            st_w2g[kw2] = C["st"]
                    st_w2fin = C["st"]
                    FST["st_w2_last"] = C["st"]

                    glob_t = th * TFT + ttc
                    so_gate = C["so"]

                    def v_y2(v, glob_t=glob_t, ttc=ttc, w=w, st_w2fin=st_w2fin,
                             so_gate=so_gate):
                        v.wait_ge(SEM["st"], st_w2fin)
                        v.wait_ge(SEM["so"], so_gate)
                        v.scalar_tensor_tensor(YT[:, 0:512], PS[1][:, 0:512],
                                               iv2c[:, ttc:ttc + 1],
                                               XB[:, glob_t * D:glob_t * D + 512],
                                               op0=MUL, op1=ADD)
                        v.scalar_tensor_tensor(YT[:, 512:1024], PS[1][:, 512:1024],
                                               iv2c[:, ttc:ttc + 1],
                                               XB[:, glob_t * D + 512:(glob_t + 1) * D],
                                               op0=MUL, op1=ADD)
                        v.bn_stats(bnb[:, 0:6], YT[:, 0:512])
                        v.bn_stats(bnb[:, 6:12], YT[:, 512:1024])
                        v.bn_aggr(mvb[:, 0:2], bnb[:, 0:12]) \
                            .then_inc(SEM["sv"], 1)
                    vq(v_y2)
                    C["sv"] += 1
                    sv_y2 = C["sv"]
                    FST["sv_ps1"] = C["sv"]

                    def v_x2(v, sv_y2=sv_y2):
                        v.wait_ge(SEM["sv"], sv_y2)
                        v.tensor_scalar(stat[:, 2:3], mvb[:, 1:2], epsc[:, 0:1],
                                        None, op0=ADD)
                        s2i = stat[:, 2:3].bitcast(I32)
                        s1i = stat[:, 1:2].bitcast(I32)
                        v.tensor_scalar(s1i, s2i, 1, None,
                                        op0=AluOpType.arith_shift_right)
                        v.tensor_scalar(s1i, s1i, -1, 0x5F3759DF, op0=MUL, op1=ADD)
                        for _ in range(3):
                            v.tensor_tensor(stat[:, 3:4], stat[:, 1:2], stat[:, 1:2],
                                            op=MUL)
                            v.tensor_tensor(stat[:, 3:4], stat[:, 3:4], stat[:, 2:3],
                                            op=MUL)
                            v.tensor_scalar(stat[:, 3:4], stat[:, 3:4], -0.5, 1.5,
                                            op0=MUL, op1=ADD)
                            v.tensor_tensor(stat[:, 1:2], stat[:, 1:2], stat[:, 3:4],
                                            op=MUL)
                        v.tensor_scalar(YT[:, :], YT[:, :], mvb[:, 0:1], stat[:, 1:2],
                                        op0=SUB, op1=MUL).then_inc(SEM["sv"], 1)
                    vq(v_x2)
                    C["sv"] += 1
                    sv_x2 = C["sv"]

                    def s_out(s, glob_t=glob_t, q0=q0, sv_x2=sv_x2):
                        s.wait_ge(SEM["sv"], sv_x2)
                        s.dma_start(out=out_h[q0 + glob_t * 128:
                                              q0 + (glob_t + 1) * 128, :],
                                    in_=YT[:, :]).then_inc(SEM["so"], 16)
                    sq(s_out)
                    C["so"] += 16

        def ffn_window_seq(w):
            for th in range(NTF):
                ffn_th_chunk(w, th)
                for ttc in range(TFT):
                    ffn_ttc_A(w, th, ttc)
                for ttc in range(TFT):
                    ffn_ttc_B(w, th, ttc)

        def _dbg(tag):
            if DEBUG_LEDGER:
                print(f"LEDGER {tag}: C_sv={C['sv']} "
                      f"actual={PTOT['v'].get(SEM['sv'], 0)} "
                      f"C_ss={C['ss']} actual_s={PTOT['s'].get(SEM['ss'], 0)}")

        sv_x1 = {}
        _dbg("startup")
        if STAGE >= 3:
            sv_x1[0] = attn_window(0)
            _dbg("attn0")
            if STAGE >= 4:
                for w in range(1, NW):
                    sv_x1[w] = attn_window(w)
                    _dbg(f"attn{w}")
                if STAGE >= 5:
                    ffn_window_seq(NW - 1)
                    _dbg(f"ffn{NW - 1}")

        SO_FIN = C["so"]

        def s_fin(s):
            s.wait_ge(SEM["so"], SO_FIN)
        sq(s_fin)

        for _k, _sem in SEM.items():
            _tv = PTOT["v"].get(_sem, 0)
            _ts = PTOT["s"].get(_sem, 0)
            if _tv or _ts:
                assert C[_k] == _tv + _ts, \
                    f"ledger mismatch {_k}: C={C[_k]} proxied={_tv + _ts}"

    return nc


_NC_CACHE = {}


def kernel(**inputs):
    S = inputs["L0_attn_h_t"].shape[1]
    B = inputs["L0_attn_h_t"].shape[0]
    WIN = 512 if S % 512 == 0 else 256
    key = (S, WIN)
    if key not in _NC_CACHE:
        _NC_CACHE[key] = build_nc(S=S, WIN=WIN, TF=256)
    nc = _NC_CACHE[key]

    f = np.float32
    asc = np.ascontiguousarray
    base = {
        "Wproj": asc(np.concatenate([inputs["L0_attn_W_DKV"], inputs["L0_attn_W_KR"],
                                     inputs["L0_attn_W_DQ"]], axis=1), dtype=f),
        "bkvr": asc(np.concatenate([np.asarray(inputs["L0_attn_b_DKV"]),
                                    np.zeros(32, f)])[:, None], dtype=f),
        "bq": asc(np.asarray(inputs["L0_attn_b_DQ"])[:, None], dtype=f),
        "WUQT": asc(np.asarray(inputs["L0_attn_W_UQ"]).T, dtype=f),
        "WUKT": asc(np.asarray(inputs["L0_attn_W_UK"]).T, dtype=f),
        "WUVT": asc(np.asarray(inputs["L0_attn_W_UV"]).T, dtype=f),
        "bUQ": asc(np.asarray(inputs["L0_attn_b_UQ"])[:, None], dtype=f),
        "bUV": asc(np.asarray(inputs["L0_attn_b_UV"])[:, None], dtype=f),
        "WQR": asc(np.asarray(inputs["L0_attn_W_QR"]), dtype=f),
        "bQR": asc(np.asarray(inputs["L0_attn_b_QR"])[None, :], dtype=f),
        "WO": asc(np.asarray(inputs["L0_attn_W_O"]), dtype=f),
        "bO": asc(np.asarray(inputs["L0_attn_b_O"])[None, :], dtype=f),
        "w1T": asc(np.asarray(inputs["L0_ffn_w1"]).T, dtype=f),
        "w2T": asc(np.asarray(inputs["L0_ffn_w2"]).T, dtype=f),
        "convw_pre": asc(np.asarray(inputs["L0_ffn_conv_w"]).reshape(NHC, 128, 3)
                         .transpose(1, 2, 0).reshape(128, 3 * NHC), dtype=f),
        "convb_pre": asc(np.asarray(inputs["L0_ffn_conv_b"]).reshape(NHC, 128).T,
                         dtype=f),
    }
    ht = np.asarray(inputs["L0_attn_h_t"], dtype=f)
    in_maps = []
    for b in range(B):
        m = dict(base)
        m["x0"] = asc(ht[b])
        m["x0T"] = asc(ht[b].T)
        in_maps.append(m)
    res = run_bass_kernel_spmd(nc, in_maps, core_ids=list(range(B)))
    return np.stack([np.asarray(res.results[b]["out"], dtype=np.float32)
                     for b in range(B)], axis=0)


# version 21 (final)
# speedup vs baseline: 1.0295x; 1.0046x over previous
import math
import contextlib
import numpy as np
import ml_dtypes
import concourse.bass as bass
import concourse.mybir as mybir
from concourse.alu_op_type import AluOpType
from concourse.bass_utils import run_bass_kernel_spmd

F32 = mybir.dt.float32
BF16 = mybir.dt.bfloat16
FP8 = mybir.dt.float8e4
I32 = mybir.dt.int32
AF = mybir.ActivationFunctionType
AX = mybir.AxisListType
SUB = AluOpType.subtract
ADD = AluOpType.add
MUL = AluOpType.mult

D = 1024
HEADS = 8
DC = 32
DR = 32
DK = 128
DH = 4096
NHC = DH // 128
SOFT_SCALE = 1.0 / math.sqrt(DK + DR)
CSHIFT = 30.0
EPS = 1e-5
QEPS = 1e-5
RMAGIC = 12582912.0


STAGE = 99
DEBUG_LEDGER = False
DEBUG_TAP = False


def build_nc(S=4096, WIN=1024, TF=128):
    NW = S // WIN
    NKT = S // 128
    WTT = WIN // 128
    NTF = WIN // TF
    TFT = TF // 128
    NCH = min(WIN, 512)
    NNC = WIN // NCH
    assert NW >= 2 and TF % 128 == 0 and TF + 2 <= 512 and WIN % TF == 0 \
        and (WIN % 512 == 0 or WIN == 256)

    nc = bass.Bass("TRN2", target_bir_lowering=False)

    x0_h = nc.dram_tensor("x0", [S, D], F32, kind="ExternalInput").ap()
    x0T_h = nc.dram_tensor("x0T", [D, S], F32, kind="ExternalInput").ap()
    Wproj_h = nc.dram_tensor("Wproj", [D, 96], F32, kind="ExternalInput").ap()
    bkvr_h = nc.dram_tensor("bkvr", [64, 1], F32, kind="ExternalInput").ap()
    bq_h = nc.dram_tensor("bq", [32, 1], F32, kind="ExternalInput").ap()
    WUQT_h = nc.dram_tensor("WUQT", [HEADS * DK, DC], F32, kind="ExternalInput").ap()
    WUKT_h = nc.dram_tensor("WUKT", [HEADS * DK, DC], F32, kind="ExternalInput").ap()
    WUVT_h = nc.dram_tensor("WUVT", [HEADS * DK, DC], F32, kind="ExternalInput").ap()
    bUQ_h = nc.dram_tensor("bUQ", [HEADS * DK, 1], F32, kind="ExternalInput").ap()
    bUV_h = nc.dram_tensor("bUV", [HEADS * DK, 1], F32, kind="ExternalInput").ap()
    WQR_h = nc.dram_tensor("WQR", [DC, HEADS * DR], F32, kind="ExternalInput").ap()
    bQR_h = nc.dram_tensor("bQR", [1, HEADS * DR], F32, kind="ExternalInput").ap()
    WO_h = nc.dram_tensor("WO", [HEADS * DK, D], F32, kind="ExternalInput").ap()
    bO_h = nc.dram_tensor("bO", [1, D], F32, kind="ExternalInput").ap()
    w1T_h = nc.dram_tensor("w1T", [D, DH], F32, kind="ExternalInput").ap()
    w2T_h = nc.dram_tensor("w2T", [DH, D], F32, kind="ExternalInput").ap()
    cw_h = nc.dram_tensor("convw_pre", [128, 3 * NHC], F32, kind="ExternalInput").ap()
    cb_h = nc.dram_tensor("convb_pre", [128, NHC], F32, kind="ExternalInput").ap()
    out_h = nc.dram_tensor("out", [S, D], F32, kind="ExternalOutput").ap()
    w1q_d = nc.dram_tensor("w1q_scr", [128, NHC * 1024], FP8, kind="Internal").ap()
    w2q_d = nc.dram_tensor("w2q_scr", [128, NHC * 1024], FP8, kind="Internal").ap()
    dbg_d = (nc.dram_tensor("dbg_x1", [S, 1024], F32, kind="Internal").ap()
             if DEBUG_TAP else None)

    C = {k: 0 for k in ["si", "sm", "st", "sv", "ss", "so",
                        "wa", "wb", "w1a", "w1b", "w1c", "w1d",
                        "w2a", "w2b", "w2c", "w2d", "xr",
                        "sb", "qa", "qb", "sa", "wq"]}
    es = contextlib.ExitStack()
    cms = {}
    astack = []

    def mkalloc(name, shape, dt):
        cm = nc.sbuf_tensor(name, shape, dt)
        t = cm.__enter__()
        cms[name] = cm
        astack.append(name)
        return t

    def free(*names):
        want = set(names)
        for n in reversed(list(names)):
            pass
        while want:
            top = astack.pop()
            assert top in want, f"non-LIFO free: {top} vs {want}"
            want.remove(top)
            cms[top].__exit__(None, None, None)

    def free_rest():
        while astack:
            cms[astack.pop()].__exit__(None, None, None)

    with es:
        es.callback(free_rest)
        SEM = {k: es.enter_context(nc.semaphore(k)) for k in C}

        x1 = [mkalloc(f"x1_{i}", [128, WTT * D], F32) for i in range(2)]
        K1 = mkalloc("K1", [128, S], BF16)
        K2 = mkalloc("K2", [64, S], BF16)
        CST1 = mkalloc("CST1", [128, NKT * 65], BF16)
        CST2 = mkalloc("CST2", [128, NKT * 33], BF16)
        CQD = mkalloc("CQD", [97, S], BF16)
        CQL = mkalloc("CQL", [33, S], BF16)
        QBH = mkalloc("QBH", [97, HEADS * 64], BF16)
        QBL = mkalloc("QBL", [33, HEADS * 64], BF16)
        GH = mkalloc("GH", [128, 2 * D], BF16)
        GL = mkalloc("GL", [128, 2 * D], BF16)
        BE2 = mkalloc("BE2", [2, D], BF16)
        ONE2 = mkalloc("ONE2", [2, 128], BF16)
        Pb = [mkalloc(f"P_{i}", [128, WIN], F32) for i in range(2)]
        Pl = [mkalloc(f"Pl_{i}", [128, WIN], BF16) for i in range(2)]
        Q2 = [mkalloc(f"Q2_{i}", [128, WIN], BF16) for i in range(2)]
        QST = mkalloc("QST", [64, WIN], BF16)
        UallH = mkalloc("UallH", [128, 2 * WIN], BF16)
        UallL = mkalloc("UallL", [128, 2 * WIN], BF16)
        UEV = mkalloc("UEV", [65, WIN], F32)
        UMV = mkalloc("UMV", [33, WIN], F32)
        UST = mkalloc("UST", [32, WIN], F32)
        USH = mkalloc("USH", [32, WIN], BF16)
        USL = mkalloc("USL", [32, WIN], BF16)
        ROWF = mkalloc("ROWF", [1, WIN + 128], F32)
        ROWH = mkalloc("ROWH", [1, WIN + 128], BF16)
        ROWL = mkalloc("ROWL", [1, WIN + 128], BF16)
        x0r = [mkalloc(f"x0r_{i}", [128, D], F32) for i in range(1)]
        YT = mkalloc("YT", [128, D], F32)
        INVSB = [mkalloc(f"INVSB_{i}", [128, WIN + 2], F32) for i in range(2)]
        IVX = mkalloc("IVX", [128, 1], F32)
        XQC = WIN + 2
        xqW = [mkalloc(f"xqW_{i}", [128, 8 * XQC], BF16) for i in range(2)]
        hsil = mkalloc("hsil", [128, NHC * TF], BF16)
        cvA = mkalloc("cvA", [128, TF], F32)
        amx = mkalloc("amx", [128, 128], F32)
        s2b = mkalloc("s2b", [128, TF], F32)
        iv2c = mkalloc("iv2c", [128, TFT], F32)
        w1r = [mkalloc(f"w1r_{i}", [128, 1024], FP8) for i in range(4)]
        w2r = [mkalloc(f"w2r_{i}", [128, 1024], FP8) for i in range(4)]
        Jb = [mkalloc(f"J_{i}", [128, TF + 2], F32) for i in range(2)]
        idB = mkalloc("idB", [128, 128], BF16)
        idF = mkalloc("idF", [128, 128], F32)
        cneg = mkalloc("cneg", [128, 1], F32)
        epsc = mkalloc("epsc", [128, 1], F32)
        convb = mkalloc("convb", [128, NHC], F32)
        convw = mkalloc("convw", [128, 3 * NHC], F32)
        stat = mkalloc("stat", [128, 16], F32)
        bnb = mkalloc("bnb", [128, 12], F32)
        mvb = mkalloc("mvb", [128, 2], F32)
        xqi = mkalloc("xqi", [128, D], I32)
        xqb = mkalloc("xqb", [128, D], BF16)
        wiv = mkalloc("wiv", [128, 4], F32)
        rs32 = mkalloc("rs32", [33, 128], F32)
        fen = mkalloc("fen", [1, 8], F32)

        PS = [es.enter_context(nc.psum_tensor(f"ps{i}", [128, 1024], F32))
              for i in range(4)]

        blk = es.enter_context(nc.Block())

        _VOPS = {"tensor_copy", "tensor_tensor", "tensor_scalar",
                 "scalar_tensor_tensor", "tensor_reduce", "reciprocal", "memset",
                 "bn_stats", "bn_aggr", "activation"}

        QC = {"v": 0, "s": 0}
        QS = {"v": es.enter_context(nc.semaphore("qv")),
              "s": es.enter_context(nc.semaphore("qs"))}
        PTOT = {"v": {}, "s": {}}
        PEND = {"v": {}, "s": {}}

        def _drain_pend(eng, k):
            for sm_, vv in PEND[k].items():
                eng.wait_ge(sm_, vv)
            PEND[k].clear()

        class _Proxy:
            def __init__(self, eng, key):
                self._e = eng
                self._k = key

            def then_inc(self, sem, amt):
                k = self._k
                _drain_pend(self._e, k)
                self._e.wait_ge(QS[k], QC[k])
                r = self._e.nop().then_inc(sem, amt)
                PTOT[k][sem] = PTOT[k].get(sem, 0) + amt
                PEND[k][sem] = PTOT[k][sem]
                return r

        class _Serial:
            def __init__(self, eng, key):
                self._e = eng
                self._k = key

            def __getattr__(self, name):
                a = getattr(self._e, name)
                if name in _VOPS:
                    def call(*args, _a=a, **kw):
                        _drain_pend(self._e, self._k)
                        if QC[self._k] > 0:
                            self._e.wait_ge(QS[self._k], QC[self._k])
                        r = _a(*args, **kw)
                        r.then_inc(QS[self._k], 1)
                        QC[self._k] += 1
                        return _Proxy(self._e, self._k)
                    return call
                return a

        def vq(f):
            blk.vector(lambda v: f(_Serial(v, "v")))

        def sq(f):
            blk.scalar(lambda s2_: f(_Serial(s2_, "s")))

        yq, tq, gq = blk.sync, blk.tensor, blk.gpsimd


        def g0(g):
            g.memset(idF[:], 0.0).then_inc(SEM["so"], 1)
            g.memset(idB[:], 0.0).then_inc(SEM["so"], 1)
            g.wait_ge(SEM["so"], 2)
            g.affine_select(out=idF[:], in_=idF[:], compare_op=AluOpType.not_equal,
                            fill=1.0, base=0, pattern=[[-1, 128]],
                            channel_multiplier=1).then_inc(SEM["so"], 1)
            g.affine_select(out=idB[:], in_=idB[:], compare_op=AluOpType.not_equal,
                            fill=1.0, base=0, pattern=[[-1, 128]],
                            channel_multiplier=1).then_inc(SEM["so"], 1)
            g.memset(cneg[:], -CSHIFT).then_inc(SEM["so"], 1)
            g.memset(epsc[:], EPS).then_inc(SEM["so"], 1)
            g.memset(ONE2[:], 1.0).then_inc(SEM["so"], 1)
            g.memset(CST1[:], 0.0).then_inc(SEM["so"], 1)
            g.memset(CST2[:], 0.0).then_inc(SEM["so"], 1)
            g.memset(xqW[0][:], 0.0).then_inc(SEM["so"], 1)
            g.memset(xqW[1][:], 0.0).then_inc(SEM["so"], 1)
            g.memset(INVSB[0][:], 0.0).then_inc(SEM["so"], 1)
            g.memset(INVSB[1][:], 0.0).then_inc(SEM["so"], 1)
            g.memset(CQL[:, :], 0.0).then_inc(SEM["so"], 1)
            g.memset(stat[:], 0.0).then_inc(SEM["so"], 1)
            g.wait_ge(SEM["so"], 15)
            g.nop().then_inc(SEM["sm"], 1)
        gq(g0)
        C["so"] += 15
        C["sm"] += 1
        SM_G0 = C["sm"]

        def v0(v):
            v.memset(CQD[32:64, :], 0.0).then_inc(SEM["sv"], 1)
            v.memset(QBH[32:64, :], 0.0)
            v.wait_ge(SEM["sv"], C["sv"] + 1)
            v.memset(CQD[32:33, :], 1.0) \
                .then_inc(SEM["sv"], 1)
        vq(v0)
        C["sv"] += 2

        WP = mkalloc("WP", [128, 8 * 96], F32)
        WPH = mkalloc("WPH", [128, 8 * 96], BF16)
        WPL = mkalloc("WPL", [128, 8 * 96], BF16)
        BKV = mkalloc("BKV", [64, 1], F32)
        BQ = mkalloc("BQ", [32, 1], F32)
        MQW = mkalloc("MQW", [128, HEADS * 33], F32)
        MKW = mkalloc("MKW", [128, HEADS * 32], F32)
        MVW = mkalloc("MVW", [128, HEADS * 33], F32)
        WOC = YT
        WOHs = mkalloc("WOHs", [128, D], BF16)
        WOLs = mkalloc("WOLs", [128, D], BF16)
        QRW = mkalloc("QRW", [33, HEADS * DR], F32)

        def y1(y):
            n = 0
            for fc in range(8):
                y.dma_start(out=WP[:, fc * 96:fc * 96 + 96],
                            in_=Wproj_h[fc * 128:(fc + 1) * 128, :]).then_inc(SEM["si"], 16)
                n += 1
            y.dma_start(out=BKV[:, :], in_=bkvr_h[:, :]).then_inc(SEM["si"], 16)
            y.dma_start(out=BQ[:, :], in_=bq_h[:, :]).then_inc(SEM["si"], 16)
            n += 2
            for h in range(HEADS):
                y.dma_start(out=MQW[:, h * 33:h * 33 + 32],
                            in_=WUQT_h[h * DK:(h + 1) * DK, :]).then_inc(SEM["si"], 16)
                y.dma_start(out=MQW[:, h * 33 + 32:h * 33 + 33],
                            in_=bUQ_h[h * DK:(h + 1) * DK, :]).then_inc(SEM["si"], 16)
                y.dma_start(out=MKW[:, h * 32:(h + 1) * 32],
                            in_=WUKT_h[h * DK:(h + 1) * DK, :]).then_inc(SEM["si"], 16)
                y.dma_start(out=MVW[:, h * 33:h * 33 + 32],
                            in_=WUVT_h[h * DK:(h + 1) * DK, :]).then_inc(SEM["si"], 16)
                y.dma_start(out=MVW[:, h * 33 + 32:h * 33 + 33],
                            in_=bUV_h[h * DK:(h + 1) * DK, :]).then_inc(SEM["si"], 16)
                n += 5
            y.dma_start(out=QRW[0:32, :], in_=WQR_h[:, :]).then_inc(SEM["si"], 16)
            y.dma_start(out=QRW[32:33, :], in_=bQR_h[:, :]).then_inc(SEM["si"], 16)
            n += 2
            y.dma_start(out=convb[:, :], in_=cb_h[:, :]).then_inc(SEM["si"], 16)
            y.dma_start(out=convw[:, :], in_=cw_h[:, :]).then_inc(SEM["si"], 16)
            n += 2
            C["si"] += 16 * n
        yq(y1)
        SI_SMALL = C["si"]

        def w_unit_src(widx, hc):
            if widx == 0:
                return w1T_h[:, hc * 128:(hc + 1) * 128] \
                    .rearrange("(fc p) c -> p fc c", p=128)
            return w2T_h[hc * 128:(hc + 1) * 128, :]

        WQB = {}

        def emit_wq_alloc():
            WQB["WFa"] = mkalloc("WFa", [128, 1024], F32)
            WQB["WFb"] = mkalloc("WFb", [128, 1024], F32)
            WQB["WQ8"] = mkalloc("WQ8", [128, 1024], FP8)
            WQB["WAB"] = mkalloc("WAB", [128, 2 * NHC], F32)
            WQB["sv0"] = C["sv"]
            WQB["sm0"] = C["sm"]
            WQB["si0"] = C["si"]

        def emit_wq_scan_part(part, nparts):
            WFa, WFb, WAB = WQB["WFa"], WQB["WFb"], WQB["WAB"]
            per = (2 * NHC) // nparts
            for idx in range(part * per, (part + 1) * per):
                widx, hc = divmod(idx, NHC)
                buf = WFa if hc % 2 == 0 else WFb
                gate = max(0, C["sv"] - 1)

                wsem = "wa" if hc % 2 == 0 else "wb"

                def yA(y, buf=buf, widx=widx, hc=hc, gate=gate, idx=idx,
                       wsem=wsem, g0=dict(WQB)):
                    if idx >= 2:
                        y.wait_ge(SEM["sv"], gate)
                    else:
                        y.wait_ge(SEM["sv"], g0["sv0"])
                        y.wait_ge(SEM["sm"], g0["sm0"])
                        y.wait_ge(SEM["si"], g0["si0"])
                    y.dma_start(out=buf[:, :], in_=w_unit_src(widx, hc)) \
                        .then_inc(SEM[wsem], 16)
                yq(yA)
                C[wsem] += 16
                si_here = C[wsem]

                def vAb(v, buf=buf, col=WAB[:, idx:idx + 1], si_here=si_here,
                        wsem=wsem):
                    v.wait_ge(SEM[wsem], si_here)
                    v.tensor_reduce(col, buf[:, :], axis=AX.X, op=ADD,
                                    apply_absolute_value=True) \
                        .then_inc(SEM["sv"], 1)
                vq(vAb)
                C["sv"] += 1
            WQB["SV_ABS"] = C["sv"]

        def emit_wq_tail():
            WFa, WFb = WQB["WFa"], WQB["WFb"]
            WQ8, WAB = WQB["WQ8"], WQB["WAB"]
            WI = xqi
            SV_ABS = WQB["SV_ABS"]

            def vA(v):
                v.wait_ge(SEM["sv"], SV_ABS)
                for widx in range(2):
                    v.tensor_reduce(stat[:, widx:widx + 1],
                                    WAB[:, widx * NHC:(widx + 1) * NHC],
                                    axis=AX.X, op=ADD)
                v.tensor_copy(stat[:, 15:16], stat[:, 15:16]).then_inc(SEM["sv"], 1)
            vq(vA)
            C["sv"] += 1
            SV_A = C["sv"]

            def tA(t, SV_A=SV_A):
                t.wait_ge(SEM["sv"], SV_A)
                t.matmul(PS[3][0:1, 0:128], stat[:, 0:1], idF[:], start=True,
                         stop=True, is_transpose=True)
                t.matmul(PS[3][0:1, 128:256], stat[:, 1:2], idF[:], start=True,
                         stop=True, is_transpose=True).then_inc(SEM["st"], 1)
            tq(tA)
            C["st"] += 1
            ST_A = C["st"]

            def vB(v, ST_A=ST_A):
                v.wait_ge(SEM["st"], ST_A)
                for widx in range(2):
                    v.tensor_reduce(stat[0:1, 4 + widx:5 + widx],
                                    PS[3][0:1, widx * 128:widx * 128 + 128],
                                    axis=AX.X, op=ADD)
                    v.tensor_scalar(stat[0:1, 4 + widx:5 + widx],
                                    stat[0:1, 4 + widx:5 + widx],
                                    1.0 / (DH * D), QEPS, op0=MUL,
                                    op1=AluOpType.max)
                    v.reciprocal(stat[0:1, 6 + widx:7 + widx],
                                 stat[0:1, 4 + widx:5 + widx])
                v.tensor_copy(stat[0:1, 8:10], stat[0:1, 6:8])
                v.tensor_copy(stat[0:1, 10:12], stat[0:1, 4:6])
                v.tensor_copy(ROWH[0:1, 0:4], stat[0:1, 8:12])
                v.tensor_tensor(ROWL[0:1, 0:4], stat[0:1, 8:12], ROWH[0:1, 0:4],
                                op=SUB) \
                    .then_inc(SEM["sv"], 1)
            vq(vB)
            C["sv"] += 1
            SV_B = C["sv"]

            def tB(t, SV_B=SV_B):
                t.wait_ge(SEM["sv"], SV_B)
                t.matmul(PS[3][:, 0:4], ONE2[0:1, 0:128], ROWH[0:1, 0:4],
                         start=True, stop=False)
                t.matmul(PS[3][:, 0:4], ONE2[0:1, 0:128], ROWL[0:1, 0:4],
                         start=False, stop=True).then_inc(SEM["st"], 1)
            tq(tB)
            C["st"] += 1
            ST_B = C["st"]

            def vC(v, ST_B=ST_B):
                v.wait_ge(SEM["st"], ST_B)
                v.tensor_copy(wiv[:, 0:4], PS[3][:, 0:4]).then_inc(SEM["sv"], 1)
            vq(vC)
            C["sv"] += 1
            FST["sv_wq_c"] = C["sv"]

            for widx in range(2):
                for hc in range(NHC):
                    buf = WFa if hc % 2 == 0 else WFb
                    idx = widx * NHC + hc
                    gate = C["sv"] - 1 if idx >= 2 else C["sv"]

                    wsem = "wa" if hc % 2 == 0 else "wb"

                    def yBq(y, buf=buf, widx=widx, hc=hc, gate=gate, wsem=wsem):
                        y.wait_ge(SEM["sv"], gate)
                        y.dma_start(out=buf[:, :], in_=w_unit_src(widx, hc)) \
                            .then_inc(SEM[wsem], 16)
                    yq(yBq)
                    C[wsem] += 16
                    si_here = C[wsem]
                    wq_gate = C["wq"]

                    def vQ(v, buf=buf, widx=widx, si_here=si_here,
                           wq_gate=wq_gate, wsem=wsem):
                        v.wait_ge(SEM[wsem], si_here)
                        v.wait_ge(SEM["wq"], wq_gate)
                        v.tensor_scalar(WI[:, :], buf[:, :],
                                        wiv[:, widx:widx + 1], None, op0=MUL)
                        v.tensor_scalar(WI[:, :], WI[:, :], 1, -1,
                                        op0=AluOpType.min, op1=AluOpType.max)
                        v.tensor_copy(WQ8[:, :], WI[:, :]).then_inc(SEM["sv"], 1)
                    vq(vQ)
                    C["sv"] += 1
                    sv_here = C["sv"]

                    def yS(y, widx=widx, hc=hc, sv_here=sv_here):
                        y.wait_ge(SEM["sv"], sv_here)
                        dst = (w1q_d if widx == 0
                               else w2q_d)[:, hc * 1024:(hc + 1) * 1024]
                        y.dma_start(out=dst, in_=WQ8[:, :]).then_inc(SEM["wq"], 16)
                    yq(yS)
                    C["wq"] += 16
            FST["so_wq"] = C["wq"]
            free("WFa", "WFb", "WQ8", "WAB")

        X0P = [mkalloc(f"X0P_{i}", [128, NCH], F32) for i in range(2)]
        X0PH = [mkalloc(f"X0PH_{i}", [128, NCH], BF16) for i in range(2)]
        X0PL = [mkalloc(f"X0PL_{i}", [128, NCH], BF16) for i in range(2)]
        NPC = S // NCH
        assert NPC % 4 == 0 or NPC <= 4

        def vWp(v):
            v.wait_ge(SEM["si"], SI_SMALL)
            v.tensor_copy(WPH[:, :], WP[:, :])
            v.tensor_tensor(WPL[:, :], WP[:, :], WPH[:, :], op=SUB) \
                .then_inc(SEM["sv"], 1)
        vq(vWp)
        C["sv"] += 1

        NRND = max(1, NPC // 4)
        RTC = min(4, NPC)
        sv_split = {}
        st_mm = {}
        for rnd in range(NRND):
            for tci in range(RTC):
                tc = rnd * RTC + tci
                for fc in range(8):
                    k = tc * 8 + fc
                    slot = k % 2
                    if tci == 0 and fc == 0:
                        waits = [("sv", sv_split.get(rnd - 1, C["sv"]))]
                    elif (k - 2) in st_mm:
                        waits = [("st", st_mm[k - 2])]
                    else:
                        waits = []

                    wsem = "wa" if slot == 0 else "wb"

                    def yPp(y, tc=tc, fc=fc, slot=slot, waits=tuple(waits), wsem=wsem):
                        for sem, val in waits:
                            y.wait_ge(SEM[sem], val)
                        y.dma_start(out=X0P[slot][:, :],
                                    in_=x0T_h[fc * 128:(fc + 1) * 128,
                                              tc * NCH:(tc + 1) * NCH]) \
                            .then_inc(SEM[wsem], 16)
                    yq(yPp)
                    C[wsem] += 16
                    si_p = C[wsem]

                    def vPp(v, slot=slot, si_p=si_p, waits=tuple(waits), wsem=wsem):
                        v.wait_ge(SEM[wsem], si_p)
                        for sem, val in waits:
                            v.wait_ge(SEM[sem], val)
                        v.tensor_copy(X0PH[slot][:, :], X0P[slot][:, :])
                        v.tensor_tensor(X0PL[slot][:, :], X0P[slot][:, :],
                                        X0PH[slot][:, :], op=SUB) \
                            .then_inc(SEM["sv"], 1)
                    vq(vPp)
                    C["sv"] += 1
                    sv_p = C["sv"]

                    def tPp(t, tci=tci, fc=fc, slot=slot, sv_p=sv_p):
                        t.wait_ge(SEM["sv"], sv_p)
                        if fc == 0 and tci == 0:
                            t.wait_ge(SEM["sm"], SM_G0)
                        lo = fc * 96
                        a1 = PS[tci // 2][0:64, (tci % 2) * 512:(tci % 2) * 512 + NCH]
                        a2 = PS[2 + tci // 2][0:32, (tci % 2) * 512:(tci % 2) * 512 + NCH]
                        st0 = (fc == 0)
                        sp = (fc == 7)
                        t.matmul(a1, WPH[:, lo:lo + 64], X0PH[slot][:, :],
                                 start=st0, stop=False)
                        t.matmul(a1, WPL[:, lo:lo + 64], X0PH[slot][:, :],
                                 start=False, stop=False)
                        mm1 = t.matmul(a1, WPH[:, lo:lo + 64], X0PL[slot][:, :],
                                       start=False, stop=sp)
                        t.matmul(a2, WPH[:, lo + 64:lo + 96], X0PH[slot][:, :],
                                 start=st0, stop=False)
                        t.matmul(a2, WPL[:, lo + 64:lo + 96], X0PH[slot][:, :],
                                 start=False, stop=False)
                        mm2 = t.matmul(a2, WPH[:, lo + 64:lo + 96], X0PL[slot][:, :],
                                       start=False, stop=sp)
                        mm2.then_inc(SEM["st"], 1)
                    tq(tPp)
                    C["st"] += 1
                    st_mm[k] = C["st"]
            st_rnd = C["st"]

            def vEvR(v, rnd=rnd, st_rnd=st_rnd):
                v.wait_ge(SEM["st"], st_rnd)
                for tci in range(RTC):
                    tc = rnd * RTC + tci
                    cols = slice(tc * NCH, (tc + 1) * NCH)
                    a1 = PS[tci // 2][0:64, (tci % 2) * 512:(tci % 2) * 512 + NCH]
                    a2 = PS[2 + tci // 2][0:32, (tci % 2) * 512:(tci % 2) * 512 + NCH]
                    v.tensor_scalar(K1[0:64, cols], a1, BKV[:, 0:1], None, op0=ADD)
                    v.scalar_tensor_tensor(K2[0:64, cols], a1, BKV[:, 0:1],
                                           K1[0:64, cols], op0=ADD, op1=SUB)
                    v.tensor_scalar(CQD[0:32, cols], a2, BQ[:, 0:1], None, op0=ADD)
                    v.scalar_tensor_tensor(CQL[0:32, cols], a2, BQ[:, 0:1],
                                           CQD[0:32, cols], op0=ADD, op1=SUB)
                v.tensor_copy(stat[:, 15:16], stat[:, 15:16]).then_inc(SEM["sv"], 1)
            vq(vEvR)
            C["sv"] += 1
            sv_split[rnd] = C["sv"]
        SV_STK = C["sv"]

        def yDup(y):
            y.wait_ge(SEM["sv"], SV_STK)
            y.dma_start(out=K1[64:128, :], in_=K1[0:64, :]).then_inc(SEM["sm"], 16)
            y.dma_start(out=CQD[64:97, :], in_=CQD[0:33, :]).then_inc(SEM["sm"], 16)
        yq(yDup)
        C["sm"] += 32
        SM_DUP = C["sm"]

        st_cst0 = C["st"]

        def tCst(t):
            pb = PS[3][:, 0:64].bitcast(BF16)
            for kt in range(NKT):
                t.wait_ge(SEM["sv"], SV_STK + kt)
                t.matmul(pb[:, 0:32], K1[0:32, kt * 128:(kt + 1) * 128],
                         idB[0:32, 0:32], start=True, stop=True, is_transpose=True)
                t.matmul(pb[:, 32:64], K2[0:32, kt * 128:(kt + 1) * 128],
                         idB[0:32, 0:32], start=True, stop=True, is_transpose=True) \
                    .then_inc(SEM["st"], 1)
        tq(tCst)
        C["st"] += NKT

        def vCst(v):
            pb = PS[3][:, 0:64].bitcast(BF16)
            for kt in range(NKT):
                v.wait_ge(SEM["st"], st_cst0 + kt + 1)
                v.tensor_copy(CST1[:, kt * 65 + 0:kt * 65 + 32], pb[:, 0:32])
                v.tensor_copy(CST1[:, kt * 65 + 32:kt * 65 + 64], pb[:, 32:64])
                v.tensor_copy(CST2[:, kt * 33 + 0:kt * 33 + 32], pb[:, 0:32]) \
                    .then_inc(SEM["sv"], 1)
            v.memset(CST1[:].rearrange("p (kt c) -> p kt c", c=65)[:, :, 64:65], 1.0)
            v.memset(CST2[:].rearrange("p (kt c) -> p kt c", c=33)[:, :, 32:33], 1.0) \
                .then_inc(SEM["sv"], 1)
        vq(vCst)
        C["sv"] += NKT + 1
        free("X0PL_1", "X0PL_0", "X0PH_1", "X0PH_0", "X0P_1", "X0P_0")

        MQH = mkalloc("MQH", [128, HEADS * 33], BF16)
        MQL = mkalloc("MQL", [128, HEADS * 33], BF16)
        MKH = mkalloc("MKH", [128, HEADS * 32], BF16)
        MKL = mkalloc("MKL", [128, HEADS * 32], BF16)
        MVH = mkalloc("MVH", [128, HEADS * 33], BF16)
        MVL = mkalloc("MVL", [128, HEADS * 33], BF16)
        QBF = mkalloc("QBF", [33, HEADS * 64], F32)
        GSF = mkalloc("GSF", [33, D], F32)
        GSH = mkalloc("GSH", [33, D], BF16)
        GSL = mkalloc("GSL", [33, D], BF16)
        BEF = mkalloc("BEF", [33, D], F32)

        def vMq(v):
            v.wait_ge(SEM["si"], SI_SMALL)
            v.tensor_copy(MQH[:, :], MQW[:, :])
            v.tensor_tensor(MQL[:, :], MQW[:, :], MQH[:, :], op=SUB)
            v.tensor_copy(MKH[:, :], MKW[:, :])
            v.tensor_tensor(MKL[:, :], MKW[:, :], MKH[:, :], op=SUB)
            v.tensor_copy(MVH[:, :], MVW[:, :])
            v.tensor_tensor(MVL[:, :], MVW[:, :], MVH[:, :], op=SUB) \
                .then_inc(SEM["sv"], 1)
        vq(vMq)
        C["sv"] += 1
        SV_MQ = C["sv"]

        st_mq0 = C["st"]

        def tMq(t):
            for h in range(HEADS):
                t.wait_ge(SEM["sv"], SV_MQ + h)
                a = PS[2][0:33, 0:32]
                t.matmul(a, MQH[:, h * 33:(h + 1) * 33], MKH[:, h * 32:(h + 1) * 32],
                         start=True, stop=False)
                t.matmul(a, MQL[:, h * 33:(h + 1) * 33], MKH[:, h * 32:(h + 1) * 32],
                         start=False, stop=False)
                t.matmul(a, MQH[:, h * 33:(h + 1) * 33], MKL[:, h * 32:(h + 1) * 32],
                         start=False, stop=True).then_inc(SEM["st"], 1)
        tq(tMq)
        C["st"] += HEADS

        def vMq2(v):
            for h in range(HEADS):
                v.wait_ge(SEM["st"], st_mq0 + h + 1)
                v.tensor_copy(QBF[:, h * 64:h * 64 + 32], PS[2][0:33, 0:32]) \
                    .then_inc(SEM["sv"], 1)
            for h in range(HEADS):
                v.tensor_copy(QBF[:, h * 64 + 32:h * 64 + 64],
                              QRW[:, h * DR:(h + 1) * DR])
            v.tensor_copy(QBH[0:33, :], QBF[:, :])
            v.tensor_tensor(QBL[0:33, :], QBF[:, :], QBH[0:33, :], op=SUB) \
                .then_inc(SEM["sv"], 1)
        vq(vMq2)
        C["sv"] += HEADS + 1
        SV_QB = C["sv"]

        def yQb(y):
            y.wait_ge(SEM["sv"], SV_QB)
            y.dma_start(out=QBH[64:97, :], in_=QBL[0:33, :]).then_inc(SEM["sm"], 16)
        yq(yQb)
        C["sm"] += 16

        for h in range(HEADS):
            gate = C["sm"]

            def yG(y, h=h, gate=gate):
                y.wait_ge(SEM["sm"], gate)
                y.dma_start(out=WOC[:, :], in_=WO_h[h * DK:(h + 1) * DK, :]) \
                    .then_inc(SEM["si"], 16)
            yq(yG)
            C["si"] += 16
            si_g = C["si"]

            def vGs(v, si_g=si_g, h=h):
                v.wait_ge(SEM["si"], si_g)
                v.tensor_copy(WOHs[:, :], WOC[:, :])
                v.tensor_tensor(WOLs[:, :], WOC[:, :], WOHs[:, :], op=SUB) \
                    .then_inc(SEM["sv"], 1)
            vq(vGs)
            C["sv"] += 1
            sv_g = C["sv"]
            st_g0 = C["st"]

            def tG(t, h=h, sv_g=sv_g):
                t.wait_ge(SEM["sv"], sv_g)
                for nck in range(2):
                    a = PS[2][0:33, nck * 512:(nck + 1) * 512]
                    t.matmul(a, MVH[:, h * 33:(h + 1) * 33],
                             WOHs[:, nck * 512:(nck + 1) * 512], start=True, stop=False)
                    t.matmul(a, MVL[:, h * 33:(h + 1) * 33],
                             WOHs[:, nck * 512:(nck + 1) * 512], start=False, stop=False)
                    t.matmul(a, MVH[:, h * 33:(h + 1) * 33],
                             WOLs[:, nck * 512:(nck + 1) * 512], start=False,
                             stop=True).then_inc(SEM["st"], 1)
            tq(tG)
            C["st"] += 2
            st_g1 = C["st"]

            def vGe(v, h=h, st_g1=st_g1, sm_gm=C["sm"]):
                v.wait_ge(SEM["st"], st_g1)
                v.wait_ge(SEM["sm"], sm_gm)
                v.tensor_copy(GSF[:, 0:512], PS[2][0:33, 0:512])
                v.tensor_copy(GSF[:, 512:1024], PS[2][0:33, 512:1024])
                if h == 0:
                    v.tensor_copy(BEF[32:33, :], GSF[32:33, :])
                else:
                    v.tensor_tensor(BEF[32:33, :], BEF[32:33, :], GSF[32:33, :],
                                    op=ADD)
                v.tensor_copy(GSH[0:32, :], GSF[0:32, :])
                v.tensor_tensor(GSL[0:32, :].bitcast(F32)[0:32, 0:512]
                                if False else GSL[0:32, :], GSF[0:32, :],
                                GSH[0:32, :], op=SUB) \
                    .then_inc(SEM["sv"], 1)
            vq(vGe)
            C["sv"] += 1
            sv_ge = C["sv"]

            def yGm(y, h=h, sv_ge=sv_ge):
                y.wait_ge(SEM["sv"], sv_ge)
                kc, j = h // 4, h % 4
                y.dma_start(out=GH[32 * j:32 * j + 32, kc * D:(kc + 1) * D],
                            in_=GSH[0:32, :]).then_inc(SEM["sm"], 16)
                y.dma_start(out=GL[32 * j:32 * j + 32, kc * D:(kc + 1) * D],
                            in_=GSL[0:32, :]).then_inc(SEM["sm"], 16)
            yq(yGm)
            C["sm"] += 32

        sv_all_ge = C["sv"]

        def yBo(y, sv_all_ge=sv_all_ge):
            y.wait_ge(SEM["sv"], sv_all_ge)
            y.dma_start(out=GSF[32:33, :], in_=bO_h[:, :]).then_inc(SEM["si"], 16)
        yq(yBo)
        C["si"] += 16
        SI_BO = C["si"]

        def vBe(v, sm_all=C["sm"]):
            v.wait_ge(SEM["si"], SI_BO)
            v.wait_ge(SEM["sm"], sm_all)
            v.tensor_tensor(BEF[32:33, :], BEF[32:33, :], GSF[32:33, :], op=ADD) \
                .then_inc(SEM["sv"], 1)
        vq(vBe)
        C["sv"] += 1
        SV_BEA = C["sv"]

        def yBm(y, SV_BEA=SV_BEA):
            y.wait_ge(SEM["sv"], SV_BEA)
            y.dma_start(out=BEF[0:1, :], in_=BEF[32:33, :]).then_inc(SEM["sm"], 16)
        yq(yBm)
        C["sm"] += 16
        SM_BEM = C["sm"]

        def vBe2(v, SM_BEM=SM_BEM):
            v.wait_ge(SEM["sm"], SM_BEM)
            v.tensor_copy(BE2[0:1, :], BEF[0:1, :])
            v.tensor_tensor(GSF[0:1, :], BEF[0:1, :], BE2[0:1, :], op=SUB)
            v.tensor_copy(GSH[0:1, :], GSF[0:1, :]) \
                .then_inc(SEM["sv"], 1)
        vq(vBe2)
        C["sv"] += 1
        SV_BE = C["sv"]

        def yBe(y):
            y.wait_ge(SEM["sv"], SV_BE)
            y.dma_start(out=BE2[1:2, :], in_=GSH[0:1, :]).then_inc(SEM["sm"], 16)
        yq(yBe)
        C["sm"] += 16
        free("MQW", "MKW", "MVW", "MQH", "MQL", "MKH", "MKL", "MVH", "MVL",
             "QBF", "GSF", "GSH", "GSL", "BEF", "WOHs", "WOLs", "QRW",
             "WP", "WPH", "WPL", "BQ", "BKV")

        emit_wq_alloc()

        def bcast_row(n, dst_evac, extra_scale_col=None, sa_gate=None):
            gate = C["sv"]

            rfh = ROWF[:].bitcast(BF16).rearrange("p (n two) -> p n two", two=2)

            def v_(v, n=n, gate=gate, sa_gate=sa_gate):
                v.wait_ge(SEM["sv"], gate)
                if sa_gate is not None:
                    v.wait_ge(SEM["sa"], sa_gate)
                if extra_scale_col is not None:
                    v.tensor_scalar(ROWF[0:1, 0:n], ROWF[0:1, 0:n],
                                    extra_scale_col, None, op0=MUL)
                v.tensor_tensor(ROWL[0:1, 0:n], ROWF[0:1, 0:n], rfh[0:1, 0:n, 1],
                                op=SUB) \
                    .then_inc(SEM["sv"], 1)
            vq(v_)
            C["sv"] += 1
            svh = C["sv"]
            nch = (n + 511) // 512
            st0 = C["st"]
            sv0 = C["sv"]

            def t_(t, n=n, svh=svh, nch=nch, sv0=sv0):
                t.wait_ge(SEM["sv"], svh)
                for ci in range(nch):
                    if ci >= 2:
                        t.wait_ge(SEM["sv"], sv0 + ci - 1)
                    ln = min(512, n - ci * 512)
                    half = PS[2][:, (ci % 2) * 512:(ci % 2) * 512 + 512]
                    t.matmul(half[:, 0:ln], ONE2[0:1, 0:128],
                             rfh[0:1, ci * 512:ci * 512 + ln, 1],
                             start=True, stop=False)
                    t.matmul(half[:, 0:ln], ONE2[0:1, 0:128],
                             ROWL[0:1, ci * 512:ci * 512 + ln],
                             start=False, stop=True).then_inc(SEM["st"], 1)
            tq(t_)
            C["st"] += nch

            def v2_(v, n=n, st0=st0, nch=nch):
                for ci in range(nch):
                    v.wait_ge(SEM["st"], st0 + ci + 1)
                    ln = min(512, n - ci * 512)
                    half = PS[2][:, (ci % 2) * 512:(ci % 2) * 512 + 512]
                    dst_evac(v, ci, half[:, 0:ln])
                    v.tensor_copy(stat[:, 15:16], stat[:, 15:16]) \
                        .then_inc(SEM["sv"], 1)
            vq(v2_)
            C["sv"] += nch

        def qbuild(h, slot, w, gate_sv=None):
            q0 = w * WIN
            if gate_sv is None:
                gate_sv = C["sv"]
            gate_sm = C["sm"]

            def t_(t, h=h, slot=slot, q0=q0, gate_sv=gate_sv, gate_sm=gate_sm):
                t.wait_ge(SEM["sv"], gate_sv)
                t.wait_ge(SEM["sm"], gate_sm)
                for ncx in range(NNC):
                    a = PS[3][0:64, ncx * NCH:(ncx + 1) * NCH]
                    cols = slice(q0 + ncx * NCH, q0 + (ncx + 1) * NCH)
                    t.matmul(a, QBH[:, h * 64:(h + 1) * 64], CQD[:, cols],
                             start=True, stop=False)
                    t.matmul(a, QBL[:, h * 64:(h + 1) * 64], CQL[:, cols],
                             start=False, stop=True).then_inc(SEM["st"], 1)
            tq(t_)
            C["st"] += NNC
            st_q = C["st"]

            def v_(v, slot=slot, st_q=st_q):
                v.wait_ge(SEM["st"], st_q)
                v.tensor_copy(Q2[slot][0:64, :], PS[3][0:64, 0:WIN])
                v.tensor_tensor(QST[0:64, :], PS[3][0:64, 0:WIN], Q2[slot][0:64, :],
                                op=SUB) \
                    .then_inc(SEM["sv"], 1)
            vq(v_)
            C["sv"] += 1
            sv_q = C["sv"]

            qk = "qa" if slot == 0 else "qb"

            def y_(y, slot=slot, sv_q=sv_q, qk=qk):
                y.wait_ge(SEM["sv"], sv_q)
                y.dma_start(out=Q2[slot][64:128, :], in_=QST[0:64, :]) \
                    .then_inc(SEM[qk], 16)
            yq(y_)
            C[qk] += 16
            return (qk, C[qk])

        def attn_window(w):
            q0 = w * WIN
            sm_q = qbuild(0, 0, w,
                          gate_sv=(FST["sv_vy_last"] if w >= 1 else None))
            sv_y_hist = []

            for h in range(HEADS):
                slot = h % 2
                plan = []
                for kt in range(NKT):
                    plan.append(("sc", kt))
                    if kt >= 1:
                        plan.append(("av", kt - 1))
                plan.append(("av", NKT - 1))
                st_base = C["st"]
                st_sc, st_av = {}, {}
                n = 0
                for op, kt in plan:
                    n += 1
                    (st_sc if op == "sc" else st_av)[kt] = st_base + n
                ss_base = C["ss"]
                ss_exp = {kt: ss_base + kt + 1 for kt in range(NKT)}
                sv_base = C["sv"]
                sv_sub = {kt: sv_base + kt + 1 for kt in range(NKT)}

                def t_loop(t, slot=slot, plan=tuple(plan), ss_exp=dict(ss_exp),
                           sv_sub=dict(sv_sub), sm_q=sm_q):
                    t.wait_ge(SEM[sm_q[0]], sm_q[1])
                    for op, kt in plan:
                        if op == "sc":
                            if kt >= 2:
                                t.wait_ge(SEM["ss"], ss_exp[kt - 2])
                            pss = PS[2 + (kt % 2)]
                            for ncx in range(NNC):
                                pc = slice(512 + ncx * NCH, 512 + (ncx + 1) * NCH)
                                t.matmul(pss[:, pc], K1[:, kt * 128:(kt + 1) * 128],
                                         Q2[slot][:, ncx * NCH:(ncx + 1) * NCH],
                                         start=True, stop=False)
                                mm = t.matmul(pss[:, pc],
                                              K2[:, kt * 128:(kt + 1) * 128],
                                              Q2[slot][0:64,
                                                       ncx * NCH:(ncx + 1) * NCH],
                                              start=False, stop=True)
                                if ncx == NNC - 1:
                                    mm.then_inc(SEM["st"], 1)
                        else:
                            t.wait_ge(SEM["sv"], sv_sub[kt])
                            phv = Pb[kt % 2][:].bitcast(BF16) \
                                .rearrange("p (n two) -> p n two", two=2)
                            for ncx in range(NNC):
                                pc = slice(ncx * NCH, (ncx + 1) * NCH)
                                t.matmul(PS[2][0:65, pc],
                                         CST1[:, kt * 65:(kt + 1) * 65],
                                         phv[:, pc, 1],
                                         start=(kt == 0), stop=(kt == NKT - 1))
                                mm = t.matmul(PS[3][0:33, pc],
                                              CST2[:, kt * 33:(kt + 1) * 33],
                                              Pl[kt % 2][:, pc],
                                              start=(kt == 0), stop=(kt == NKT - 1))
                                if ncx == NNC - 1:
                                    mm.then_inc(SEM["st"], 1)
                tq(t_loop)
                C["st"] += len(plan)

                def s_loop(s, st_sc=dict(st_sc), st_av=dict(st_av)):
                    for kt in range(NKT):
                        s.wait_ge(SEM["st"], st_sc[kt])
                        if kt >= 2:
                            s.wait_ge(SEM["st"], st_av[kt - 2])
                        s.activation(out=Pb[kt % 2][:, :],
                                     in_=PS[2 + (kt % 2)][:, 512:512 + WIN],
                                     func=AF.Exp, bias=cneg[:, 0:1],
                                     scale=SOFT_SCALE).then_inc(SEM["ss"], 1)
                sq(s_loop)
                C["ss"] += NKT

                def v_loop(v, ss_exp=dict(ss_exp)):
                    for kt in range(NKT):
                        v.wait_ge(SEM["ss"], ss_exp[kt])
                        phv = Pb[kt % 2][:].bitcast(BF16) \
                            .rearrange("p (n two) -> p n two", two=2)
                        v.tensor_tensor(Pl[kt % 2][:, :], Pb[kt % 2][:, :],
                                        phv[:, :, 1], op=SUB).then_inc(SEM["sv"], 1)
                vq(v_loop)
                C["sv"] += NKT

                st_av_fin = st_av[NKT - 1]

                def v_u1(v, st_av_fin=st_av_fin):
                    v.wait_ge(SEM["st"], st_av_fin)
                    v.tensor_copy(UEV[0:65, :], PS[2][0:65, 0:WIN]) \
                        .then_inc(SEM["sv"], 1)
                vq(v_u1)
                C["sv"] += 1
                sv_u1 = C["sv"]

                def y_u1(y, sv_u1=sv_u1):
                    y.wait_ge(SEM["sv"], sv_u1)
                    y.dma_start(out=UMV[0:33, :], in_=UEV[32:65, :]) \
                        .then_inc(SEM["sa"], 16)
                yq(y_u1)
                C["sa"] += 16
                sm_u1 = C["sa"]

                def v_u2(v, sm_u1=sm_u1):
                    v.wait_ge(SEM["sa"], sm_u1)
                    v.tensor_tensor(UST[0:32, :], UEV[0:32, :], UMV[0:32, :], op=ADD)
                    v.tensor_tensor(UST[0:32, :], UST[0:32, :], PS[3][0:32, 0:WIN],
                                    op=ADD)
                    v.tensor_tensor(UMV[32:33, :], UMV[32:33, :], PS[3][32:33, 0:WIN],
                                    op=ADD)
                    v.reciprocal(UMV[32:33, :], UMV[32:33, :]) \
                        .then_inc(SEM["sv"], 1)
                vq(v_u2)
                C["sv"] += 1
                sv_u2 = C["sv"]

                def y_u2(y, sv_u2=sv_u2):
                    y.wait_ge(SEM["sv"], sv_u2)
                    y.dma_start(out=ROWF[0:1, 0:WIN], in_=UMV[32:33, :]) \
                        .then_inc(SEM["sa"], 16)
                yq(y_u2)
                C["sa"] += 16

                def evac_r(v, ci, ps):
                    ln = ps.ap[-1][1] if False else min(512, WIN - ci * 512)
                    v.tensor_tensor(UST[0:32, ci * 512:ci * 512 + ln],
                                    UST[0:32, ci * 512:ci * 512 + ln],
                                    ps[0:32, :], op=MUL)
                bcast_row(WIN, evac_r, sa_gate=C["sa"])
                sv_u3 = C["sv"]

                def v_u4(v, sv_u3=sv_u3):
                    v.wait_ge(SEM["sv"], sv_u3)
                    v.tensor_copy(USH[0:32, :], UST[0:32, :])
                    v.tensor_tensor(USL[0:32, :], UST[0:32, :], USH[0:32, :],
                                    op=SUB) \
                        .then_inc(SEM["sv"], 1)
                vq(v_u4)
                C["sv"] += 1
                sv_u4 = C["sv"]

                def y_u3(y, h=h, sv_u4=sv_u4):
                    y.wait_ge(SEM["sv"], sv_u4)
                    kc, j = h // 4, h % 4
                    y.dma_start(out=UallH[32 * j:32 * j + 32, kc * WIN:(kc + 1) * WIN],
                                in_=USH[0:32, :]).then_inc(SEM["sa"], 16)
                    y.dma_start(out=UallL[32 * j:32 * j + 32, kc * WIN:(kc + 1) * WIN],
                                in_=USL[0:32, :]).then_inc(SEM["sa"], 16)
                yq(y_u3)
                C["sa"] += 32

                if h + 1 < HEADS:
                    sm_q = qbuild(h + 1, (h + 1) % 2, w, gate_sv=sv_u2)
                if STAGE >= 5 and w >= 1 and NTF >= 2:
                    ngrp = NHC // 4
                    tot = (NTF - 1) * ngrp
                    per = -(-tot // HEADS)
                    for k_ in range(h * per, min((h + 1) * per, tot)):
                        ffn_th_chunk(w - 1, k_ // ngrp, groups=[k_ % ngrp])
                if w == 0:
                    emit_wq_scan_part(h, HEADS)

            sm_uall = C["sa"]

            if w == 0:
                emit_wq_tail()

            fslots = [[] for _ in range(WTT)]
            if STAGE >= 5 and w >= 1:
                items = []
                for th_ in range(NTF):
                    if th_ == NTF - 1:
                        items.append(("chunk", th_, 0))
                    for ttc_ in range(TFT):
                        items.append(("A", th_, ttc_))
                    for ttc_ in range(TFT):
                        items.append(("B", th_, ttc_))
                psl = -(-len(items) // WTT)
                for i_, it_ in enumerate(items):
                    fslots[min(i_ // psl, WTT - 1)].append(it_)

            def emit_fslot(si, wp=w - 1):
                for kind_, th_, ttc_ in fslots[si]:
                    if kind_ == "chunk":
                        ffn_th_chunk(wp, th_)
                    elif kind_ == "A":
                        ffn_ttc_A(wp, th_, ttc_)
                    else:
                        ffn_ttc_B(wp, th_, ttc_)

            sv_y = {}
            for tc in range(WTT):
                gate_x0 = sv_y.get(tc - 1, 0)

                def y_x0(y, tc=tc, q0=q0, gate_x0=gate_x0):
                    if gate_x0:
                        y.wait_ge(SEM["sv"], gate_x0)
                    y.dma_start(out=x0r[0][:, :],
                                in_=x0_h[q0 + tc * 128:q0 + (tc + 1) * 128, :]) \
                        .then_inc(SEM["xr"], 16)
                yq(y_x0)
                C["xr"] += 16
                si_x0 = C["xr"]
                gate_yt = sv_y.get(tc - 1, 0)

                wq_c = FST["sv_wq_c"] if w == 0 else 0

                def t_y(t, tc=tc, sm_uall=sm_uall, gate_yt=gate_yt, wq_c=wq_c):
                    t.wait_ge(SEM["sa"], sm_uall)
                    if wq_c:
                        t.wait_ge(SEM["sv"], wq_c)
                    if gate_yt:
                        t.wait_ge(SEM["sv"], gate_yt)
                    for nc2 in range(2):
                        a = PS[3][:, nc2 * 512:(nc2 + 1) * 512]
                        for kc in range(2):
                            u_sl = slice(kc * WIN + tc * 128, kc * WIN + (tc + 1) * 128)
                            g_sl = slice(kc * D + nc2 * 512, kc * D + (nc2 + 1) * 512)
                            t.matmul(a, UallH[:, u_sl], GH[:, g_sl],
                                     start=(kc == 0), stop=False)
                            t.matmul(a, UallL[:, u_sl], GH[:, g_sl],
                                     start=False, stop=False)
                            t.matmul(a, UallH[:, u_sl], GL[:, g_sl],
                                     start=False, stop=False)
                        t.matmul(a, ONE2[:, 0:128], BE2[:, nc2 * 512:(nc2 + 1) * 512],
                                 start=False, stop=True).then_inc(SEM["st"], 1)
                tq(t_y)
                C["st"] += 2
                st_y = C["st"]

                so_gate_y = C["so"]

                def v_y(v, tc=tc, w=w, st_y=st_y, si_x0=si_x0,
                        so_gate_y=so_gate_y):
                    v.wait_ge(SEM["st"], st_y)
                    v.wait_ge(SEM["xr"], si_x0)
                    v.wait_ge(SEM["so"], so_gate_y)
                    v.tensor_tensor(YT[:, 0:512], x0r[0][:, 0:512],
                                    PS[3][:, 0:512], op=ADD)
                    v.tensor_tensor(YT[:, 512:1024], x0r[0][:, 512:1024],
                                    PS[3][:, 512:1024], op=ADD)
                    v.bn_stats(bnb[:, 0:6], YT[:, 0:512])
                    v.bn_stats(bnb[:, 6:12], YT[:, 512:1024])
                    v.bn_aggr(mvb[:, 0:2], bnb[:, 0:12]) \
                        .then_inc(SEM["sv"], 1)
                vq(v_y)
                C["sv"] += 1
                sv_vy = C["sv"]
                FST["sv_vy_last"] = sv_vy

                def v_x1(v, tc=tc, w=w, sv_vy=sv_vy):
                    v.wait_ge(SEM["sv"], sv_vy)
                    v.tensor_scalar(stat[:, 2:3], mvb[:, 1:2], epsc[:, 0:1], None,
                                    op0=ADD)
                    s2i = stat[:, 2:3].bitcast(I32)
                    s1i = stat[:, 1:2].bitcast(I32)
                    v.tensor_scalar(s1i, s2i, 1, None,
                                    op0=AluOpType.arith_shift_right)
                    v.tensor_scalar(s1i, s1i, -1, 0x5F3759DF, op0=MUL, op1=ADD)
                    for _ in range(3):
                        v.tensor_tensor(stat[:, 3:4], stat[:, 1:2], stat[:, 1:2],
                                        op=MUL)
                        v.tensor_tensor(stat[:, 3:4], stat[:, 3:4], stat[:, 2:3],
                                        op=MUL)
                        v.tensor_scalar(stat[:, 3:4], stat[:, 3:4], -0.5, 1.5,
                                        op0=MUL, op1=ADD)
                        v.tensor_tensor(stat[:, 1:2], stat[:, 1:2], stat[:, 3:4],
                                        op=MUL)
                    v.tensor_scalar(x1[w % 2][:, tc * D:(tc + 1) * D], YT[:, :],
                                    mvb[:, 0:1], stat[:, 1:2], op0=SUB, op1=MUL) \
                        .then_inc(SEM["sv"], 1)
                vq(v_x1)
                C["sv"] += 1
                sv_y[tc] = C["sv"]
                if DEBUG_TAP:
                    def y_dbg(y, tc=tc, w=w, gate=sv_y[tc]):
                        y.wait_ge(SEM["sv"], gate)
                        y.dma_start(out=dbg_d[w * WIN + tc * 128:
                                              w * WIN + (tc + 1) * 128, :],
                                    in_=x1[w % 2][:, tc * D:(tc + 1) * D]) \
                            .then_inc(SEM["so"], 16)
                    yq(y_dbg)
                    C["so"] += 16
                if STAGE >= 5 and tc >= 1:
                    if tc == 1 and w >= 1:
                        emit_lhalo(w)
                    quant_tile(w, tc - 1, sv_y[tc - 1])
                    emit_fslot(tc - 1)
            if STAGE >= 5:
                quant_tile(w, WTT - 1, sv_y[WTT - 1])
                if w == NW - 1:
                    emit_zhalo(w)
                emit_fslot(WTT - 1)
            return C["sv"]

        FST = {"st_i1": {}, "sv_j": {}, "st_w2": {}, "ni1": 0, "nw2": 0,
               "sv_q": {}, "st_i1_w": {}, "sv_zhalo": 0, "sv_ps1": 0,
               "ss_sil": {}, "sv_xq2": {}, "st_w2_last": 0,
               "sv_wq_c": 0, "so_wq": 0, "sv_vy_last": 0}
        W1S = ["w1a", "w1b", "w1c", "w1d"]
        W2S = ["w2a", "w2b", "w2c", "w2d"]

        def quant_tile(w, tc, sv_gate):
            src = x1[w % 2][:, tc * D:(tc + 1) * D]
            st_gate = FST["st_i1_w"].get(w - 2, 0)

            def v_q(v, src=src, sv_gate=sv_gate):
                v.wait_ge(SEM["sv"], sv_gate)
                v.tensor_reduce(stat[:, 4:5], src, axis=AX.X, op=AluOpType.max,
                                apply_absolute_value=True)
                v.tensor_scalar(stat[:, 4:5], stat[:, 4:5], QEPS, None,
                                op0=AluOpType.max)
                v.reciprocal(stat[:, 5:6], stat[:, 4:5])
                v.tensor_scalar(stat[:, 5:6], stat[:, 5:6], 127.0, None, op0=MUL)
                v.tensor_scalar(IVX[:, 0:1], stat[:, 4:5], 1.0 / 127.0, None, op0=MUL)
                v.tensor_scalar(xqi[:, :], src, stat[:, 5:6], None, op0=MUL)
                v.tensor_copy(xqb[:, :], xqi[:, :]) \
                    .then_inc(SEM["sv"], 1)
            vq(v_q)
            C["sv"] += 1
            sv_qq = C["sv"]

            def t_tp(t, sv_qq=sv_qq):
                t.wait_ge(SEM["sv"], sv_qq)
                pb = PS[2][:, 0:512].bitcast(BF16)
                for fc in range(8):
                    t.matmul(pb[:, fc * 128:(fc + 1) * 128],
                             xqb[:, fc * 128:(fc + 1) * 128], idB[:],
                             start=True, stop=True, is_transpose=True) \
                        .then_inc(SEM["st"], 1)
                t.matmul(PS[2][0:1, 512:640], IVX[:, 0:1], idF[:], start=True,
                         stop=True, is_transpose=True).then_inc(SEM["st"], 1)
            tq(t_tp)
            C["st"] += 9
            st_tpe = C["st"]

            def v_tpe(v, w=w, tc=tc, st_tpe=st_tpe, st_gate=st_gate):
                v.wait_ge(SEM["st"], st_tpe)
                if st_gate:
                    v.wait_ge(SEM["st"], st_gate)
                pb = PS[2][:, 0:512].bitcast(BF16)
                buf = xqW[w % 2]
                r = None
                for fc in range(8):
                    base = fc * XQC
                    r = v.tensor_copy(
                        buf[:, base + 1 + tc * 128:base + 1 + (tc + 1) * 128],
                        pb[:, fc * 128:(fc + 1) * 128])
                    if tc == 0 and w >= 1:
                        r = v.tensor_copy(
                            xqW[(w - 1) % 2][:, base + WIN + 1:base + WIN + 2],
                            pb[:, fc * 128:fc * 128 + 1])
                r.then_inc(SEM["sv"], 1)
            vq(v_tpe)
            C["sv"] += 1

            def v_tpe2(v):
                v.tensor_copy(ROWF[0:1, 0:128], PS[2][0:1, 512:640]) \
                    .then_inc(SEM["sv"], 1)
            vq(v_tpe2)
            C["sv"] += 1

            def evac_iv(v, ci, ps, w=w, tc=tc):
                v.tensor_copy(INVSB[w % 2][:, 1 + tc * 128:1 + (tc + 1) * 128],
                              ps[:, 0:128])
                if tc == 0 and w >= 1:
                    v.tensor_copy(INVSB[(w - 1) % 2][:, WIN + 1:WIN + 2],
                                  ps[:, 0:1])
            bcast_row(128, evac_iv, extra_scale_col=wiv[0:1, 2:3])
            FST["sv_q"][(w, tc)] = C["sv"]
            return C["sv"]

        def emit_lhalo(w):
            st_gate = FST["st_i1_w"].get(w - 2, 0)

            def v_lh(v, w=w, st_gate=st_gate):
                if st_gate:
                    v.wait_ge(SEM["st"], st_gate)
                r = None
                for fc in range(8):
                    base = fc * XQC
                    r = v.tensor_copy(xqW[w % 2][:, base:base + 1],
                                      xqW[(w - 1) % 2][:, base + WIN:base + WIN + 1])
                r = v.tensor_copy(INVSB[w % 2][:, 0:1],
                                  INVSB[(w - 1) % 2][:, WIN:WIN + 1])
                r.then_inc(SEM["sv"], 1)
            vq(v_lh)
            C["sv"] += 1

        def emit_zhalo(w):
            st_gate = FST["st_i1_w"].get(w - 2, 0)

            def v_zh(v, w=w, st_gate=st_gate):
                if st_gate:
                    v.wait_ge(SEM["st"], st_gate)
                r = None
                for fc in range(8):
                    base = fc * XQC
                    r = v.memset(xqW[w % 2][:, base + WIN + 1:base + WIN + 2], 0.0)
                r = v.memset(INVSB[w % 2][:, WIN + 1:WIN + 2], 0.0)
                r.then_inc(SEM["sv"], 1)
            vq(v_zh)
            C["sv"] += 1
            FST["sv_zhalo"] = C["sv"]

        def ffn_th_chunk(w, th, groups=None):
            st_i1 = FST["st_i1"]
            sv_j = FST["sv_j"]
            if th == NTF - 1:
                gate_q = FST["sv_q"].get((w + 1, 0), FST["sv_zhalo"])
            else:
                gate_q = FST["sv_q"][(w, min(WTT - 1, (th * TF + TF) // 128))]
            st_w2_prev = FST["st_w2_last"]
            grange = (range(0, NHC, 4) if groups is None
                      else [g * 4 for g in groups])

            for g0_ in grange:
                base_ki = FST["ni1"]
                grp = []
                for gi_, hc in enumerate(range(g0_, g0_ + 4)):
                    ki1 = FST["ni1"]
                    FST["ni1"] += 1
                    gate_ld = st_i1.get(ki1 - 4, 0)
                    wsem = W1S[hc % 4]

                    def y_w1(y, hc=hc, gate_ld=gate_ld, wsem=wsem,
                             wqg=FST["so_wq"]):
                        y.wait_ge(SEM["wq"], wqg)
                        if gate_ld:
                            y.wait_ge(SEM["st"], gate_ld)
                        y.dma_start(out=w1r[hc % 4][:, :],
                                    in_=w1q_d[:, hc * 1024:(hc + 1) * 1024]) \
                            .then_inc(SEM[wsem], 16)
                    gq(y_w1)
                    C[wsem] += 16
                    grp.append((hc, ki1, C[wsem], wsem))

                svbase = C["sv"]

                def t_i1g(t, grp=tuple(grp), th=th, w=w, gate_q=gate_q,
                          base_ki=base_ki, svbase=svbase):
                    for gi_, (hc, ki1, si_w1, wsem) in enumerate(grp):
                        t.wait_ge(SEM[wsem], si_w1)
                        if gi_ == 0:
                            t.wait_ge(SEM["sv"], gate_q)
                        kim2 = ki1 - 2
                        if kim2 >= base_ki:
                            gate_j = svbase + (kim2 - base_ki) + 1
                        else:
                            gate_j = sv_j.get(kim2, 0)
                        if gate_j:
                            t.wait_ge(SEM["sv"], gate_j)
                        a = PS[0][:, (hc % 2) * 512:(hc % 2) * 512 + TF + 2]
                        for fc in range(8):
                            mm = t.matmul(
                                a, w1r[hc % 4][:, fc * 128:(fc + 1) * 128],
                                xqW[w % 2][:, fc * XQC + th * TF:
                                            fc * XQC + th * TF + TF + 2],
                                start=(fc == 0), stop=(fc == 7))
                        mm.then_inc(SEM["st"], 1)
                tq(t_i1g)
                for hc, ki1, _, _ in grp:
                    C["st"] += 1
                    st_i1[ki1] = C["st"]

                for hc, ki1, _, _ in grp:
                    st_here = st_i1[ki1]
                    ss_cva = C["ss"]

                    def v_j(v, hc=hc, th=th, w=w, st_here=st_here, ss_cva=ss_cva):
                        v.wait_ge(SEM["st"], st_here)
                        v.wait_ge(SEM["ss"], ss_cva)
                        a = PS[0][:, (hc % 2) * 512:(hc % 2) * 512 + TF + 2]
                        J = Jb[hc % 2]
                        v.tensor_tensor(J[:, :], a,
                                        INVSB[w % 2][:, th * TF:th * TF + TF + 2],
                                        op=MUL)
                        v.tensor_scalar(cvA[:, :], J[:, 1:TF + 1],
                                        convw[:, NHC + hc:NHC + hc + 1],
                                        convb[:, hc:hc + 1], op0=MUL, op1=ADD)
                        v.scalar_tensor_tensor(cvA[:, :], J[:, 0:TF],
                                               convw[:, hc:hc + 1], cvA[:, :],
                                               op0=MUL, op1=ADD)
                        v.scalar_tensor_tensor(cvA[:, :], J[:, 2:TF + 2],
                                               convw[:, 2 * NHC + hc:2 * NHC + hc + 1],
                                               cvA[:, :], op0=MUL, op1=ADD) \
                            .then_inc(SEM["sv"], 1)
                    vq(v_j)
                    C["sv"] += 1
                    sv_j[ki1] = C["sv"]
                    sv_cv = C["sv"]

                    def s_si(s, hc=hc, sv_cv=sv_cv, st_w2_prev=st_w2_prev):
                        s.wait_ge(SEM["sv"], sv_cv)
                        if st_w2_prev:
                            s.wait_ge(SEM["st"], st_w2_prev)
                        s.activation(out=hsil[:, hc * TF:(hc + 1) * TF], in_=cvA[:, :],
                                     func=AF.Silu) \
                            .then_inc(SEM["ss"], 1)
                    sq(s_si)
                    C["ss"] += 1
                FST["ss_sil"][(w, th)] = C["ss"]
                if th == NTF - 1:
                    FST["st_i1_w"][w] = st_i1[FST["ni1"] - 1]

        def ffn_ttc_A(w, th, ttc):
            ss_sil = FST["ss_sil"][(w, th)]
            if True:
                for ttc in [ttc]:
                    def v_am(v, ttc=ttc, ss_sil=ss_sil):
                        v.wait_ge(SEM["ss"], ss_sil)
                        h3 = hsil[:].rearrange("p (hc t) -> p t hc", hc=NHC)
                        v.tensor_reduce(amx[:, 0:128],
                                        h3[:, ttc * 128:(ttc + 1) * 128, :],
                                        axis=AX.X, op=AluOpType.max,
                                        apply_absolute_value=True) \
                            .then_inc(SEM["sv"], 1)
                    vq(v_am)
                    C["sv"] += 1
                    sv_am = C["sv"]

                    def t_am(t, sv_am=sv_am):
                        t.wait_ge(SEM["sv"], sv_am)
                        t.matmul(PS[2][:, 0:128], amx[:, 0:128], idF[:], start=True,
                                 stop=True, is_transpose=True).then_inc(SEM["st"], 1)
                    tq(t_am)
                    C["st"] += 1
                    st_am = C["st"]

                    def v_s2(v, ttc=ttc, st_am=st_am):
                        v.wait_ge(SEM["st"], st_am)
                        v.tensor_reduce(stat[:, 6:7], PS[2][:, 0:128], axis=AX.X,
                                        op=AluOpType.max)
                        v.tensor_scalar(stat[:, 6:7], stat[:, 6:7], QEPS, None,
                                        op0=AluOpType.max)
                        v.reciprocal(stat[:, 7:8], stat[:, 6:7])
                        v.tensor_scalar(stat[:, 7:8], stat[:, 7:8], 127.0, None,
                                        op0=MUL)
                        v.tensor_scalar(iv2c[:, ttc:ttc + 1], stat[:, 6:7],
                                        1.0 / 127.0, None, op0=MUL)
                        v.tensor_scalar(iv2c[:, ttc:ttc + 1], iv2c[:, ttc:ttc + 1],
                                        wiv[:, 3:4], None, op0=MUL) \
                            .then_inc(SEM["sv"], 1)
                    vq(v_s2)
                    C["sv"] += 1
                    sv_s2 = C["sv"]

                    def t_s2(t, sv_s2=sv_s2):
                        t.wait_ge(SEM["sv"], sv_s2)
                        t.matmul(PS[2][0:1, 0:128], stat[:, 7:8], idF[:], start=True,
                                 stop=True, is_transpose=True).then_inc(SEM["st"], 1)
                    tq(t_s2)
                    C["st"] += 1
                    st_s2 = C["st"]

                    def v_s2b(v, st_s2=st_s2):
                        v.wait_ge(SEM["st"], st_s2)
                        v.tensor_copy(ROWF[0:1, 0:128], PS[2][0:1, 0:128]) \
                            .then_inc(SEM["sv"], 1)
                    vq(v_s2b)
                    C["sv"] += 1

                    def evac_s2(v, ci, ps, ttc=ttc):
                        v.tensor_copy(s2b[:, ttc * 128:(ttc + 1) * 128], ps[:, 0:128])
                    bcast_row(128, evac_s2)
                    sv_s2bb = C["sv"]

                    def v_xq2(v, ttc=ttc, sv_s2bb=sv_s2bb):
                        v.wait_ge(SEM["sv"], sv_s2bb)
                        for hc in range(NHC):
                            cs = hc * TF + ttc * 128
                            v.tensor_tensor(xqi[:, 0:128], hsil[:, cs:cs + 128],
                                            s2b[:, ttc * 128:(ttc + 1) * 128], op=MUL)
                            v.tensor_copy(hsil[:, cs:cs + 128], xqi[:, 0:128])
                        v.tensor_copy(stat[:, 15:16], stat[:, 15:16]) \
                            .then_inc(SEM["sv"], 1)
                    vq(v_xq2)
                    C["sv"] += 1
                    FST["sv_xq2"][(w, th, ttc)] = C["sv"]

        def ffn_ttc_B(w, th, ttc):
            q0 = w * WIN
            XB = x1[w % 2]
            st_w2g = FST["st_w2"]
            if True:
                for ttc in [ttc]:
                    sv_xq2 = max(FST["sv_xq2"][(w, th, ttc)], FST["sv_ps1"])
                    for g0_ in range(0, NHC, 4):
                        grp = []
                        for hc in range(g0_, g0_ + 4):
                            kw2 = FST["nw2"]
                            FST["nw2"] += 1
                            gate_ld = st_w2g.get(kw2 - 4, 0)
                            wsem = W2S[hc % 4]

                            def y_w2(y, hc=hc, gate_ld=gate_ld, wsem=wsem,
                                     wqg=FST["so_wq"]):
                                y.wait_ge(SEM["wq"], wqg)
                                if gate_ld:
                                    y.wait_ge(SEM["st"], gate_ld)
                                y.dma_start(out=w2r[hc % 4][:, :],
                                            in_=w2q_d[:,
                                                      hc * 1024:(hc + 1) * 1024]) \
                                    .then_inc(SEM[wsem], 16)
                            yq(y_w2)
                            C[wsem] += 16
                            grp.append((hc, kw2, C[wsem], wsem))

                        def t_w2g(t, grp=tuple(grp), ttc=ttc, sv_xq2=sv_xq2):
                            for hc, kw2, si_w2, wsem in grp:
                                t.wait_ge(SEM[wsem], si_w2)
                                if hc == 0:
                                    t.wait_ge(SEM["sv"], sv_xq2)
                                xsl = hsil[:, hc * TF + ttc * 128:
                                           hc * TF + (ttc + 1) * 128]
                                for nc2 in range(2):
                                    mm = t.matmul(
                                        PS[1][:, nc2 * 512:(nc2 + 1) * 512], xsl,
                                        w2r[hc % 4][:, nc2 * 512:(nc2 + 1) * 512],
                                        start=(hc == 0), stop=(hc == NHC - 1))
                                mm.then_inc(SEM["st"], 1)
                        tq(t_w2g)
                        for hc, kw2, _, _ in grp:
                            C["st"] += 1
                            st_w2g[kw2] = C["st"]
                    st_w2fin = C["st"]
                    FST["st_w2_last"] = C["st"]

                    glob_t = th * TFT + ttc
                    so_gate = C["so"]

                    def v_y2(v, glob_t=glob_t, ttc=ttc, w=w, st_w2fin=st_w2fin,
                             so_gate=so_gate):
                        v.wait_ge(SEM["st"], st_w2fin)
                        v.wait_ge(SEM["so"], so_gate)
                        v.scalar_tensor_tensor(YT[:, 0:512], PS[1][:, 0:512],
                                               iv2c[:, ttc:ttc + 1],
                                               XB[:, glob_t * D:glob_t * D + 512],
                                               op0=MUL, op1=ADD)
                        v.scalar_tensor_tensor(YT[:, 512:1024], PS[1][:, 512:1024],
                                               iv2c[:, ttc:ttc + 1],
                                               XB[:, glob_t * D + 512:(glob_t + 1) * D],
                                               op0=MUL, op1=ADD)
                        v.bn_stats(bnb[:, 0:6], YT[:, 0:512])
                        v.bn_stats(bnb[:, 6:12], YT[:, 512:1024])
                        v.bn_aggr(mvb[:, 0:2], bnb[:, 0:12]) \
                            .then_inc(SEM["sv"], 1)
                    vq(v_y2)
                    C["sv"] += 1
                    sv_y2 = C["sv"]
                    FST["sv_ps1"] = C["sv"]

                    def v_x2(v, sv_y2=sv_y2):
                        v.wait_ge(SEM["sv"], sv_y2)
                        v.tensor_scalar(stat[:, 2:3], mvb[:, 1:2], epsc[:, 0:1],
                                        None, op0=ADD)
                        s2i = stat[:, 2:3].bitcast(I32)
                        s1i = stat[:, 1:2].bitcast(I32)
                        v.tensor_scalar(s1i, s2i, 1, None,
                                        op0=AluOpType.arith_shift_right)
                        v.tensor_scalar(s1i, s1i, -1, 0x5F3759DF, op0=MUL, op1=ADD)
                        for _ in range(3):
                            v.tensor_tensor(stat[:, 3:4], stat[:, 1:2], stat[:, 1:2],
                                            op=MUL)
                            v.tensor_tensor(stat[:, 3:4], stat[:, 3:4], stat[:, 2:3],
                                            op=MUL)
                            v.tensor_scalar(stat[:, 3:4], stat[:, 3:4], -0.5, 1.5,
                                            op0=MUL, op1=ADD)
                            v.tensor_tensor(stat[:, 1:2], stat[:, 1:2], stat[:, 3:4],
                                            op=MUL)
                        v.tensor_scalar(YT[:, :], YT[:, :], mvb[:, 0:1], stat[:, 1:2],
                                        op0=SUB, op1=MUL).then_inc(SEM["sv"], 1)
                    vq(v_x2)
                    C["sv"] += 1
                    sv_x2 = C["sv"]

                    def s_out(s, glob_t=glob_t, q0=q0, sv_x2=sv_x2):
                        s.wait_ge(SEM["sv"], sv_x2)
                        s.dma_start(out=out_h[q0 + glob_t * 128:
                                              q0 + (glob_t + 1) * 128, :],
                                    in_=YT[:, :]).then_inc(SEM["so"], 16)
                    sq(s_out)
                    C["so"] += 16

        def ffn_window_seq(w):
            for th in range(NTF):
                ffn_th_chunk(w, th)
                for ttc in range(TFT):
                    ffn_ttc_A(w, th, ttc)
                for ttc in range(TFT):
                    ffn_ttc_B(w, th, ttc)

        def _dbg(tag):
            if DEBUG_LEDGER:
                print(f"LEDGER {tag}: C_sv={C['sv']} "
                      f"actual={PTOT['v'].get(SEM['sv'], 0)} "
                      f"C_ss={C['ss']} actual_s={PTOT['s'].get(SEM['ss'], 0)}")

        sv_x1 = {}
        _dbg("startup")
        if STAGE >= 3:
            sv_x1[0] = attn_window(0)
            _dbg("attn0")
            if STAGE >= 4:
                for w in range(1, NW):
                    sv_x1[w] = attn_window(w)
                    _dbg(f"attn{w}")
                if STAGE >= 5:
                    ffn_window_seq(NW - 1)
                    _dbg(f"ffn{NW - 1}")

        SO_FIN = C["so"]

        def s_fin(s):
            s.wait_ge(SEM["so"], SO_FIN)
        sq(s_fin)

        for _k, _sem in SEM.items():
            _tv = PTOT["v"].get(_sem, 0)
            _ts = PTOT["s"].get(_sem, 0)
            if _tv or _ts:
                assert C[_k] == _tv + _ts, \
                    f"ledger mismatch {_k}: C={C[_k]} proxied={_tv + _ts}"

    return nc


_NC_CACHE = {}


def kernel(**inputs):
    S = inputs["L0_attn_h_t"].shape[1]
    B = inputs["L0_attn_h_t"].shape[0]
    WIN = 512 if S % 512 == 0 else 256
    key = (S, WIN)
    if key not in _NC_CACHE:
        _NC_CACHE[key] = build_nc(S=S, WIN=WIN, TF=256)
    nc = _NC_CACHE[key]

    f = np.float32
    asc = np.ascontiguousarray
    base = {
        "Wproj": asc(np.concatenate([inputs["L0_attn_W_DKV"], inputs["L0_attn_W_KR"],
                                     inputs["L0_attn_W_DQ"]], axis=1), dtype=f),
        "bkvr": asc(np.concatenate([np.asarray(inputs["L0_attn_b_DKV"]),
                                    np.zeros(32, f)])[:, None], dtype=f),
        "bq": asc(np.asarray(inputs["L0_attn_b_DQ"])[:, None], dtype=f),
        "WUQT": asc(np.asarray(inputs["L0_attn_W_UQ"]).T, dtype=f),
        "WUKT": asc(np.asarray(inputs["L0_attn_W_UK"]).T, dtype=f),
        "WUVT": asc(np.asarray(inputs["L0_attn_W_UV"]).T, dtype=f),
        "bUQ": asc(np.asarray(inputs["L0_attn_b_UQ"])[:, None], dtype=f),
        "bUV": asc(np.asarray(inputs["L0_attn_b_UV"])[:, None], dtype=f),
        "WQR": asc(np.asarray(inputs["L0_attn_W_QR"]), dtype=f),
        "bQR": asc(np.asarray(inputs["L0_attn_b_QR"])[None, :], dtype=f),
        "WO": asc(np.asarray(inputs["L0_attn_W_O"]), dtype=f),
        "bO": asc(np.asarray(inputs["L0_attn_b_O"])[None, :], dtype=f),
        "w1T": asc(np.asarray(inputs["L0_ffn_w1"]).T, dtype=f),
        "w2T": asc(np.asarray(inputs["L0_ffn_w2"]).T, dtype=f),
        "convw_pre": asc(np.asarray(inputs["L0_ffn_conv_w"]).reshape(NHC, 128, 3)
                         .transpose(1, 2, 0).reshape(128, 3 * NHC), dtype=f),
        "convb_pre": asc(np.asarray(inputs["L0_ffn_conv_b"]).reshape(NHC, 128).T,
                         dtype=f),
    }
    ht = np.asarray(inputs["L0_attn_h_t"], dtype=f)
    in_maps = []
    for b in range(B):
        m = dict(base)
        m["x0"] = asc(ht[b])
        m["x0T"] = asc(ht[b].T)
        in_maps.append(m)
    res = run_bass_kernel_spmd(nc, in_maps, core_ids=list(range(B)))
    return np.stack([np.asarray(res.results[b]["out"], dtype=np.float32)
                     for b in range(B)], axis=0)
